# version 21
# baseline (speedup 1.0000x reference)
import math
import numpy as np
import ml_dtypes

import concourse.bass as bass
import concourse.mybir as mybir
from concourse.bass_utils import run_bass_kernel_spmd

F32 = mybir.dt.float32
BF16 = mybir.dt.bfloat16
I32 = mybir.dt.int32
AF = mybir.ActivationFunctionType
ALU = mybir.AluOpType

DEBUG_DUMP = False

ENGINES = ("tensor", "vector", "scalar", "gpsimd", "sync")


class Buf:
    __slots__ = ("name", "lw", "readers")

    def __init__(self, name):
        self.name = name
        self.lw = {}
        self.readers = {}


class Stream:
    __slots__ = ("key", "cnt", "engine")

    def __init__(self, key, engine):
        self.key = key
        self.cnt = 0
        self.engine = engine


class Plan:
    def __init__(self):
        self.ops = {e: [] for e in ENGINES}
        self.cnt = {e: 0 for e in ENGINES}
        self.watermark = {e: {} for e in ENGINES}
        self.sem_keys = set(ENGINES)
        self.streams = {}

    def stream(self, name, engine):
        key = f"dma_{name}"
        assert key not in self.sem_keys
        self.sem_keys.add(key)
        s = Stream(key, engine)
        self.streams[key] = s
        return s

    def custom_sem(self, name):
        key = f"cs_{name}"
        self.sem_keys.add(key)
        return key

    def _waits_for(self, eng, reads, writes):
        need = {}

        def add(sem_key, val):
            if eng == "tensor" and sem_key == "tensor":
                return
            if self.watermark[eng].get(sem_key, 0) >= val:
                return
            if need.get(sem_key, 0) < val:
                need[sem_key] = val

        for b in reads:
            for k, v in b.lw.items():
                add(k, v)
        for b in writes:
            for k, v in b.lw.items():
                add(k, v)
            for k, v in b.readers.items():
                add(k, v)
        return need

    def _commit(self, eng, need):
        for k, v in need.items():
            self.watermark[eng][k] = v

    def _record(self, sem_key, val, reads, writes):
        for b in reads:
            if b.readers.get(sem_key, 0) < val:
                b.readers[sem_key] = val
        for b in writes:
            if b.lw.get(sem_key, 0) < val:
                b.lw[sem_key] = val

    def op(self, eng, emit, reads=(), writes=()):
        need = self._waits_for(eng, reads, writes)
        self._commit(eng, need)
        self.cnt[eng] += 1
        val = self.cnt[eng]
        self.ops[eng].append((need, emit, eng, 1))
        self._record(eng, val, reads, writes)

    def op_custom(self, eng, emit, sem_key, done_val, reads=(), writes=()):
        need = self._waits_for(eng, reads, writes)
        self._commit(eng, need)
        self.ops[eng].append((need, emit, sem_key, done_val))
        self._record(sem_key, done_val, reads, writes)

    def dma(self, stream, emit, reads=(), writes=()):
        eng = stream.engine
        need = self._waits_for(eng, reads, writes)
        if stream.cnt > 0 and self.watermark[eng].get(stream.key, 0) < stream.cnt:
            if need.get(stream.key, 0) < stream.cnt:
                need[stream.key] = stream.cnt
        self._commit(eng, need)
        stream.cnt += 16
        val = stream.cnt
        self.ops[eng].append((need, emit, stream.key, 16))
        self._record(stream.key, val, reads, writes)

    def raw(self, eng, emit):
        self.ops[eng].append((None, emit, None, 0))

    def stamp(self, bufs, stream):
        for b in bufs:
            b.lw = {stream.key: stream.cnt}
            b.readers = {}

    def barrier_all(self):
        snapshot = dict(self.cnt)
        for s in self.streams.values():
            snapshot[s.key] = s.cnt
        for e in ENGINES:
            need = {}
            for k, v in snapshot.items():
                if v > 0 and k != e and self.watermark[e].get(k, 0) < v:
                    need[k] = v
                    self.watermark[e][k] = v
            if need:
                self.ops[e].append((need, None, None, 0))

    def emit(self, nc):
        import contextlib

        with contextlib.ExitStack() as st:
            sems = {}
            for k in sorted(self.sem_keys):
                sems[k] = st.enter_context(nc.semaphore(f"s_{k}"))
            with nc.Block() as block:
                for eng in ENGINES:
                    ops = self.ops[eng]
                    if not ops:
                        continue

                    def make_body(ops=ops):
                        def body(proxy):
                            for need, emit, sem_key, inc in ops:
                                if need:
                                    for k, v in sorted(need.items()):
                                        proxy.wait_ge(sems[k], v)
                                if emit is not None:
                                    inst = emit(proxy)
                                    if inc:
                                        inst.then_inc(sems[sem_key], inc)
                        return body

                    getattr(block, eng)(make_body())


def MMG(psum_ap, pairs):
    def emit(t):
        last = None
        n = len(pairs)
        for i, (l, r) in enumerate(pairs):
            last = t.matmul(psum_ap, l, r, start=(i == 0), stop=(i == n - 1))
        return last
    return emit


B, S, D = 2, 4096, 2048
NH, DH = 16, 128
HPC = 4
W = 4
TC = 1024
F1 = 8192
L = 2
PW = 2
EPS = 1e-5
SCALE = 1.0 / math.sqrt(DH)
NDC = D // 128
NFC = F1 // 128

FB_AMPB = 0
FB_PHB = 16
FB_ENTB = 32
FB_ENTPB = 160
FB_LN = 192
FB_MOD = 448
FB_PB = 512
FB_BMS = 576
FB_EPS = 608
FB_COLS = 640


def build_program():
    nc = bass.Bass("TRN2", target_bir_lowering=False)
    P = Plan()
    import contextlib
    ctx = contextlib.ExitStack()

    x_batch = nc.dram_tensor("x_batch", [S, D], F32, kind="ExternalInput").ap()
    x_slice = nc.dram_tensor("x_slice", [TC, D], F32, kind="ExternalInput").ap()
    w_qkv = nc.dram_tensor("w_qkv", [D, 3 * HPC * 128], F32, kind="ExternalInput").ap()
    b_qk = nc.dram_tensor("b_qk", [128, 2 * HPC], F32, kind="ExternalInput").ap()
    b_v_row = nc.dram_tensor("b_v_row", [1, HPC * 128], F32, kind="ExternalInput").ap()
    w_out_d = nc.dram_tensor("w_out", [D, D], F32, kind="ExternalInput").ap()
    b_out_row = nc.dram_tensor("b_out_row", [1, D], F32, kind="ExternalInput").ap()
    amp_w_d = nc.dram_tensor("amp_w", [D, D], F32, kind="ExternalInput").ap()
    ph_w_d = nc.dram_tensor("ph_w", [D, D], F32, kind="ExternalInput").ap()
    fb_in = nc.dram_tensor("fb_in", [128, FB_COLS], F32, kind="ExternalInput").ap()
    ent_w_d = nc.dram_tensor("ent_w", [L, D, F1], F32, kind="ExternalInput").ap()
    ent_pw_d = nc.dram_tensor("ent_pw", [L, F1, D], F32, kind="ExternalInput").ap()
    path_w_d = nc.dram_tensor("path_w", [L, PW, D, D], F32, kind="ExternalInput").ap()
    masks_in = nc.dram_tensor("masks_in", [128, 4 * 512], BF16, kind="ExternalInput").ap()
    phase_r = nc.dram_tensor("phase_r", [1, L * PW * D], F32, kind="ExternalInput").ap()
    ident_in = nc.dram_tensor("ident_in", [128, 128], F32, kind="ExternalInput").ap()
    ones_in = nc.dram_tensor("ones_in", [128, 512], BF16, kind="ExternalInput").ap()
    onesf_in = nc.dram_tensor("onesf_in", [1, 128], F32, kind="ExternalInput").ap()
    onesfc_in = nc.dram_tensor("onesfc_in", [128, 1], F32, kind="ExternalInput").ap()
    tok_off = nc.dram_tensor("tok_off", [1, NDC], I32, kind="ExternalInput").ap()

    out_d = nc.dram_tensor("out", [D, TC], F32, kind="ExternalOutput").ap()

    xln_dram = nc.dram_tensor("xln_dram", [S, D], BF16).ap()
    ag_in = nc.dram_tensor("ag_in", [2, HPC * 128, S // 2], BF16).ap()
    ag_out = nc.dram_tensor("ag_out", [2 * HPC * 512, S // 2], BF16).ap()
    x2_dram = nc.dram_tensor("x2_dram", [TC, D], F32).ap()
    x2f_dram = nc.dram_tensor("x2f_dram", [D, TC], F32).ap()
    x2ln_dram = nc.dram_tensor("x2ln_dram", [TC, D], BF16).ap()
    h_dram = [nc.dram_tensor(f"h_dram{i}", [D, TC], F32).ap() for i in range(2)]
    wo_bf = nc.dram_tensor("wo_bf", [NDC, 128, D], BF16).ap()
    modb_dram = nc.dram_tensor("modb_dram", [L * PW, 128, D], BF16).ap()
    mr_dram_t = nc.dram_tensor("mr_dram", [1, 2048], F32)
    mr_dram = mr_dram_t.ap()

    dbg = {}
    if DEBUG_DUMP:
        dbg["q"] = nc.dram_tensor("dbg_q", [HPC * 128, S], BF16, kind="ExternalOutput").ap()
        dbg["k"] = nc.dram_tensor("dbg_k", [HPC * 128, S], BF16, kind="ExternalOutput").ap()
        dbg["v"] = nc.dram_tensor("dbg_v", [S, HPC * 128], BF16, kind="ExternalOutput").ap()
        dbg["att"] = nc.dram_tensor("dbg_att", [HPC * 128, S], BF16, kind="ExternalOutput").ap()
        dbg["x2"] = nc.dram_tensor("dbg_x2", [TC, D], F32, kind="ExternalOutput").ap()
        dbg["h1"] = nc.dram_tensor("dbg_h1", [D, TC], F32, kind="ExternalOutput").ap()
        dbg["hl0"] = nc.dram_tensor("dbg_hl0", [D, TC], F32, kind="ExternalOutput").ap()
        dbg["modb"] = nc.dram_tensor("dbg_modb", [L * PW, 128, D], BF16, kind="ExternalOutput").ap()
        dbg["mrow"] = nc.dram_tensor("dbg_mrow", [1, 3 * L * PW * D], F32, kind="ExternalOutput").ap()

    bXlnD = Buf("xlnD"); bX2fD = Buf("x2fD")
    bAgIn = [[Buf(f"agIn{h}_{hf}") for hf in range(2)] for h in range(HPC)]
    bAgOut = [[Buf(f"agOut{h}_{hf}") for hf in range(2)] for h in range(HPC)]
    bX2D = Buf("x2D"); bX2lnD = Buf("x2lnD"); bMrD = Buf("mrD")
    bHD = [Buf("hD0"), Buf("hD1")]

    s_const = P.stream("const", "gpsimd")
    s_wq = P.stream("wq", "gpsimd")
    s_wo = P.stream("wo", "gpsimd")
    s_wo2 = P.stream("wo2", "scalar")
    s_x = [P.stream(f"x{i}", "sync") for i in range(2)]
    s_xlnst = P.stream("xlnst", "scalar")
    s_xfm = [P.stream(f"xfm{i}", "sync") for i in range(4)]
    s_agst = P.stream("agst", "scalar")
    s_agld = [P.stream(f"agld{i}", "sync") for i in range(2)]
    s_xs = [P.stream(f"xs{i}", "sync") for i in range(2)]
    s_x2st = P.stream("x2st", "scalar")
    s_x2lnst = P.stream("x2lnst", "scalar")
    s_x2fm = P.stream("x2fm", "sync")
    s_wa = [P.stream(f"wa{i}", "gpsimd") for i in range(4)]
    s_we = [P.stream(f"we{i}", "gpsimd") for i in range(2)]
    s_hb = [P.stream(f"hb{i}", "gpsimd") for i in range(2)]
    s_wp = [P.stream(f"wp{i}", "gpsimd") for i in range(2)]
    s_hsp = [P.stream(f"hsp{i}", "scalar") for i in range(2)]
    s_hld = [P.stream(f"hld{i}", "sync") for i in range(2)]
    s_mrst = P.stream("mrst", "scalar")
    s_mrld = P.stream("mrld", "sync")
    s_x2r = [P.stream(f"x2r{i}", "sync") for i in range(2)]
    s_toff = P.stream("toff", "sync")
    s_outst = P.stream("outst", "scalar")
    s_dbg = P.stream("dbg", "scalar") if DEBUG_DUMP else None
    cc_keys = [[P.custom_sem(f"cc{h}_{hf}") for hf in range(2)] for h in range(HPC)]
    bWoBf = Buf("woBf")
    bModbD = Buf("modbD")
    s_modb = P.stream("modb", "scalar")
    s_modl = P.stream("modl", "sync")

    ps = [ctx.enter_context(nc.psum_tensor(f"ps{i}", [128, 512], F32)) for i in range(8)]
    bPs = [Buf(f"ps{i}") for i in range(8)]

    ones_sb = ctx.enter_context(nc.sbuf_tensor("ones", [128, 512], BF16))
    ident_sb = ctx.enter_context(nc.sbuf_tensor("ident", [128, 128], F32))
    onesf_sb = ctx.enter_context(nc.sbuf_tensor("onesf", [1, 128], F32))
    onesfc_sb = ctx.enter_context(nc.sbuf_tensor("onesfc", [128, 1], F32))
    bias_qk = ctx.enter_context(nc.sbuf_tensor("bias_qk", [128, 2 * HPC], F32))
    fbias = ctx.enter_context(nc.sbuf_tensor("fbias", [128, FB_COLS], F32))
    stat6 = ctx.enter_context(nc.sbuf_tensor("stat6", [128, 2, 24], F32))
    stat2 = ctx.enter_context(nc.sbuf_tensor("stat2", [128, 2, 4], F32))

    bOnes = Buf("ones"); bIdent = Buf("ident"); bOnesF = Buf("onesf")
    bOnesFC = Buf("onesfc"); bBqk = Buf("bqk"); bFB = Buf("fbias")
    bRows = Buf("rows"); bStat6 = [Buf("stat6a"), Buf("stat6b")]; bStat2 = [Buf("stat2a"), Buf("stat2b")]

    fbc = lambda c: fbias[:, c:c + 1]
    EPS_AP = fbias[:, FB_EPS:FB_EPS + 1]
    ONESB = ones_sb

    def cl(dst, src):
        P.dma(s_const, lambda g, dst=dst, src=src: g.dma_start(out=dst, in_=src))

    cl(ident_sb[:, :], ident_in[:, :])
    cl(ones_sb[:, :], ones_in[:, :])
    cl(onesf_sb[:, :], onesf_in[:, :])
    cl(onesfc_sb[:, :], onesfc_in[:, :])
    cl(bias_qk[:, :], b_qk[:, :])
    cl(fbias[:, :], fb_in[:, :])
    P.stamp([bOnes, bIdent, bOnesF, bOnesFC, bBqk, bFB], s_const)

    NP4 = L * PW * NDC
    with nc.sbuf_tensor("modtmp", [128, 4 * NP4], F32) as mq:
        bMq = Buf("mq")
        a4 = mq[:, 0:NP4]; a8 = mq[:, NP4:2 * NP4]
        c4 = mq[:, 2 * NP4:3 * NP4]; tq = mq[:, 3 * NP4:4 * NP4]
        phs = fbias[:, FB_MOD:FB_MOD + NP4]
        P.op("scalar", lambda s, a4=a4, phs=phs: s.activation(a4, phs, AF.Sin, scale=0.25), reads=[bFB], writes=[bMq])
        P.op("scalar", lambda s, a8=a8, phs=phs: s.activation(a8, phs, AF.Sin, scale=0.125), reads=[bFB], writes=[bMq])
        P.op("vector", lambda v, c4=c4, a8=a8: v.tensor_mul(c4, a8, a8), reads=[bMq], writes=[bMq])
        P.op("vector", lambda v, c4=c4: v.tensor_scalar(c4, c4, -2.0, 1.0, ALU.mult, ALU.add), reads=[bMq], writes=[bMq])
        P.op("vector", lambda v, tq=tq, a4=a4, c4=c4: v.scalar_tensor_tensor(tq, a4, 2.0, c4, ALU.mult, ALU.mult), reads=[bMq], writes=[bMq])
        P.op("vector", lambda v, c4=c4, a4=a4: v.tensor_mul(c4, a4, a4), reads=[bMq], writes=[bMq])
        P.op("vector", lambda v, c4=c4: v.tensor_scalar(c4, c4, -2.0, 1.0, ALU.mult, ALU.add), reads=[bMq], writes=[bMq])
        P.op("vector", lambda v, a4=a4, tq=tq, c4=c4: v.scalar_tensor_tensor(a4, tq, 2.0, c4, ALU.mult, ALU.mult), reads=[bMq], writes=[bMq])
        P.op("vector", lambda v, a8=a8, tq=tq: v.tensor_mul(a8, tq, tq), reads=[bMq], writes=[bMq])
        P.op("vector", lambda v, a8=a8: v.tensor_scalar(a8, a8, -2.0, 1.0, ALU.mult, ALU.add), reads=[bMq], writes=[bMq])
        P.op("vector", lambda v, phs=phs, a8=a8, a4=a4: v.tensor_add(phs, a8, a4), reads=[bMq, bFB], writes=[bFB])
        for l in range(L):
            dstc = fbias[:, FB_BMS + l * NDC:FB_BMS + (l + 1) * NDC]
            p0b = fbias[:, FB_PB + (l * PW) * NDC:FB_PB + (l * PW + 1) * NDC]
            p0m = fbias[:, FB_MOD + (l * PW) * NDC:FB_MOD + (l * PW + 1) * NDC]
            p1b = fbias[:, FB_PB + (l * PW + 1) * NDC:FB_PB + (l * PW + 2) * NDC]
            p1m = fbias[:, FB_MOD + (l * PW + 1) * NDC:FB_MOD + (l * PW + 2) * NDC]
            tmp = mq[:, 0:NDC]
            P.op("vector", lambda v, dstc=dstc, p0b=p0b, p0m=p0m: v.tensor_mul(dstc, p0b, p0m), reads=[bFB], writes=[bFB])
            P.op("vector", lambda v, tmp=tmp, p1b=p1b, p1m=p1m: v.tensor_mul(tmp, p1b, p1m), reads=[bFB], writes=[bMq])
            P.op("vector", lambda v, dstc=dstc, tmp=tmp: v.tensor_add(dstc, dstc, tmp), reads=[bMq, bFB], writes=[bFB])
    P.barrier_all()

    with (
        nc.sbuf_tensor("mrow", [1, 3 * L * PW * D], F32) as mrow_sb,
        nc.sbuf_tensor("modb_t", [128, D], BF16) as modbt_sb,
    ):
        bMrow = Buf("mrow"); bModbT = Buf("modbT")
        NR = L * PW * D
        ra4 = mrow_sb[0:1, 0:NR]
        ra8 = mrow_sb[0:1, NR:2 * NR]
        rc4 = mrow_sb[0:1, 2 * NR:3 * NR]
        P.dma(s_modl, lambda sy: sy.dma_start(out=ra4, in_=phase_r[:, :]), writes=[bMrow])
        P.op("scalar", lambda s: s.activation(ra8, ra4, AF.Sin, scale=0.125), reads=[bMrow], writes=[bMrow])
        P.op("scalar", lambda s: s.activation(ra4, ra4, AF.Sin, scale=0.25), reads=[bMrow], writes=[bMrow])
        P.op("vector", lambda v: v.tensor_mul(rc4, ra8, ra8), reads=[bMrow], writes=[bMrow])
        P.op("vector", lambda v: v.tensor_scalar(rc4, rc4, -2.0, 1.0, ALU.mult, ALU.add), reads=[bMrow], writes=[bMrow])
        P.op("vector", lambda v: v.scalar_tensor_tensor(ra8, ra4, 2.0, rc4, ALU.mult, ALU.mult), reads=[bMrow], writes=[bMrow])
        P.op("vector", lambda v: v.tensor_mul(rc4, ra4, ra4), reads=[bMrow], writes=[bMrow])
        P.op("vector", lambda v: v.tensor_scalar(rc4, rc4, -2.0, 1.0, ALU.mult, ALU.add), reads=[bMrow], writes=[bMrow])
        P.op("vector", lambda v: v.scalar_tensor_tensor(ra4, ra8, 2.0, rc4, ALU.mult, ALU.mult), reads=[bMrow], writes=[bMrow])
        P.op("vector", lambda v: v.tensor_mul(rc4, ra8, ra8), reads=[bMrow], writes=[bMrow])
        P.op("vector", lambda v: v.tensor_scalar(rc4, rc4, -2.0, 1.0, ALU.mult, ALU.add), reads=[bMrow], writes=[bMrow])
        P.op("vector", lambda v: v.tensor_add(ra4, rc4, ra4), reads=[bMrow], writes=[bMrow])
        if DEBUG_DUMP:
            P.dma(s_dbg, lambda s: s.dma_start(out=dbg["mrow"][:, :], in_=mrow_sb[0:1, :]), reads=[bMrow])
        for lp in range(L * PW):
            for ch in range(4):
                P.op("tensor", MMG(ps[ch % 2][:, :],
                     [(onesf_sb[0:1, :], ra4[0:1, lp * D + ch * 512:lp * D + (ch + 1) * 512])]),
                     reads=[bMrow, bOnesF], writes=[bPs[ch % 2]])
                P.op("vector", lambda v, ch=ch: v.tensor_copy(
                    modbt_sb[:, ch * 512:(ch + 1) * 512], ps[ch % 2][:, :]),
                    reads=[bPs[ch % 2]], writes=[bModbT])
            P.dma(s_modb, lambda s, lp=lp: s.dma_start(
                out=modb_dram[lp, :, :], in_=modbt_sb[:, :]),
                reads=[bModbT], writes=[bModbD])

    if DEBUG_DUMP:
        P.dma(s_dbg, lambda s: s.dma_start(out=dbg["modb"][:, :, :], in_=modb_dram[:, :, :]), reads=[bModbD])
    P.barrier_all()

    with (
        nc.sbuf_tensor("q_sb", [128, HPC, S], BF16) as q_sb,
        nc.sbuf_tensor("k_sb", [128, HPC, S], BF16) as k_sb,
        nc.sbuf_tensor("v_sb", [128, S // 128, HPC * 128], BF16) as v_sb,
    ):
        bQ = Buf("q"); bK = Buf("k"); bV = Buf("v")

        with (
            nc.sbuf_tensor("wqk", [128, NDC, 2 * HPC * 128], BF16) as wqk_sb,
            nc.sbuf_tensor("wv", [128, NDC, HPC * 128], BF16) as wv_sb,
            nc.sbuf_tensor("bv", [1, HPC * 128], BF16) as bv_sb,
            nc.sbuf_tensor("x_t", [128, 2, D], F32) as x_t,
            nc.sbuf_tensor("xln_t", [128, D], BF16) as xln_t,
            nc.sbuf_tensor("xfm", [128, 2, NDC, 512], BF16) as xfm_sb,
        ):
            bWqk = Buf("wqk"); bWv = Buf("wv"); bBv = Buf("bv")
            bXt = [Buf("xt0"), Buf("xt1")]
            bXln = Buf("xln")
            bXfm = [Buf("xfm0"), Buf("xfm1")]

            P.dma(s_wq, lambda g: g.dma_start(
                out=wqk_sb[:, :, :],
                in_=w_qkv.rearrange("(c p) f -> p c f", p=128)[:, :, 0:2 * HPC * 128]),
                writes=[bWqk])
            P.dma(s_wq, lambda g: g.dma_start(
                out=wv_sb[:, :, :],
                in_=w_qkv.rearrange("(c p) f -> p c f", p=128)[:, :, 2 * HPC * 128:3 * HPC * 128]),
                writes=[bWv])
            P.dma(s_const, lambda g: g.dma_start(out=bv_sb[:, :], in_=b_v_row[:, :]), writes=[bBv])

            def qkv_prep(blk):
                pb = blk % 2
                for tt in range(4):
                    g_tt = blk * 4 + tt
                    pt = g_tt % 2
                    xt = x_t[:, pt, :]
                    P.dma(s_x[pt], lambda sy, g_tt=g_tt, xt=xt: sy.dma_start(
                        out=xt, in_=x_batch[g_tt * 128:(g_tt + 1) * 128, :]), writes=[bXt[pt]])
                    for qd in range(4):
                        P.op("vector", lambda v, xt=xt, qd=qd, pt=pt: v.bn_stats(
                            stat6[:, pt, qd * 6:qd * 6 + 6], xt[:, qd * 512:(qd + 1) * 512]),
                            reads=[bXt[pt]], writes=[bStat6[pt]])
                    P.op("vector", lambda v, pt=pt: v.bn_aggr(stat2[:, pt, 0:2], stat6[:, pt, 0:24]), reads=[bStat6[pt]], writes=[bStat2[pt]])
                    P.op("scalar", lambda s, pt=pt: s.activation(stat2[:, pt, 2:3], stat2[:, pt, 1:2], AF.Sqrt, bias=EPS_AP), reads=[bStat2[pt]], writes=[bStat2[pt]])
                    P.op("vector", lambda v, pt=pt: v.reciprocal(stat2[:, pt, 3:4], stat2[:, pt, 2:3]), reads=[bStat2[pt]], writes=[bStat2[pt]])
                    P.op("vector", lambda v, xt=xt, pt=pt: v.tensor_scalar(
                        xln_t[:, :], xt, stat2[:, pt, 0:1], stat2[:, pt, 3:4], ALU.subtract, ALU.mult),
                        reads=[bXt[pt], bStat2[pt]], writes=[bXln])
                    P.dma(s_xlnst, lambda s, g_tt=g_tt: s.dma_start(
                        out=xln_dram[g_tt * 128:(g_tt + 1) * 128, :], in_=xln_t[:, :]),
                        reads=[bXln], writes=[bXlnD])
                for dc in range(NDC):
                    P.dma(s_xfm[pb * 2 + dc % 2], lambda sy, blk=blk, dc=dc, pb=pb: sy.dma_start(
                        out=xfm_sb[:, pb, dc, :],
                        in_=xln_dram[blk * 512:(blk + 1) * 512, dc * 128:(dc + 1) * 128],
                        transpose=True), reads=[bXlnD], writes=[bXfm[pb]])

            def qkv_mms(blk):
                pb = blk % 2
                for fh in range(2 * HPC):
                    bank = fh % 4
                    fcol = fh * 128
                    pairs = [(wqk_sb[:, dc, fcol:fcol + 128], xfm_sb[:, pb, dc, :]) for dc in range(NDC)]
                    P.op("tensor", MMG(ps[bank][:, :], pairs),
                         reads=[bWqk, bXfm[pb]], writes=[bPs[bank]])
                    h = fh % HPC
                    dst = (q_sb if fh < HPC else k_sb)[:, h, blk * 512:(blk + 1) * 512]
                    dstbuf = bQ if fh < HPC else bK
                    P.op("vector", lambda v, dst=dst, bank=bank, fh=fh: v.tensor_scalar(
                        dst, ps[bank][:, :], bias_qk[:, fh:fh + 1], None, ALU.add),
                        reads=[bPs[bank], bBqk], writes=[dstbuf])
                for tt in range(4):
                    g_tt = blk * 4 + tt
                    bank = 4 + tt % 2
                    pairs = [(xfm_sb[:, pb, dc, tt * 128:(tt + 1) * 128], wv_sb[:, dc, :]) for dc in range(NDC)]
                    pairs.append((ONESB[0:1, 0:128], bv_sb[0:1, :]))
                    P.op("tensor", MMG(ps[bank][:, :], pairs),
                         reads=[bWv, bXfm[pb], bOnes, bBv], writes=[bPs[bank]])
                    P.op("vector", lambda v, g_tt=g_tt, bank=bank: v.tensor_copy(
                        v_sb[:, g_tt, :], ps[bank][:, :]), reads=[bPs[bank]], writes=[bV])

            qkv_prep(0)
            for blk in range(1, 8):
                qkv_prep(blk)
                qkv_mms(blk - 1)
            qkv_mms(7)
        P.barrier_all()

        if DEBUG_DUMP:
            for h in range(HPC):
                P.dma(s_dbg, lambda s, h=h: s.dma_start(out=dbg["q"][h * 128:(h + 1) * 128, :], in_=q_sb[:, h, :]), reads=[bQ])
                P.dma(s_dbg, lambda s, h=h: s.dma_start(out=dbg["k"][h * 128:(h + 1) * 128, :], in_=k_sb[:, h, :]), reads=[bK])
            for g_tt in range(32):
                P.dma(s_dbg, lambda s, g_tt=g_tt: s.dma_start(out=dbg["v"][g_tt * 128:(g_tt + 1) * 128, :], in_=v_sb[:, g_tt, :]), reads=[bV])

        att_scope = contextlib.ExitStack()
        masks_sb = att_scope.enter_context(nc.sbuf_tensor("masks", [128, 4 * 512], BF16))
        E_sb = att_scope.enter_context(nc.sbuf_tensor("E_sb", [128, 4, 512], BF16))
        rows = att_scope.enter_context(nc.sbuf_tensor("rows_a", [1, 1024], F32))
        rcp_sb = att_scope.enter_context(nc.sbuf_tensor("rcp", [128, 2, 512], F32))
        oev_sb = att_scope.enter_context(nc.sbuf_tensor("oev", [128, 2, 512], BF16))
        bMasks = Buf("masks")
        bE = [Buf("E0"), Buf("E1"), Buf("E2"), Buf("E3")]
        bRcp = [Buf("rcp0"), Buf("rcp1")]
        bOev = [Buf("oev0"), Buf("oev1")]
        MASK = lambda j: masks_sb[:, j * 512:(j + 1) * 512]
        P.dma(s_const, lambda g: g.dma_start(out=masks_sb[:, :], in_=masks_in[:, :]), writes=[bMasks])
        def emit_score(side, h, qc, kt):
            bank_sc = 2 * side + kt % 2
            P.op("tensor", MMG(ps[bank_sc][:, :],
                 [(k_sb[:, h, kt * 128:(kt + 1) * 128], q_sb[:, h, qc * 512:(qc + 1) * 512])]),
                 reads=[bK, bQ], writes=[bPs[bank_sc]])

        def make_tail(side, h, qc):
            bank_o = 4 + side
            bank_s = 6 + side

            def tail():
                rrow = rows[0:1, 512 * side:512 * side + 512]
                P.op("vector", lambda v, bank_s=bank_s, rrow=rrow: v.reciprocal(rrow, ps[bank_s][0:1, :]),
                     reads=[bPs[bank_s]], writes=[bRows])
                P.op("tensor", MMG(ps[bank_s][:, :], [(onesf_sb[0:1, :], rrow)]),
                     reads=[bRows, bOnesF], writes=[bPs[bank_s]])
                P.op("vector", lambda v, side=side, bank_s=bank_s: v.tensor_copy(
                    rcp_sb[:, side, :], ps[bank_s][:, :]), reads=[bPs[bank_s]], writes=[bRcp[side]])
                P.op("vector", lambda v, side=side, bank_o=bank_o: v.tensor_mul(
                    oev_sb[:, side, :], ps[bank_o][:, :], rcp_sb[:, side, :]),
                    reads=[bPs[bank_o], bRcp[side]], writes=[bOev[side]])
                P.dma(s_agst, lambda s, h=h, qc=qc, side=side: s.dma_start(
                    out=ag_in[qc // 4, h * 128:(h + 1) * 128, (qc % 4) * 512:(qc % 4 + 1) * 512],
                    in_=oev_sb[:, side, :]), reads=[bOev[side]], writes=[bAgIn[h][qc // 4]])
            return tail

        def emit_consume(side, h, qc, kt, nkt):
            pe = 2 * side + kt % 2
            bank_sc = 2 * side + kt % 2
            bank_o = 4 + side
            bank_s = 6 + side
            P.op("scalar", lambda s, pe=pe, bank_sc=bank_sc: s.activation(
                E_sb[:, pe, :], ps[bank_sc][:, :], AF.Exp, scale=SCALE),
                reads=[bPs[bank_sc]], writes=[bE[pe]])
            j = kt - 4 * qc
            if j >= 0:
                P.op("vector", lambda v, pe=pe, j=j: v.tensor_mul(
                    E_sb[:, pe, :], E_sb[:, pe, :], MASK(j)),
                    reads=[bE[pe], bMasks], writes=[bE[pe]])
            if kt + 2 < nkt:
                emit_score(side, h, qc, kt + 2)
            P.op("tensor", lambda t, pe=pe, bank_s=bank_s, kt=kt, nkt=nkt: t.matmul(
                ps[bank_s][0:1, :], ONESB[:, 0:1], E_sb[:, pe, :],
                start=(kt == 0), stop=(kt == nkt - 1)),
                reads=[bE[pe], bOnes], writes=[bPs[bank_s]])
            P.op("tensor", lambda t, pe=pe, bank_o=bank_o, kt=kt, nkt=nkt, h=h: t.matmul(
                ps[bank_o][:, :], v_sb[:, kt, h * 128:(h + 1) * 128], E_sb[:, pe, :],
                start=(kt == 0), stop=(kt == nkt - 1)),
                reads=[bE[pe], bV], writes=[bPs[bank_o]])

        def emit_cc(h, hf):
            r0 = hf * HPC * 512 + h * 512
            P.op_custom("gpsimd", lambda g, h=h, hf=hf, r0=r0: g.collective_compute(
                "AllGather", ALU.bypass,
                replica_groups=[[0, 1, 2, 3], [4, 5, 6, 7]],
                ins=[ag_in[hf, h * 128:(h + 1) * 128, :].opt()],
                outs=[ag_out[r0:r0 + 512, :].opt()]),
                cc_keys[h][hf], 1, reads=[bAgIn[h][hf]], writes=[bAgOut[h][hf]])

        pending_tails = []
        for hp in (0, 2):
            for qc in range(8):
                hA, hB = hp, hp + 1
                nkt = 4 * qc + 4
                emit_score(0, hA, qc, 0)
                emit_score(1, hB, qc, 0)
                if nkt > 1:
                    emit_score(0, hA, qc, 1)
                    emit_score(1, hB, qc, 1)
                for t in pending_tails:
                    t()
                pending_tails = []
                for kt in range(nkt):
                    emit_consume(0, hA, qc, kt, nkt)
                    emit_consume(1, hB, qc, kt, nkt)
                pending_tails = [make_tail(0, hA, qc), make_tail(1, hB, qc)]
                if qc == 4:
                    emit_cc(hA, 0)
                    emit_cc(hB, 0)
            for t in pending_tails:
                t()
            pending_tails = []
            emit_cc(hp, 1)
            emit_cc(hp + 1, 1)

        if DEBUG_DUMP:
            for h in range(HPC):
                P.dma(s_dbg, lambda s, h=h: s.dma_start(
                    out=dbg["att"][h * 128:(h + 1) * 128, :], in_=ag_in[h * 128:(h + 1) * 128, :]), reads=[bAgIn[h]])

        att_scope.close()

    P.barrier_all()

    with (
        nc.sbuf_tensor("acat", [128, NDC, TC // 128, 128], BF16) as acat_sb,
        nc.sbuf_tensor("wo", [128, NDC, D], BF16) as wo_sb,
        nc.sbuf_tensor("bo", [1, D], BF16) as bo_sb,
        nc.sbuf_tensor("toff", [1, NDC], I32) as toff_t,
        nc.sbuf_tensor("xs_t", [128, 2, D], F32) as xs_t,
        nc.sbuf_tensor("ao_t", [128, 2, D], F32) as ao_t,
        nc.sbuf_tensor("x2_t", [128, 2, D], F32) as x2_t,
        nc.sbuf_tensor("x2ln_t", [128, 2, D], BF16) as x2ln_t,
    ):
        bAcat = Buf("acat"); bWo = Buf("wo"); bBo = Buf("bo"); bToff = Buf("toff")
        bXs = [Buf("xs0"), Buf("xs1")]
        bAo = [Buf("ao0"), Buf("ao1")]
        bX2 = [Buf("x20"), Buf("x21")]
        bX2ln = [Buf("x2ln0"), Buf("x2ln1")]

        P.dma(s_const, lambda g: g.dma_start(out=bo_sb[:, :], in_=b_out_row[:, :]), writes=[bBo])
        P.dma(s_toff, lambda sy: sy.dma_start(out=toff_t[:, :], in_=tok_off[:, :]), writes=[bToff])

        reg_cell = [None]

        def ag_slice_emit(dc):
            def emit(sy):
                if reg_cell[0] is None:
                    reg_cell[0] = sy.alloc_register("tokoff")
                sy.reg_load(reg_cell[0], toff_t[:1, dc:dc + 1])
                off = sy.snap(reg_cell[0], min_val=0, max_val=2 * HPC * 512 * (S // 2))
                agf = ag_out.rearrange("a b -> (a b)")
                return sy.dma_start(
                    out=acat_sb[:, dc, :, :].rearrange("p t f -> p (t f)"),
                    in_=bass.AP(agf.tensor, off, [[S // 2, 128], [1, TC]]))
            return emit

        for dc in range(NDC):
            hh = dc % 4
            P.dma(s_agld[dc % 2], ag_slice_emit(dc),
                  reads=[bAgOut[hh][0], bAgOut[hh][1], bToff], writes=[bAcat])
        P.dma(s_wo, lambda g: g.dma_start(
            out=wo_sb[:, :, :], in_=w_out_d.rearrange("(c p) f -> p c f", p=128)),
            writes=[bWo])

        for ttc in range(TC // 128):
            pt = ttc % 2
            P.dma(s_xs[pt], lambda sy, ttc=ttc, pt=pt: sy.dma_start(
                out=xs_t[:, pt, :], in_=x_slice[ttc * 128:(ttc + 1) * 128, :]), writes=[bXs[pt]])
            for f2c in range(4):
                bank = pt * 4 + f2c
                pairs = [(acat_sb[:, dc, ttc, :], wo_sb[:, dc, f2c * 512:(f2c + 1) * 512]) for dc in range(NDC)]
                pairs.append((ONESB[0:1, 0:128], bo_sb[0:1, f2c * 512:(f2c + 1) * 512]))
                P.op("tensor", MMG(ps[bank][:, :], pairs),
                     reads=[bAcat, bWo, bOnes, bBo], writes=[bPs[bank]])
            for f2c in range(4):
                bank = pt * 4 + f2c
                P.op("vector", lambda v, bank=bank, f2c=f2c, pt=pt: v.bn_stats(
                    stat6[:, pt, f2c * 6:f2c * 6 + 6], ps[bank][:, :]),
                    reads=[bPs[bank]], writes=[bStat6[pt]])
            P.op("vector", lambda v, pt=pt: v.bn_aggr(stat2[:, pt, 0:2], stat6[:, pt, 0:24]), reads=[bStat6[pt]], writes=[bStat2[pt]])
            P.op("scalar", lambda s, pt=pt: s.activation(stat2[:, pt, 2:3], stat2[:, pt, 1:2], AF.Sqrt, bias=EPS_AP), reads=[bStat2[pt]], writes=[bStat2[pt]])
            P.op("vector", lambda v, pt=pt: v.reciprocal(stat2[:, pt, 3:4], stat2[:, pt, 2:3]), reads=[bStat2[pt]], writes=[bStat2[pt]])
            for f2c in range(4):
                bank = pt * 4 + f2c
                P.op("vector", lambda v, bank=bank, pt=pt, f2c=f2c: v.tensor_scalar(
                    ao_t[:, pt, f2c * 512:(f2c + 1) * 512], ps[bank][:, :],
                    stat2[:, pt, 0:1], stat2[:, pt, 3:4], ALU.subtract, ALU.mult),
                    reads=[bPs[bank], bStat2[pt]], writes=[bAo[pt]])
            P.op("vector", lambda v, pt=pt: v.tensor_add(
                x2_t[:, pt, :], ao_t[:, pt, :], xs_t[:, pt, :]),
                reads=[bAo[pt], bXs[pt]], writes=[bX2[pt]])
            P.dma(s_x2st, lambda s, ttc=ttc, pt=pt: s.dma_start(
                out=x2_dram[ttc * 128:(ttc + 1) * 128, :], in_=x2_t[:, pt, :]),
                reads=[bX2[pt]], writes=[bX2D])
            for qd in range(4):
                P.op("vector", lambda v, pt=pt, qd=qd: v.bn_stats(
                    stat6[:, pt, qd * 6:qd * 6 + 6], x2_t[:, pt, qd * 512:(qd + 1) * 512]),
                    reads=[bX2[pt]], writes=[bStat6[pt]])
            P.op("vector", lambda v, pt=pt: v.bn_aggr(stat2[:, pt, 0:2], stat6[:, pt, 0:24]), reads=[bStat6[pt]], writes=[bStat2[pt]])
            P.op("scalar", lambda s, pt=pt: s.activation(stat2[:, pt, 2:3], stat2[:, pt, 1:2], AF.Sqrt, bias=EPS_AP), reads=[bStat2[pt]], writes=[bStat2[pt]])
            P.op("vector", lambda v, pt=pt: v.reciprocal(stat2[:, pt, 3:4], stat2[:, pt, 2:3]), reads=[bStat2[pt]], writes=[bStat2[pt]])
            P.op("vector", lambda v, pt=pt: v.tensor_scalar(
                x2ln_t[:, pt, :], x2_t[:, pt, :], stat2[:, pt, 0:1], stat2[:, pt, 3:4],
                ALU.subtract, ALU.mult), reads=[bX2[pt], bStat2[pt]], writes=[bX2ln[pt]])
            P.dma(s_x2lnst, lambda s, ttc=ttc, pt=pt: s.dma_start(
                out=x2ln_dram[ttc * 128:(ttc + 1) * 128, :], in_=x2ln_t[:, pt, :]),
                reads=[bX2ln[pt]], writes=[bX2lnD])

        if DEBUG_DUMP:
            P.dma(s_dbg, lambda s: s.dma_start(out=dbg["x2"][:, :], in_=x2_dram[:, :]), reads=[bX2D])

    P.barrier_all()

    hbf_sb = None

    def ln_fm(P, src, bSrc, sqt_sb, bSqt, bc, bBc, bank0, bank1, g_col=None, b_col=None,
              dstbf=None, bDstBf=None):
        for th in range(2):
            pairs = [(onesfc_sb[:, 0:1], src[dc][:, th * 512:(th + 1) * 512]) for dc in range(NDC)]
            P.op("tensor", MMG(ps[bank0][0:1, :], pairs), reads=[bSrc, bOnesFC], writes=[bPs[bank0]])
            P.op("vector", lambda v, th=th: v.tensor_scalar(
                rows[0:1, th * 512:(th + 1) * 512], ps[bank0][0:1, :], 1.0 / D, None, ALU.mult),
                reads=[bPs[bank0]], writes=[bRows])
            for dc in range(NDC):
                sl = dc % 4
                P.op("scalar", lambda s, dc=dc, sl=sl, th=th: s.activation(
                    sqt_sb[:, sl, :], src[dc][:, th * 512:(th + 1) * 512], AF.Square),
                    reads=[bSrc], writes=[bSqt[sl]])
                P.op("tensor", lambda t, sl=sl, dc=dc, bank1=bank1: t.matmul(
                    ps[bank1][0:1, :], onesfc_sb[:, 0:1], sqt_sb[:, sl, :],
                    start=(dc == 0), stop=(dc == NDC - 1)),
                    reads=[bSqt[sl], bOnesFC], writes=[bPs[bank1]])
            P.op("vector", lambda v, th=th: v.tensor_scalar(
                rows[0:1, 1024 + th * 512:1024 + (th + 1) * 512], ps[bank1][0:1, :], 1.0 / D, None, ALU.mult),
                reads=[bPs[bank1]], writes=[bRows])
        for th in range(2):
            sc = rows[0:1, 2048:2560]
            mm_ = rows[0:1, th * 512:(th + 1) * 512]
            vv = rows[0:1, 1024 + th * 512:1024 + (th + 1) * 512]
            P.op("vector", lambda v, sc=sc, mm_=mm_: v.tensor_mul(sc, mm_, mm_), reads=[bRows], writes=[bRows])
            P.op("vector", lambda v, vv=vv, sc=sc: v.tensor_sub(vv, vv, sc), reads=[bRows], writes=[bRows])
            P.op("scalar", lambda s, sc=sc, vv=vv: s.activation(sc, vv, AF.Sqrt, bias=EPS_AP[0:1, :]), reads=[bRows], writes=[bRows])
            P.op("vector", lambda v, vv=vv, sc=sc: v.reciprocal(vv, sc), reads=[bRows], writes=[bRows])
        for th in range(2):
            mm_ = rows[0:1, th * 512:(th + 1) * 512]
            rr = rows[0:1, 1024 + th * 512:1024 + (th + 1) * 512]
            P.op("tensor", MMG(ps[bank0][:, :], [(onesf_sb[0:1, :], mm_)]), reads=[bRows, bOnesF], writes=[bPs[bank0]])
            P.op("vector", lambda v, th=th: v.tensor_copy(bc[:, 0, th * 512:(th + 1) * 512], ps[bank0][:, :]),
                 reads=[bPs[bank0]], writes=[bBc])
            P.op("tensor", MMG(ps[bank1][:, :], [(onesf_sb[0:1, :], rr)]), reads=[bRows, bOnesF], writes=[bPs[bank1]])
            P.op("vector", lambda v, th=th: v.tensor_copy(bc[:, 1, th * 512:(th + 1) * 512], ps[bank1][:, :]),
                 reads=[bPs[bank1]], writes=[bBc])
        for th in range(2):
            hsl = slice(th * 512, (th + 1) * 512)
            for dc in range(NDC):
                P.op("vector", lambda v, dc=dc, hsl=hsl: v.tensor_sub(
                    src[dc][:, hsl], src[dc][:, hsl], bc[:, 0, hsl]),
                    reads=[bSrc, bBc], writes=[bSrc])
                P.op("vector", lambda v, dc=dc, hsl=hsl: v.tensor_mul(
                    src[dc][:, hsl], src[dc][:, hsl], bc[:, 1, hsl]),
                    reads=[bSrc, bBc], writes=[bSrc])
                if g_col is not None:
                    P.op("vector", lambda v, dc=dc, hsl=hsl: v.tensor_scalar(
                        src[dc][:, hsl], src[dc][:, hsl], g_col(dc), b_col(dc), ALU.mult, ALU.add),
                        reads=[bSrc, bFB], writes=[bSrc])
                if dstbf is not None:
                    P.op("vector", lambda v, dc=dc, hsl=hsl: v.tensor_copy(
                        dstbf[dc][:, hsl], src[dc][:, hsl]),
                        reads=[bSrc], writes=[bDstBf])

    with (
        nc.sbuf_tensor("hbf", [128, NDC, TC], BF16) as hbf_sb,
        nc.sbuf_tensor("x2fm", [128, NDC, TC], BF16) as x2fm_sb,
        nc.sbuf_tensor("bigA", [128, 3, NDC, 512], F32) as bigA_sb,
        nc.sbuf_tensor("wsl1", [128, 4, NDC, 128], BF16) as wsl1_sb,
        nc.sbuf_tensor("ht1", [128, 4, 512], F32) as ht1_sb,
    ):
        bHbf = Buf("hbf"); bX2fm = Buf("x2fm"); bBigA = Buf("bigA")
        bWsl1 = [Buf("wsl1a"), Buf("wsl1b"), Buf("wsl1c"), Buf("wsl1d")]
        bHt1 = [Buf("ht1a"), Buf("ht1b"), Buf("ht1c"), Buf("ht1d")]

        for dc in range(NDC):
            P.dma(s_x2fm, lambda sy, dc=dc: sy.dma_start(
                out=x2fm_sb[:, dc, :],
                in_=x2ln_dram[:, dc * 128:(dc + 1) * 128], transpose=True),
                reads=[bX2lnD], writes=[bX2fm])

        TH2 = 512
        tanh_t = lambda fc: bigA_sb[:, 0, fc, :]
        s2_t = lambda fc: bigA_sb[:, 1, fc, :]
        s4_t = lambda fc: bigA_sb[:, 2, fc, :]
        for tqi in range(2):
            tsl = slice(tqi * TH2, (tqi + 1) * TH2)
            for fc in range(NDC):
                sl = fc % 4
                P.dma(s_wa[sl], lambda g, fc=fc, sl=sl: g.dma_start(
                    out=wsl1_sb[:, sl, :, :],
                    in_=ph_w_d[:, fc * 128:(fc + 1) * 128].rearrange("(c p) f -> p c f", p=128)),
                    writes=[bWsl1[sl]])
                bank = fc % 4
                pairs = [(wsl1_sb[:, sl, dc, :], x2fm_sb[:, dc, tsl]) for dc in range(NDC)]
                P.op("tensor", MMG(ps[bank][:, :], pairs), reads=[bWsl1[sl], bX2fm], writes=[bPs[bank]])
                P.op("scalar", lambda s, fc=fc, bank=bank: s.activation(
                    tanh_t(fc), ps[bank][:, :], AF.Tanh, bias=fbias[:, FB_PHB + fc:FB_PHB + fc + 1]),
                    reads=[bPs[bank], bFB], writes=[bBigA])
            for fc in range(NDC):
                P.op("scalar", lambda s, fc=fc: s.activation(s2_t(fc), tanh_t(fc), AF.Sin, scale=math.pi / 2),
                     reads=[bBigA], writes=[bBigA])
                P.op("scalar", lambda s, fc=fc: s.activation(s4_t(fc), tanh_t(fc), AF.Sin, scale=math.pi / 4),
                     reads=[bBigA], writes=[bBigA])
            amp_ps = []
            for fc in range(NDC):
                sl = fc % 4
                P.dma(s_wa[sl], lambda g, fc=fc, sl=sl: g.dma_start(
                    out=wsl1_sb[:, sl, :, :],
                    in_=amp_w_d[:, fc * 128:(fc + 1) * 128].rearrange("(c p) f -> p c f", p=128)),
                    writes=[bWsl1[sl]])
                bank = fc % 4
                pairs = [(wsl1_sb[:, sl, dc, :], x2fm_sb[:, dc, tsl]) for dc in range(NDC)]
                P.op("tensor", MMG(ps[bank][:, :], pairs), reads=[bWsl1[sl], bX2fm], writes=[bPs[bank]])
                amp_ps.append(bank)
                sig = ht1_sb[:, 2 + fc % 2, :]
                P.op("scalar", lambda s, bank=bank, sig=sig, fc=fc: s.activation(
                    sig, ps[bank][:, :], AF.Sigmoid, bias=fbias[:, FB_AMPB + fc:FB_AMPB + fc + 1]),
                    reads=[bPs[bank], bFB], writes=[bHt1[2 + fc % 2]])
                cs = tanh_t(fc)
                u = ht1_sb[:, fc % 2, :]
                P.op("vector", lambda v, fc=fc, u=u: v.tensor_mul(u, s4_t(fc), s4_t(fc)), reads=[bBigA], writes=[bHt1[fc % 2]])
                P.op("vector", lambda v, u=u: v.tensor_scalar(u, u, -2.0, 1.0, ALU.mult, ALU.add), reads=[bHt1[fc % 2]], writes=[bHt1[fc % 2]])
                P.op("vector", lambda v, fc=fc, u=u: v.scalar_tensor_tensor(u, s2_t(fc), 2.0, u, ALU.mult, ALU.mult),
                     reads=[bBigA, bHt1[fc % 2]], writes=[bHt1[fc % 2]])
                P.op("vector", lambda v, fc=fc, cs=cs: v.tensor_mul(cs, s2_t(fc), s2_t(fc)), reads=[bBigA], writes=[bBigA])
                P.op("vector", lambda v, cs=cs: v.tensor_scalar(cs, cs, -2.0, 1.0, ALU.mult, ALU.add), reads=[bBigA], writes=[bBigA])
                P.op("vector", lambda v, cs=cs, u=u: v.tensor_add(cs, cs, u), reads=[bBigA, bHt1[fc % 2]], writes=[bBigA])
                hf = ht1_sb[:, fc % 2, :]
                P.op("vector", lambda v, fc=fc, sig=sig, hf=hf: v.tensor_mul(hf, tanh_t(fc), sig),
                     reads=[bBigA, bHt1[2 + fc % 2]], writes=[bHt1[fc % 2]])
                P.op("vector", lambda v, fc=fc, hf=hf, tsl=tsl: v.tensor_copy(hbf_sb[:, fc, tsl], hf),
                     reads=[bHt1[fc % 2]], writes=[bHbf])
                P.dma(s_hsp[fc % 2], lambda s, fc=fc, tqi=tqi, hf=hf: s.dma_start(
                    out=h_dram[0][fc * 128:(fc + 1) * 128, tqi * TH2:(tqi + 1) * TH2], in_=hf),
                    reads=[bHt1[fc % 2]], writes=[bHD[0]])

    P.barrier_all()
    if DEBUG_DUMP:
        P.dma(s_dbg, lambda s: s.dma_start(out=dbg["h1"][:, :], in_=h_dram[0][:, :]), reads=[bHD[0]])

    with (
        nc.sbuf_tensor("ea", [128, NDC, TC], F32) as ea_sb,
        nc.sbuf_tensor("sqt", [128, 4, 512], F32) as sqt_sb,
        nc.sbuf_tensor("rows_b", [1, 2560], F32) as rows,
    ):
        bEa = Buf("ea")
        bSqt = [Buf("sqt0"), Buf("sqt1"), Buf("sqt2"), Buf("sqt3")]
        EA = lambda dc: ea_sb[:, dc, :]
        layer_scope = contextlib.ExitStack()
        hbf2_sb = layer_scope.enter_context(nc.sbuf_tensor("hbf2", [128, NDC, TC], BF16))
        eh_sb = layer_scope.enter_context(nc.sbuf_tensor("eh", [128, 16, TC], BF16))
        wsl_sb = layer_scope.enter_context(nc.sbuf_tensor("wsl", [128, 2, NDC, 128], BF16))
        pwsl_sb = layer_scope.enter_context(nc.sbuf_tensor("pwsl", [128, 2, 16, 128], BF16))
        ht2_sb = layer_scope.enter_context(nc.sbuf_tensor("ht2", [128, 2, TC], F32))
        bc2_sb = layer_scope.enter_context(nc.sbuf_tensor("bc2", [128, 2, TC], F32))
        modb_sb = layer_scope.enter_context(nc.sbuf_tensor("modb", [128, 2, D], BF16))
        bModb = Buf("modb")
        bHbf2 = Buf("hbf2"); bEh = Buf("eh")
        bWsl = [Buf("wsl0"), Buf("wsl1")]
        bPwsl = [Buf("pwsl0"), Buf("pwsl1")]
        bHt2 = [Buf("ht2a"), Buf("ht2b")]
        bBc2 = Buf("bc2")

        for dc in range(NDC):
            P.dma(s_hb[dc % 2], lambda g, dc=dc: g.dma_start(
                out=hbf2_sb[:, dc, :], in_=h_dram[0][dc * 128:(dc + 1) * 128, :]),
                reads=[bHD[0]], writes=[bHbf2])

        cur_h = 0
        for l in range(L):
            for fq in range(4):
                for fc_l in range(16):
                    fc = fq * 16 + fc_l
                    sl = fc % 2
                    P.dma(s_we[sl], lambda g, l=l, fc=fc, sl=sl: g.dma_start(
                        out=wsl_sb[:, sl, :, :],
                        in_=ent_w_d[l, :, fc * 128:(fc + 1) * 128].rearrange("(c p) f -> p c f", p=128)),
                        writes=[bWsl[sl]])
                    for th in range(2):
                        bank = (fc * 2 + th) % 4
                        pairs = [(wsl_sb[:, sl, dc, :], hbf2_sb[:, dc, th * 512:(th + 1) * 512]) for dc in range(NDC)]
                        P.op("tensor", MMG(ps[bank][:, :], pairs), reads=[bWsl[sl], bHbf2], writes=[bPs[bank]])
                        P.op("vector", lambda v, fc_l=fc_l, th=th, bank=bank, l=l, fc=fc: v.tensor_scalar(
                            eh_sb[:, fc_l, th * 512:(th + 1) * 512], ps[bank][:, :],
                            fbias[:, FB_ENTB + l * NFC + fc:FB_ENTB + l * NFC + fc + 1], None, ALU.add),
                            reads=[bPs[bank], bFB], writes=[bEh])
                for d2c in range(NDC):
                    sl = d2c % 2
                    P.dma(s_wp[sl], lambda g, l=l, fq=fq, d2c=d2c, sl=sl: g.dma_start(
                        out=pwsl_sb[:, sl, :, :],
                        in_=ent_pw_d[l, fq * 2048:(fq + 1) * 2048, d2c * 128:(d2c + 1) * 128].rearrange("(c p) f -> p c f", p=128)),
                        writes=[bPwsl[sl]])
                    for th in range(2):
                        bank = 4 + (d2c * 2 + th) % 4
                        pairs = [(pwsl_sb[:, sl, fc_l, :], eh_sb[:, fc_l, th * 512:(th + 1) * 512]) for fc_l in range(16)]
                        P.op("tensor", MMG(ps[bank][:, :], pairs), reads=[bPwsl[sl], bEh], writes=[bPs[bank]])
                        if fq == 0:
                            P.op("vector", lambda v, d2c=d2c, th=th, bank=bank, l=l: v.tensor_scalar(
                                ea_sb[:, d2c, th * 512:(th + 1) * 512], ps[bank][:, :],
                                fbias[:, FB_ENTPB + l * NDC + d2c:FB_ENTPB + l * NDC + d2c + 1], None, ALU.add),
                                reads=[bPs[bank], bFB], writes=[bEa])
                        else:
                            P.op("vector", lambda v, d2c=d2c, th=th, bank=bank: v.tensor_add(
                                ea_sb[:, d2c, th * 512:(th + 1) * 512],
                                ea_sb[:, d2c, th * 512:(th + 1) * 512], ps[bank][:, :]),
                                reads=[bPs[bank], bEa], writes=[bEa])
            for dc in range(NDC):
                pt = dc % 2
                P.dma(s_hld[pt], lambda sy, dc=dc, pt=pt, cur_h=cur_h: sy.dma_start(
                    out=ht2_sb[:, pt, :], in_=h_dram[cur_h][dc * 128:(dc + 1) * 128, :]),
                    reads=[bHD[cur_h]], writes=[bHt2[pt]])
                P.op("vector", lambda v, dc=dc, pt=pt: v.tensor_add(
                    ea_sb[:, dc, :], ea_sb[:, dc, :], ht2_sb[:, pt, :]),
                    reads=[bEa, bHt2[pt]], writes=[bEa])
            ln_fm(P, [EA(dc) for dc in range(NDC)], bEa, sqt_sb, bSqt, bc2_sb, bBc2, 6, 7,
                  g_col=lambda dc, l=l: fbias[:, FB_LN + (l * 4 + 0) * NDC + dc:FB_LN + (l * 4 + 0) * NDC + dc + 1],
                  b_col=lambda dc, l=l: fbias[:, FB_LN + (l * 4 + 1) * NDC + dc:FB_LN + (l * 4 + 1) * NDC + dc + 1],
                  dstbf=[hbf2_sb[:, dc, :] for dc in range(NDC)], bDstBf=bHbf2)
            for p in range(PW):
                P.dma(s_modl, lambda sy, l=l, p=p: sy.dma_start(
                    out=modb_sb[:, p, :], in_=modb_dram[l * PW + p, :, :]),
                    reads=[bModbD], writes=[bModb])
            for fc in range(NDC):
                sl = fc % 2
                P.dma(s_wa[sl], lambda g, l=l, fc=fc, sl=sl: g.dma_start(
                    out=wsl_sb[:, sl, :, :],
                    in_=path_w_d[l, 0, :, fc * 128:(fc + 1) * 128].rearrange("(c p2) f -> p2 c f", p2=128)),
                    writes=[bWsl[sl]])
                P.dma(s_wp[sl], lambda g, l=l, fc=fc, sl=sl: g.dma_start(
                    out=pwsl_sb[:, sl, :, :],
                    in_=path_w_d[l, 1, :, fc * 128:(fc + 1) * 128].rearrange("(c p2) f -> p2 c f", p2=128)),
                    writes=[bPwsl[sl]])
                mb0 = bass.AP(modb_sb, fc * 128, [[2 * D, 128], [0, NDC], [1, 128]])
                mb1 = bass.AP(modb_sb, D + fc * 128, [[2 * D, 128], [0, NDC], [1, 128]])
                P.op("vector", lambda v, sl=sl, mb0=mb0: v.tensor_tensor(
                    wsl_sb[:, sl, :, :], wsl_sb[:, sl, :, :], mb0, ALU.mult),
                    reads=[bWsl[sl], bModb], writes=[bWsl[sl]])
                P.op("vector", lambda v, sl=sl, mb1=mb1: v.tensor_tensor(
                    pwsl_sb[:, sl, :, :], pwsl_sb[:, sl, :, :], mb1, ALU.mult),
                    reads=[bPwsl[sl], bModb], writes=[bPwsl[sl]])
                P.op("vector", lambda v, sl=sl: v.tensor_tensor(
                    wsl_sb[:, sl, :, :].rearrange("p c f -> p (c f)"),
                    wsl_sb[:, sl, :, :].rearrange("p c f -> p (c f)"),
                    pwsl_sb[:, sl, :, :].rearrange("p c f -> p (c f)"), ALU.add),
                    reads=[bWsl[sl], bPwsl[sl]], writes=[bWsl[sl]])
                for th in range(2):
                    bank = (fc * 2 + th) % 4
                    pairs = [(wsl_sb[:, sl, dc, :], hbf2_sb[:, dc, th * 512:(th + 1) * 512]) for dc in range(NDC)]
                    P.op("tensor", MMG(ps[bank][:, :], pairs),
                         reads=[bWsl[sl], bHbf2], writes=[bPs[bank]])
                    P.op("vector", lambda v, fc=fc, th=th, bank=bank: v.tensor_add(
                        ea_sb[:, fc, th * 512:(th + 1) * 512],
                        ea_sb[:, fc, th * 512:(th + 1) * 512], ps[bank][:, :]),
                        reads=[bPs[bank], bEa], writes=[bEa])
            for dc in range(NDC):
                P.op("vector", lambda v, dc=dc, l=l: v.tensor_scalar(
                    ea_sb[:, dc, :], ea_sb[:, dc, :],
                    fbias[:, FB_BMS + l * NDC + dc:FB_BMS + l * NDC + dc + 1], None, ALU.add),
                    reads=[bEa, bFB], writes=[bEa])
            last = (l == L - 1)
            ln_fm(P, [EA(dc) for dc in range(NDC)], bEa, sqt_sb, bSqt, bc2_sb, bBc2, 6, 7,
                  g_col=lambda dc, l=l: fbias[:, FB_LN + (l * 4 + 2) * NDC + dc:FB_LN + (l * 4 + 2) * NDC + dc + 1],
                  b_col=lambda dc, l=l: fbias[:, FB_LN + (l * 4 + 3) * NDC + dc:FB_LN + (l * 4 + 3) * NDC + dc + 1],
                  dstbf=None if last else [hbf2_sb[:, dc, :] for dc in range(NDC)],
                  bDstBf=bHbf2)
            if not last:
                nxt = 1 - cur_h
                for dc in range(NDC):
                    P.dma(s_hsp[dc % 2], lambda s, dc=dc, nxt=nxt: s.dma_start(
                        out=h_dram[nxt][dc * 128:(dc + 1) * 128, :], in_=ea_sb[:, dc, :]),
                        reads=[bEa], writes=[bHD[nxt]])
                cur_h = nxt
            if DEBUG_DUMP and l == 0:
                for dc in range(NDC):
                    P.dma(s_dbg, lambda s, dc=dc: s.dma_start(
                        out=dbg["hl0"][dc * 128:(dc + 1) * 128, :], in_=ea_sb[:, dc, :]), reads=[bEa])

        layer_scope.close()
        P.barrier_all()

        fin_scope = contextlib.ExitStack()
        x2t2_sb = fin_scope.enter_context(nc.sbuf_tensor("x2t2", [128, 2, D], F32))
        x2ff_sb = fin_scope.enter_context(nc.sbuf_tensor("x2ff", [128, NDC, TC], F32))
        bX2t2 = [Buf("x2t2a"), Buf("x2t2b")]
        bX2ff = Buf("x2ff")
        for ttc in range(TC // 128):
            pt = ttc % 2
            P.dma(s_x2r[pt], lambda sy, ttc=ttc, pt=pt: sy.dma_start(
                out=x2t2_sb[:, pt, :], in_=x2_dram[ttc * 128:(ttc + 1) * 128, :]),
                reads=[bX2D], writes=[bX2t2[pt]])
            for dc in range(NDC):
                bank = 4 + pt
                P.op("tensor", lambda t, pt=pt, dc=dc, bank=bank: t.transpose(
                    ps[bank][:, 0:128], x2t2_sb[:, pt, dc * 128:(dc + 1) * 128], ident_sb[:, :]),
                    reads=[bX2t2[pt], bIdent], writes=[bPs[bank]])
                P.op("vector", lambda v, pt=pt, dc=dc, bank=bank, ttc=ttc: v.tensor_copy(
                    x2ff_sb[:, dc, ttc * 128:(ttc + 1) * 128], ps[bank][:, 0:128]),
                    reads=[bPs[bank]], writes=[bX2ff])

        for th in range(2):
            pairs = [(onesfc_sb[:, 0:1], ea_sb[:, dc, th * 512:(th + 1) * 512]) for dc in range(NDC)]
            P.op("tensor", MMG(ps[6][0:1, :], pairs), reads=[bEa, bOnesFC], writes=[bPs[6]])
            P.op("vector", lambda v, th=th: v.tensor_scalar(
                rows[0:1, th * 512:(th + 1) * 512], ps[6][0:1, :], 1.0 / D, None, ALU.mult),
                reads=[bPs[6]], writes=[bRows])
            for dc in range(NDC):
                sl = dc % 4
                P.op("scalar", lambda s, dc=dc, sl=sl, th=th: s.activation(
                    sqt_sb[:, sl, :], ea_sb[:, dc, th * 512:(th + 1) * 512], AF.Square),
                    reads=[bEa], writes=[bSqt[sl]])
                P.op("tensor", lambda t, sl=sl, dc=dc: t.matmul(
                    ps[7][0:1, :], onesfc_sb[:, 0:1], sqt_sb[:, sl, :],
                    start=(dc == 0), stop=(dc == NDC - 1)),
                    reads=[bSqt[sl], bOnesFC], writes=[bPs[7]])
            P.op("vector", lambda v, th=th: v.tensor_scalar(
                rows[0:1, 1024 + th * 512:1024 + (th + 1) * 512], ps[7][0:1, :], 1.0 / D, None, ALU.mult),
                reads=[bPs[7]], writes=[bRows])
        for th in range(2):
            sc = rows[0:1, 2048:2560]
            mm_ = rows[0:1, th * 512:(th + 1) * 512]
            vv = rows[0:1, 1024 + th * 512:1024 + (th + 1) * 512]
            P.op("vector", lambda v, sc=sc, mm_=mm_: v.tensor_mul(sc, mm_, mm_), reads=[bRows], writes=[bRows])
            P.op("vector", lambda v, vv=vv, sc=sc: v.tensor_sub(vv, vv, sc), reads=[bRows], writes=[bRows])
            P.op("scalar", lambda s, sc=sc, vv=vv: s.activation(sc, vv, AF.Sqrt, bias=EPS_AP[0:1, :]), reads=[bRows], writes=[bRows])
            P.op("vector", lambda v, vv=vv, sc=sc: v.reciprocal(vv, sc), reads=[bRows], writes=[bRows])
        with nc.sbuf_tensor("mrb", [128, 2, TC], F32) as mrb_sb:
            bMrb = Buf("mrb")
            for th in range(2):
                mm_ = rows[0:1, th * 512:(th + 1) * 512]
                rr = rows[0:1, 1024 + th * 512:1024 + (th + 1) * 512]
                P.op("tensor", MMG(ps[0][:, :], [(onesf_sb[0:1, :], mm_)]), reads=[bRows, bOnesF], writes=[bPs[0]])
                P.op("vector", lambda v, th=th: v.tensor_copy(mrb_sb[:, 0, th * 512:(th + 1) * 512], ps[0][:, :]),
                     reads=[bPs[0]], writes=[bMrb])
                P.op("tensor", MMG(ps[1][:, :], [(onesf_sb[0:1, :], rr)]), reads=[bRows, bOnesF], writes=[bPs[1]])
                P.op("vector", lambda v, th=th: v.tensor_copy(mrb_sb[:, 1, th * 512:(th + 1) * 512], ps[1][:, :]),
                     reads=[bPs[1]], writes=[bMrb])
            for dc in range(NDC):
                P.op("vector", lambda v, dc=dc: v.tensor_sub(
                    ea_sb[:, dc, :], ea_sb[:, dc, :], mrb_sb[:, 0, :]),
                    reads=[bEa, bMrb], writes=[bEa])
                P.op("vector", lambda v, dc=dc: v.tensor_mul(
                    ea_sb[:, dc, :], ea_sb[:, dc, :], mrb_sb[:, 1, :]),
                    reads=[bEa, bMrb], writes=[bEa])
                P.op("vector", lambda v, dc=dc: v.tensor_add(
                    ea_sb[:, dc, :], ea_sb[:, dc, :], x2ff_sb[:, dc, :]),
                    reads=[bEa, bX2ff], writes=[bEa])
                P.dma(s_outst, lambda s, dc=dc: s.dma_start(
                    out=out_d[dc * 128:(dc + 1) * 128, :], in_=ea_sb[:, dc, :]),
                    reads=[bEa], writes=[])
        fin_scope.close()

    P.barrier_all()
    P.emit(nc)
    ctx.close()
    return nc


_PROGRAM_CACHE = {}


def _build_cached():
    if "nc" not in _PROGRAM_CACHE:
        _PROGRAM_CACHE["nc"] = build_program()
    return _PROGRAM_CACHE["nc"]


def _chunk_cols(v):
    return np.ascontiguousarray(np.asarray(v, dtype=np.float32).reshape(-1, 128).T)


def make_in_maps(inputs):
    x = np.asarray(inputs["L0_attn_x"], dtype=np.float32)
    w_in = np.asarray(inputs["L0_attn_w_in"], dtype=np.float32)
    b_in = np.asarray(inputs["L0_attn_b_in"], dtype=np.float32)
    w_out = np.ascontiguousarray(np.asarray(inputs["L0_attn_w_out"], dtype=np.float32))
    b_out = np.asarray(inputs["L0_attn_b_out"], dtype=np.float32)
    amp_w = np.ascontiguousarray(np.asarray(inputs["L0_ffn_amp_w"], dtype=np.float32))
    amp_b = np.asarray(inputs["L0_ffn_amp_b"], dtype=np.float32)
    ph_w = np.ascontiguousarray(np.asarray(inputs["L0_ffn_ph_w"], dtype=np.float32))
    ph_b = np.asarray(inputs["L0_ffn_ph_b"], dtype=np.float32)
    ent_w = np.ascontiguousarray(np.asarray(inputs["L0_ffn_ent_w"], dtype=np.float32))
    ent_b = np.asarray(inputs["L0_ffn_ent_b"], dtype=np.float32)
    ent_pw = np.ascontiguousarray(np.asarray(inputs["L0_ffn_ent_pw"], dtype=np.float32))
    ent_pb = np.asarray(inputs["L0_ffn_ent_pb"], dtype=np.float32)
    path_w = np.ascontiguousarray(np.asarray(inputs["L0_ffn_path_w"], dtype=np.float32))
    path_b = np.asarray(inputs["L0_ffn_path_b"], dtype=np.float32)
    phase = np.asarray(inputs["L0_ffn_phase"], dtype=np.float32)
    ln3_g = np.asarray(inputs["L0_ffn_ln3_g"], dtype=np.float32)
    ln3_b = np.asarray(inputs["L0_ffn_ln3_b"], dtype=np.float32)
    ln4_g = np.asarray(inputs["L0_ffn_ln4_g"], dtype=np.float32)
    ln4_b = np.asarray(inputs["L0_ffn_ln4_b"], dtype=np.float32)

    fb = np.zeros((128, FB_COLS), dtype=np.float32)
    fb[:, FB_AMPB:FB_AMPB + 16] = _chunk_cols(amp_b)
    fb[:, FB_PHB:FB_PHB + 16] = _chunk_cols(ph_b)
    for l in range(L):
        fb[:, FB_ENTB + l * NFC:FB_ENTB + (l + 1) * NFC] = _chunk_cols(ent_b[l])
        fb[:, FB_ENTPB + l * NDC:FB_ENTPB + (l + 1) * NDC] = _chunk_cols(ent_pb[l])
        fb[:, FB_LN + (l * 4 + 0) * NDC:FB_LN + (l * 4 + 1) * NDC] = _chunk_cols(ln3_g[l])
        fb[:, FB_LN + (l * 4 + 1) * NDC:FB_LN + (l * 4 + 2) * NDC] = _chunk_cols(ln3_b[l])
        fb[:, FB_LN + (l * 4 + 2) * NDC:FB_LN + (l * 4 + 3) * NDC] = _chunk_cols(ln4_g[l])
        fb[:, FB_LN + (l * 4 + 3) * NDC:FB_LN + (l * 4 + 4) * NDC] = _chunk_cols(ln4_b[l])
        for p in range(PW):
            fb[:, FB_MOD + (l * PW + p) * NDC:FB_MOD + (l * PW + p + 1) * NDC] = _chunk_cols(phase[l, p])
            fb[:, FB_PB + (l * PW + p) * NDC:FB_PB + (l * PW + p + 1) * NDC] = _chunk_cols(path_b[l, p])

    fb[:, FB_EPS] = EPS

    masks = np.zeros((128, 4, 512), dtype=np.float32)
    ki = np.arange(128)[:, None]
    qq = np.arange(512)[None, :]
    for j in range(4):
        masks[:, j, :] = (qq >= 128 * j + ki).astype(np.float32)
    masks = np.ascontiguousarray(masks.reshape(128, 2048)).astype(ml_dtypes.bfloat16)

    ident = np.eye(128, dtype=np.float32)
    ones_bf = np.ones((128, 512), dtype=ml_dtypes.bfloat16)
    onesf = np.ones((1, 128), dtype=np.float32)
    onesfc = np.ones((128, 1), dtype=np.float32)

    in_maps = []
    for c in range(8):
        b = c // W
        j = c % W
        heads = [W * j + i for i in range(HPC)]
        qcols = np.concatenate([w_in[:, h * 128:(h + 1) * 128] for h in heads], axis=1)
        kcols = np.concatenate([w_in[:, D + h * 128:D + (h + 1) * 128] for h in heads], axis=1)
        vcols = np.concatenate([w_in[:, 2 * D + h * 128:2 * D + (h + 1) * 128] for h in heads], axis=1)
        wqkv = np.ascontiguousarray(np.concatenate([qcols, kcols, vcols], axis=1))
        bqk = np.stack([b_in[h * 128:(h + 1) * 128] for h in heads] +
                       [b_in[D + h * 128:D + (h + 1) * 128] for h in heads], axis=1)
        bv = np.concatenate([b_in[2 * D + h * 128:2 * D + (h + 1) * 128] for h in heads])[None, :]
        m = {
            "x_batch": np.ascontiguousarray(x[b]),
            "x_slice": np.ascontiguousarray(x[b, j * TC:(j + 1) * TC]),
            "w_qkv": wqkv,
            "b_qk": np.ascontiguousarray(bqk),
            "b_v_row": np.ascontiguousarray(bv),
            "w_out": w_out,
            "b_out_row": np.ascontiguousarray(b_out[None, :]),
            "amp_w": amp_w,
            "ph_w": ph_w,
            "fb_in": fb,
            "ent_w": ent_w,
            "ent_pw": ent_pw,
            "path_w": path_w,
            "masks_in": masks,
            "phase_r": np.ascontiguousarray(phase.reshape(1, L * PW * D)),
            "ident_in": ident,
            "ones_in": ones_bf,
            "onesf_in": onesf,
            "onesfc_in": onesfc,
            "tok_off": np.array([[(j // 2) * (HPC * 512 * (S // 2))
                                  + ((dc % 4) * 512 + (dc // 4) * 128) * (S // 2)
                                  + (j % 2) * TC
                                  for dc in range(NDC)]], dtype=np.int32),
        }
        in_maps.append(m)
    return in_maps


def kernel(**inputs):
    nc = _build_cached()
    in_maps = make_in_maps(inputs)
    res = run_bass_kernel_spmd(nc, in_maps, core_ids=list(range(8)))
    out = np.empty((B, S, D), dtype=np.float32)
    for c in range(8):
        b = c // W
        j = c % W
        out[b, j * TC:(j + 1) * TC] = np.asarray(res.results[c]["out"]).T
    return out


if __name__ == "__main__":
    build_program()
    print("build ok")


# version 22
# speedup vs baseline: 1.0103x; 1.0103x over previous
import math
import numpy as np
import ml_dtypes

import concourse.bass as bass
import concourse.mybir as mybir
from concourse.bass_utils import run_bass_kernel_spmd

F32 = mybir.dt.float32
BF16 = mybir.dt.bfloat16
I32 = mybir.dt.int32
AF = mybir.ActivationFunctionType
ALU = mybir.AluOpType

DEBUG_DUMP = False

ENGINES = ("tensor", "vector", "scalar", "gpsimd", "sync")


class Buf:
    __slots__ = ("name", "lw", "readers")

    def __init__(self, name):
        self.name = name
        self.lw = {}
        self.readers = {}


class Stream:
    __slots__ = ("key", "cnt", "engine")

    def __init__(self, key, engine):
        self.key = key
        self.cnt = 0
        self.engine = engine


class Plan:
    def __init__(self):
        self.ops = {e: [] for e in ENGINES}
        self.cnt = {e: 0 for e in ENGINES}
        self.watermark = {e: {} for e in ENGINES}
        self.sem_keys = set(ENGINES)
        self.streams = {}

    def stream(self, name, engine):
        key = f"dma_{name}"
        assert key not in self.sem_keys
        self.sem_keys.add(key)
        s = Stream(key, engine)
        self.streams[key] = s
        return s

    def custom_sem(self, name):
        key = f"cs_{name}"
        self.sem_keys.add(key)
        return key

    def _waits_for(self, eng, reads, writes):
        need = {}

        def add(sem_key, val):
            if eng == "tensor" and sem_key == "tensor":
                return
            if self.watermark[eng].get(sem_key, 0) >= val:
                return
            if need.get(sem_key, 0) < val:
                need[sem_key] = val

        for b in reads:
            for k, v in b.lw.items():
                add(k, v)
        for b in writes:
            for k, v in b.lw.items():
                add(k, v)
            for k, v in b.readers.items():
                add(k, v)
        return need

    def _commit(self, eng, need):
        for k, v in need.items():
            self.watermark[eng][k] = v

    def _record(self, sem_key, val, reads, writes):
        for b in reads:
            if b.readers.get(sem_key, 0) < val:
                b.readers[sem_key] = val
        for b in writes:
            if b.lw.get(sem_key, 0) < val:
                b.lw[sem_key] = val

    def op(self, eng, emit, reads=(), writes=()):
        need = self._waits_for(eng, reads, writes)
        self._commit(eng, need)
        self.cnt[eng] += 1
        val = self.cnt[eng]
        self.ops[eng].append((need, emit, eng, 1))
        self._record(eng, val, reads, writes)

    def op_custom(self, eng, emit, sem_key, done_val, reads=(), writes=()):
        need = self._waits_for(eng, reads, writes)
        self._commit(eng, need)
        self.ops[eng].append((need, emit, sem_key, done_val))
        self._record(sem_key, done_val, reads, writes)

    def dma(self, stream, emit, reads=(), writes=()):
        eng = stream.engine
        need = self._waits_for(eng, reads, writes)
        if stream.cnt > 0 and self.watermark[eng].get(stream.key, 0) < stream.cnt:
            if need.get(stream.key, 0) < stream.cnt:
                need[stream.key] = stream.cnt
        self._commit(eng, need)
        stream.cnt += 16
        val = stream.cnt
        self.ops[eng].append((need, emit, stream.key, 16))
        self._record(stream.key, val, reads, writes)

    def raw(self, eng, emit):
        self.ops[eng].append((None, emit, None, 0))

    def stamp(self, bufs, stream):
        for b in bufs:
            b.lw = {stream.key: stream.cnt}
            b.readers = {}

    def barrier_all(self):
        snapshot = dict(self.cnt)
        for s in self.streams.values():
            snapshot[s.key] = s.cnt
        for e in ENGINES:
            need = {}
            for k, v in snapshot.items():
                if v > 0 and k != e and self.watermark[e].get(k, 0) < v:
                    need[k] = v
                    self.watermark[e][k] = v
            if need:
                self.ops[e].append((need, None, None, 0))

    def emit(self, nc):
        import contextlib

        with contextlib.ExitStack() as st:
            sems = {}
            for k in sorted(self.sem_keys):
                sems[k] = st.enter_context(nc.semaphore(f"s_{k}"))
            with nc.Block() as block:
                for eng in ENGINES:
                    ops = self.ops[eng]
                    if not ops:
                        continue

                    def make_body(ops=ops):
                        def body(proxy):
                            for need, emit, sem_key, inc in ops:
                                if need:
                                    for k, v in sorted(need.items()):
                                        proxy.wait_ge(sems[k], v)
                                if emit is not None:
                                    inst = emit(proxy)
                                    if inc:
                                        inst.then_inc(sems[sem_key], inc)
                        return body

                    getattr(block, eng)(make_body())


def MMG(psum_ap, pairs):
    def emit(t):
        last = None
        n = len(pairs)
        for i, (l, r) in enumerate(pairs):
            last = t.matmul(psum_ap, l, r, start=(i == 0), stop=(i == n - 1))
        return last
    return emit


B, S, D = 2, 4096, 2048
NH, DH = 16, 128
HPC = 4
W = 4
TC = 1024
F1 = 8192
L = 2
PW = 2
EPS = 1e-5
SCALE = 1.0 / math.sqrt(DH)
NDC = D // 128
NFC = F1 // 128

FB_AMPB = 0
FB_PHB = 16
FB_ENTB = 32
FB_ENTPB = 160
FB_LN = 192
FB_MOD = 448
FB_PB = 512
FB_BMS = 576
FB_EPS = 608
FB_COLS = 640


def build_program():
    nc = bass.Bass("TRN2", target_bir_lowering=False)
    P = Plan()
    import contextlib
    ctx = contextlib.ExitStack()

    x_batch = nc.dram_tensor("x_batch", [S, D], F32, kind="ExternalInput").ap()
    x_slice = nc.dram_tensor("x_slice", [TC, D], F32, kind="ExternalInput").ap()
    w_qkv = nc.dram_tensor("w_qkv", [D, 3 * HPC * 128], F32, kind="ExternalInput").ap()
    b_qk = nc.dram_tensor("b_qk", [128, 2 * HPC], F32, kind="ExternalInput").ap()
    b_v_row = nc.dram_tensor("b_v_row", [1, HPC * 128], F32, kind="ExternalInput").ap()
    w_out_d = nc.dram_tensor("w_out", [D, D], F32, kind="ExternalInput").ap()
    b_out_row = nc.dram_tensor("b_out_row", [1, D], F32, kind="ExternalInput").ap()
    amp_w_d = nc.dram_tensor("amp_w", [D, D], F32, kind="ExternalInput").ap()
    ph_w_d = nc.dram_tensor("ph_w", [D, D], F32, kind="ExternalInput").ap()
    fb_in = nc.dram_tensor("fb_in", [128, FB_COLS], F32, kind="ExternalInput").ap()
    ent_w_d = nc.dram_tensor("ent_w", [L, D, F1], F32, kind="ExternalInput").ap()
    ent_pw_d = nc.dram_tensor("ent_pw", [L, F1, D], F32, kind="ExternalInput").ap()
    path_w_d = nc.dram_tensor("path_w", [L, PW, D, D], F32, kind="ExternalInput").ap()
    masks_in = nc.dram_tensor("masks_in", [128, 4 * 512], BF16, kind="ExternalInput").ap()
    phase_r = nc.dram_tensor("phase_r", [1, L * PW * D], F32, kind="ExternalInput").ap()
    ident_in = nc.dram_tensor("ident_in", [128, 128], F32, kind="ExternalInput").ap()
    ones_in = nc.dram_tensor("ones_in", [128, 512], BF16, kind="ExternalInput").ap()
    onesf_in = nc.dram_tensor("onesf_in", [1, 128], F32, kind="ExternalInput").ap()
    onesfc_in = nc.dram_tensor("onesfc_in", [128, 1], F32, kind="ExternalInput").ap()
    tok_off = nc.dram_tensor("tok_off", [1, NDC], I32, kind="ExternalInput").ap()

    out_d = nc.dram_tensor("out", [TC, D], F32, kind="ExternalOutput").ap()

    xln_dram = nc.dram_tensor("xln_dram", [S, D], BF16).ap()
    ag_in = nc.dram_tensor("ag_in", [2, HPC * 128, S // 2], BF16).ap()
    ag_out = nc.dram_tensor("ag_out", [2 * HPC * 512, S // 2], BF16).ap()
    x2_dram = nc.dram_tensor("x2_dram", [TC, D], F32).ap()
    x2ln_dram = nc.dram_tensor("x2ln_dram", [TC, D], BF16).ap()
    h_dram = [nc.dram_tensor(f"h_dram{i}", [D, TC], F32).ap() for i in range(2)]
    wo_bf = nc.dram_tensor("wo_bf", [NDC, 128, D], BF16).ap()
    modr_dram_t = nc.dram_tensor("modr_dram", [L * PW, D], F32)
    modr_dram = modr_dram_t.ap()
    mr_dram_t = nc.dram_tensor("mr_dram", [1, 2048], F32)
    mr_dram = mr_dram_t.ap()

    dbg = {}
    if DEBUG_DUMP:
        dbg["q"] = nc.dram_tensor("dbg_q", [HPC * 128, S], BF16, kind="ExternalOutput").ap()
        dbg["k"] = nc.dram_tensor("dbg_k", [HPC * 128, S], BF16, kind="ExternalOutput").ap()
        dbg["v"] = nc.dram_tensor("dbg_v", [S, HPC * 128], BF16, kind="ExternalOutput").ap()
        dbg["att"] = nc.dram_tensor("dbg_att", [HPC * 128, S], BF16, kind="ExternalOutput").ap()
        dbg["x2"] = nc.dram_tensor("dbg_x2", [TC, D], F32, kind="ExternalOutput").ap()
        dbg["h1"] = nc.dram_tensor("dbg_h1", [D, TC], F32, kind="ExternalOutput").ap()
        dbg["hl0"] = nc.dram_tensor("dbg_hl0", [D, TC], F32, kind="ExternalOutput").ap()
        dbg["modb"] = nc.dram_tensor("dbg_modb", [L * PW, 128, D], BF16, kind="ExternalOutput").ap()
        dbg["mrow"] = nc.dram_tensor("dbg_mrow", [1, 3 * L * PW * D], F32, kind="ExternalOutput").ap()

    bXlnD = Buf("xlnD")
    bAgIn = [[Buf(f"agIn{h}_{hf}") for hf in range(2)] for h in range(HPC)]
    bAgOut = [[Buf(f"agOut{h}_{hf}") for hf in range(2)] for h in range(HPC)]
    bX2D = Buf("x2D"); bX2lnD = Buf("x2lnD"); bMrD = Buf("mrD")
    bHD = [Buf("hD0"), Buf("hD1")]

    s_const = P.stream("const", "gpsimd")
    s_wq = P.stream("wq", "gpsimd")
    s_wo = P.stream("wo", "gpsimd")
    s_wo2 = P.stream("wo2", "scalar")
    s_x = [P.stream(f"x{i}", "sync") for i in range(2)]
    s_xlnst = P.stream("xlnst", "scalar")
    s_xfm = [P.stream(f"xfm{i}", "sync") for i in range(4)]
    s_agst = P.stream("agst", "scalar")
    s_agld = [P.stream(f"agld{i}", "sync") for i in range(2)]
    s_xs = [P.stream(f"xs{i}", "sync") for i in range(2)]
    s_x2st = P.stream("x2st", "scalar")
    s_x2lnst = P.stream("x2lnst", "scalar")
    s_x2fm = P.stream("x2fm", "sync")
    s_wa = [P.stream(f"wa{i}", "gpsimd") for i in range(4)]
    s_we = [P.stream(f"we{i}", "gpsimd") for i in range(2)]
    s_hb = [P.stream(f"hb{i}", "gpsimd") for i in range(2)]
    s_wp = [P.stream(f"wp{i}", "gpsimd") for i in range(2)]
    s_hsp = [P.stream(f"hsp{i}", "scalar") for i in range(2)]
    s_hld = [P.stream(f"hld{i}", "sync") for i in range(2)]
    s_mrst = P.stream("mrst", "scalar")
    s_mrld = P.stream("mrld", "sync")
    s_x2r = [P.stream(f"x2r{i}", "sync") for i in range(2)]
    s_toff = P.stream("toff", "sync")
    s_outst = P.stream("outst", "scalar")
    s_dbg = P.stream("dbg", "scalar") if DEBUG_DUMP else None
    cc_keys = [[P.custom_sem(f"cc{h}_{hf}") for hf in range(2)] for h in range(HPC)]
    bWoBf = Buf("woBf")
    bModbD = Buf("modbD")
    s_modb = P.stream("modb", "scalar")
    s_modl = P.stream("modl", "sync")

    ps = [ctx.enter_context(nc.psum_tensor(f"ps{i}", [128, 512], F32)) for i in range(8)]
    bPs = [Buf(f"ps{i}") for i in range(8)]

    ones_sb = ctx.enter_context(nc.sbuf_tensor("ones", [128, 512], BF16))
    ident_sb = ctx.enter_context(nc.sbuf_tensor("ident", [128, 128], F32))
    onesf_sb = ctx.enter_context(nc.sbuf_tensor("onesf", [1, 128], F32))
    onesfc_sb = ctx.enter_context(nc.sbuf_tensor("onesfc", [128, 1], F32))
    bias_qk = ctx.enter_context(nc.sbuf_tensor("bias_qk", [128, 2 * HPC], F32))
    fbias = ctx.enter_context(nc.sbuf_tensor("fbias", [128, FB_COLS], F32))
    stat6 = ctx.enter_context(nc.sbuf_tensor("stat6", [128, 2, 24], F32))
    stat2 = ctx.enter_context(nc.sbuf_tensor("stat2", [128, 2, 4], F32))

    bOnes = Buf("ones"); bIdent = Buf("ident"); bOnesF = Buf("onesf")
    bOnesFC = Buf("onesfc"); bBqk = Buf("bqk"); bFB = Buf("fbias")
    bRows = Buf("rows"); bStat6 = [Buf("stat6a"), Buf("stat6b")]; bStat2 = [Buf("stat2a"), Buf("stat2b")]

    fbc = lambda c: fbias[:, c:c + 1]
    EPS_AP = fbias[:, FB_EPS:FB_EPS + 1]
    ONESB = ones_sb

    def cl(dst, src):
        P.dma(s_const, lambda g, dst=dst, src=src: g.dma_start(out=dst, in_=src))

    cl(ident_sb[:, :], ident_in[:, :])
    cl(ones_sb[:, :], ones_in[:, :])
    cl(onesf_sb[:, :], onesf_in[:, :])
    cl(onesfc_sb[:, :], onesfc_in[:, :])
    cl(bias_qk[:, :], b_qk[:, :])
    cl(fbias[:, :], fb_in[:, :])
    P.stamp([bOnes, bIdent, bOnesF, bOnesFC, bBqk, bFB], s_const)

    P.barrier_all()

    with (
        nc.sbuf_tensor("q_sb", [128, HPC, S], BF16) as q_sb,
        nc.sbuf_tensor("k_sb", [128, HPC, S], BF16) as k_sb,
        nc.sbuf_tensor("v_sb", [128, S // 128, HPC * 128], BF16) as v_sb,
    ):
        bQ = Buf("q"); bK = Buf("k"); bV = Buf("v")

        with (
            nc.sbuf_tensor("wqk", [128, NDC, 2 * HPC * 128], BF16) as wqk_sb,
            nc.sbuf_tensor("wv", [128, NDC, HPC * 128], BF16) as wv_sb,
            nc.sbuf_tensor("bv", [1, HPC * 128], BF16) as bv_sb,
            nc.sbuf_tensor("x_t", [128, 2, D], F32) as x_t,
            nc.sbuf_tensor("xln_t", [128, D], BF16) as xln_t,
            nc.sbuf_tensor("xfm", [128, 2, NDC, 512], BF16) as xfm_sb,
        ):
            bWqk = Buf("wqk"); bWv = Buf("wv"); bBv = Buf("bv")
            bXt = [Buf("xt0"), Buf("xt1")]
            bXln = Buf("xln")
            bXfm = [Buf("xfm0"), Buf("xfm1")]

            P.dma(s_wq, lambda g: g.dma_start(
                out=wqk_sb[:, :, :],
                in_=w_qkv.rearrange("(c p) f -> p c f", p=128)[:, :, 0:2 * HPC * 128]),
                writes=[bWqk])
            P.dma(s_wq, lambda g: g.dma_start(
                out=wv_sb[:, :, :],
                in_=w_qkv.rearrange("(c p) f -> p c f", p=128)[:, :, 2 * HPC * 128:3 * HPC * 128]),
                writes=[bWv])
            P.dma(s_const, lambda g: g.dma_start(out=bv_sb[:, :], in_=b_v_row[:, :]), writes=[bBv])

            def qkv_prep(blk):
                pb = blk % 2
                for tt in range(4):
                    g_tt = blk * 4 + tt
                    pt = g_tt % 2
                    xt = x_t[:, pt, :]
                    P.dma(s_x[pt], lambda sy, g_tt=g_tt, xt=xt: sy.dma_start(
                        out=xt, in_=x_batch[g_tt * 128:(g_tt + 1) * 128, :]), writes=[bXt[pt]])
                    for qd in range(4):
                        P.op("vector", lambda v, xt=xt, qd=qd, pt=pt: v.bn_stats(
                            stat6[:, pt, qd * 6:qd * 6 + 6], xt[:, qd * 512:(qd + 1) * 512]),
                            reads=[bXt[pt]], writes=[bStat6[pt]])
                    P.op("vector", lambda v, pt=pt: v.bn_aggr(stat2[:, pt, 0:2], stat6[:, pt, 0:24]), reads=[bStat6[pt]], writes=[bStat2[pt]])
                    P.op("scalar", lambda s, pt=pt: s.activation(stat2[:, pt, 2:3], stat2[:, pt, 1:2], AF.Sqrt, bias=EPS_AP), reads=[bStat2[pt]], writes=[bStat2[pt]])
                    P.op("vector", lambda v, pt=pt: v.reciprocal(stat2[:, pt, 3:4], stat2[:, pt, 2:3]), reads=[bStat2[pt]], writes=[bStat2[pt]])
                    P.op("vector", lambda v, xt=xt, pt=pt: v.tensor_scalar(
                        xln_t[:, :], xt, stat2[:, pt, 0:1], stat2[:, pt, 3:4], ALU.subtract, ALU.mult),
                        reads=[bXt[pt], bStat2[pt]], writes=[bXln])
                    P.dma(s_xlnst, lambda s, g_tt=g_tt: s.dma_start(
                        out=xln_dram[g_tt * 128:(g_tt + 1) * 128, :], in_=xln_t[:, :]),
                        reads=[bXln], writes=[bXlnD])
                for dc in range(NDC):
                    P.dma(s_xfm[pb * 2 + dc % 2], lambda sy, blk=blk, dc=dc, pb=pb: sy.dma_start(
                        out=xfm_sb[:, pb, dc, :],
                        in_=xln_dram[blk * 512:(blk + 1) * 512, dc * 128:(dc + 1) * 128],
                        transpose=True), reads=[bXlnD], writes=[bXfm[pb]])

            def qkv_mms(blk):
                pb = blk % 2
                for fh in range(2 * HPC):
                    bank = fh % 4
                    fcol = fh * 128
                    pairs = [(wqk_sb[:, dc, fcol:fcol + 128], xfm_sb[:, pb, dc, :]) for dc in range(NDC)]
                    P.op("tensor", MMG(ps[bank][:, :], pairs),
                         reads=[bWqk, bXfm[pb]], writes=[bPs[bank]])
                    h = fh % HPC
                    dst = (q_sb if fh < HPC else k_sb)[:, h, blk * 512:(blk + 1) * 512]
                    dstbuf = bQ if fh < HPC else bK
                    P.op("vector", lambda v, dst=dst, bank=bank, fh=fh: v.tensor_scalar(
                        dst, ps[bank][:, :], bias_qk[:, fh:fh + 1], None, ALU.add),
                        reads=[bPs[bank], bBqk], writes=[dstbuf])
                for tt in range(4):
                    g_tt = blk * 4 + tt
                    bank = 4 + tt % 2
                    pairs = [(xfm_sb[:, pb, dc, tt * 128:(tt + 1) * 128], wv_sb[:, dc, :]) for dc in range(NDC)]
                    pairs.append((ONESB[0:1, 0:128], bv_sb[0:1, :]))
                    P.op("tensor", MMG(ps[bank][:, :], pairs),
                         reads=[bWv, bXfm[pb], bOnes, bBv], writes=[bPs[bank]])
                    P.op("vector", lambda v, g_tt=g_tt, bank=bank: v.tensor_copy(
                        v_sb[:, g_tt, :], ps[bank][:, :]), reads=[bPs[bank]], writes=[bV])

            qkv_prep(0)
            for blk in range(1, 8):
                qkv_prep(blk)
                qkv_mms(blk - 1)
            qkv_mms(7)
        P.barrier_all()

        if DEBUG_DUMP:
            for h in range(HPC):
                P.dma(s_dbg, lambda s, h=h: s.dma_start(out=dbg["q"][h * 128:(h + 1) * 128, :], in_=q_sb[:, h, :]), reads=[bQ])
                P.dma(s_dbg, lambda s, h=h: s.dma_start(out=dbg["k"][h * 128:(h + 1) * 128, :], in_=k_sb[:, h, :]), reads=[bK])
            for g_tt in range(32):
                P.dma(s_dbg, lambda s, g_tt=g_tt: s.dma_start(out=dbg["v"][g_tt * 128:(g_tt + 1) * 128, :], in_=v_sb[:, g_tt, :]), reads=[bV])

        att_scope = contextlib.ExitStack()
        masks_sb = att_scope.enter_context(nc.sbuf_tensor("masks", [128, 4 * 512], BF16))
        E_sb = att_scope.enter_context(nc.sbuf_tensor("E_sb", [128, 4, 512], BF16))
        rows = att_scope.enter_context(nc.sbuf_tensor("rows_a", [1, 1024], F32))
        rcp_sb = att_scope.enter_context(nc.sbuf_tensor("rcp", [128, 2, 512], F32))
        oev_sb = att_scope.enter_context(nc.sbuf_tensor("oev", [128, 2, 512], BF16))
        bMasks = Buf("masks")
        bE = [Buf("E0"), Buf("E1"), Buf("E2"), Buf("E3")]
        bRcp = [Buf("rcp0"), Buf("rcp1")]
        bOev = [Buf("oev0"), Buf("oev1")]
        MASK = lambda j: masks_sb[:, j * 512:(j + 1) * 512]
        P.dma(s_const, lambda g: g.dma_start(out=masks_sb[:, :], in_=masks_in[:, :]), writes=[bMasks])
        def emit_score(side, h, qc, kt):
            bank_sc = 2 * side + kt % 2
            P.op("tensor", MMG(ps[bank_sc][:, :],
                 [(k_sb[:, h, kt * 128:(kt + 1) * 128], q_sb[:, h, qc * 512:(qc + 1) * 512])]),
                 reads=[bK, bQ], writes=[bPs[bank_sc]])

        def make_tail(side, h, qc):
            bank_o = 4 + side
            bank_s = 6 + side

            def tail():
                rrow = rows[0:1, 512 * side:512 * side + 512]
                P.op("vector", lambda v, bank_s=bank_s, rrow=rrow: v.reciprocal(rrow, ps[bank_s][0:1, :]),
                     reads=[bPs[bank_s]], writes=[bRows])
                P.op("tensor", MMG(ps[bank_s][:, :], [(onesf_sb[0:1, :], rrow)]),
                     reads=[bRows, bOnesF], writes=[bPs[bank_s]])
                P.op("vector", lambda v, side=side, bank_s=bank_s: v.tensor_copy(
                    rcp_sb[:, side, :], ps[bank_s][:, :]), reads=[bPs[bank_s]], writes=[bRcp[side]])
                P.op("vector", lambda v, side=side, bank_o=bank_o: v.tensor_mul(
                    oev_sb[:, side, :], ps[bank_o][:, :], rcp_sb[:, side, :]),
                    reads=[bPs[bank_o], bRcp[side]], writes=[bOev[side]])
                P.dma(s_agst, lambda s, h=h, qc=qc, side=side: s.dma_start(
                    out=ag_in[qc // 4, h * 128:(h + 1) * 128, (qc % 4) * 512:(qc % 4 + 1) * 512],
                    in_=oev_sb[:, side, :]), reads=[bOev[side]], writes=[bAgIn[h][qc // 4]])
            return tail

        def emit_consume(side, h, qc, kt, nkt):
            pe = 2 * side + kt % 2
            bank_sc = 2 * side + kt % 2
            bank_o = 4 + side
            bank_s = 6 + side
            P.op("scalar", lambda s, pe=pe, bank_sc=bank_sc: s.activation(
                E_sb[:, pe, :], ps[bank_sc][:, :], AF.Exp, scale=SCALE),
                reads=[bPs[bank_sc]], writes=[bE[pe]])
            j = kt - 4 * qc
            if j >= 0:
                P.op("vector", lambda v, pe=pe, j=j: v.tensor_mul(
                    E_sb[:, pe, :], E_sb[:, pe, :], MASK(j)),
                    reads=[bE[pe], bMasks], writes=[bE[pe]])
            if kt + 2 < nkt:
                emit_score(side, h, qc, kt + 2)
            P.op("tensor", lambda t, pe=pe, bank_s=bank_s, kt=kt, nkt=nkt: t.matmul(
                ps[bank_s][0:1, :], ONESB[:, 0:1], E_sb[:, pe, :],
                start=(kt == 0), stop=(kt == nkt - 1)),
                reads=[bE[pe], bOnes], writes=[bPs[bank_s]])
            P.op("tensor", lambda t, pe=pe, bank_o=bank_o, kt=kt, nkt=nkt, h=h: t.matmul(
                ps[bank_o][:, :], v_sb[:, kt, h * 128:(h + 1) * 128], E_sb[:, pe, :],
                start=(kt == 0), stop=(kt == nkt - 1)),
                reads=[bE[pe], bV], writes=[bPs[bank_o]])

        def emit_cc(h, hf):
            r0 = hf * HPC * 512 + h * 512
            P.op_custom("gpsimd", lambda g, h=h, hf=hf, r0=r0: g.collective_compute(
                "AllGather", ALU.bypass,
                replica_groups=[[0, 1, 2, 3], [4, 5, 6, 7]],
                ins=[ag_in[hf, h * 128:(h + 1) * 128, :].opt()],
                outs=[ag_out[r0:r0 + 512, :].opt()]),
                cc_keys[h][hf], 1, reads=[bAgIn[h][hf]], writes=[bAgOut[h][hf]])

        mod_scope = contextlib.ExitStack()
        mrow_sb = mod_scope.enter_context(nc.sbuf_tensor("mrow", [1, 3 * D], F32))
        bMrow = Buf("mrow")

        def emit_modrow(lp):
            ra = mrow_sb[0:1, 0:D]
            rb = mrow_sb[0:1, D:2 * D]
            rc = mrow_sb[0:1, 2 * D:3 * D]
            P.dma(s_modl, lambda sy, lp=lp, ra=ra: sy.dma_start(
                out=ra, in_=phase_r[0:1, lp * D:(lp + 1) * D]), writes=[bMrow])
            P.op("scalar", lambda s, ra=ra, rb=rb: s.activation(rb, ra, AF.Sin, scale=0.125), reads=[bMrow], writes=[bMrow])
            P.op("scalar", lambda s, ra=ra, rc=rc: s.activation(rc, ra, AF.Sin, scale=0.25), reads=[bMrow], writes=[bMrow])
            P.op("vector", lambda v, ra=ra, rb=rb: v.tensor_mul(ra, rb, rb), reads=[bMrow], writes=[bMrow])
            P.op("vector", lambda v, ra=ra: v.tensor_scalar(ra, ra, -2.0, 1.0, ALU.mult, ALU.add), reads=[bMrow], writes=[bMrow])
            P.op("vector", lambda v, ra=ra, rb=rb, rc=rc: v.scalar_tensor_tensor(rb, rc, 2.0, ra, ALU.mult, ALU.mult), reads=[bMrow], writes=[bMrow])
            P.op("vector", lambda v, ra=ra, rc=rc: v.tensor_mul(ra, rc, rc), reads=[bMrow], writes=[bMrow])
            P.op("vector", lambda v, ra=ra: v.tensor_scalar(ra, ra, -2.0, 1.0, ALU.mult, ALU.add), reads=[bMrow], writes=[bMrow])
            P.op("vector", lambda v, ra=ra, rb=rb, rc=rc: v.scalar_tensor_tensor(rc, rb, 2.0, ra, ALU.mult, ALU.mult), reads=[bMrow], writes=[bMrow])
            P.op("vector", lambda v, ra=ra, rb=rb: v.tensor_mul(ra, rb, rb), reads=[bMrow], writes=[bMrow])
            P.op("vector", lambda v, ra=ra: v.tensor_scalar(ra, ra, -2.0, 1.0, ALU.mult, ALU.add), reads=[bMrow], writes=[bMrow])
            P.op("vector", lambda v, ra=ra, rc=rc: v.tensor_add(ra, ra, rc), reads=[bMrow], writes=[bMrow])
            P.dma(s_modb, lambda s, lp=lp, ra=ra: s.dma_start(
                out=modr_dram[lp:lp + 1, :], in_=ra), reads=[bMrow], writes=[bModbD])

        pending_tails = []
        for hp in (0, 2):
            for qc in range(8):
                hA, hB = hp, hp + 1
                nkt = 4 * qc + 4
                emit_score(0, hA, qc, 0)
                emit_score(1, hB, qc, 0)
                if nkt > 1:
                    emit_score(0, hA, qc, 1)
                    emit_score(1, hB, qc, 1)
                for t in pending_tails:
                    t()
                pending_tails = []
                for kt in range(nkt):
                    emit_consume(0, hA, qc, kt, nkt)
                    emit_consume(1, hB, qc, kt, nkt)
                pending_tails = [make_tail(0, hA, qc), make_tail(1, hB, qc)]
                if qc == 4:
                    emit_cc(hA, 0)
                    emit_cc(hB, 0)
                if qc in (2, 6):
                    emit_modrow((hp // 2) * 2 + (qc - 2) // 4)
            for t in pending_tails:
                t()
            pending_tails = []
            emit_cc(hp, 1)
            emit_cc(hp + 1, 1)
        mod_scope.close()

        if DEBUG_DUMP:
            for h in range(HPC):
                P.dma(s_dbg, lambda s, h=h: s.dma_start(
                    out=dbg["att"][h * 128:(h + 1) * 128, :], in_=ag_in[h * 128:(h + 1) * 128, :]), reads=[bAgIn[h]])

        att_scope.close()

    P.barrier_all()

    with (
        nc.sbuf_tensor("acat", [128, NDC, TC // 128, 128], BF16) as acat_sb,
        nc.sbuf_tensor("wo", [128, NDC, D], BF16) as wo_sb,
        nc.sbuf_tensor("bo", [1, D], BF16) as bo_sb,
        nc.sbuf_tensor("toff", [1, NDC], I32) as toff_t,
        nc.sbuf_tensor("xs_t", [128, 2, D], F32) as xs_t,
        nc.sbuf_tensor("ao_t", [128, 2, D], F32) as ao_t,
        nc.sbuf_tensor("x2_t", [128, 2, D], F32) as x2_t,
        nc.sbuf_tensor("x2ln_t", [128, 2, D], BF16) as x2ln_t,
    ):
        bAcat = Buf("acat"); bWo = Buf("wo"); bBo = Buf("bo"); bToff = Buf("toff")
        bXs = [Buf("xs0"), Buf("xs1")]
        bAo = [Buf("ao0"), Buf("ao1")]
        bX2 = [Buf("x20"), Buf("x21")]
        bX2ln = [Buf("x2ln0"), Buf("x2ln1")]

        P.dma(s_const, lambda g: g.dma_start(out=bo_sb[:, :], in_=b_out_row[:, :]), writes=[bBo])
        P.dma(s_toff, lambda sy: sy.dma_start(out=toff_t[:, :], in_=tok_off[:, :]), writes=[bToff])

        reg_cell = [None]

        def ag_slice_emit(dc):
            def emit(sy):
                if reg_cell[0] is None:
                    reg_cell[0] = sy.alloc_register("tokoff")
                sy.reg_load(reg_cell[0], toff_t[:1, dc:dc + 1])
                off = sy.snap(reg_cell[0], min_val=0, max_val=2 * HPC * 512 * (S // 2))
                agf = ag_out.rearrange("a b -> (a b)")
                return sy.dma_start(
                    out=acat_sb[:, dc, :, :].rearrange("p t f -> p (t f)"),
                    in_=bass.AP(agf.tensor, off, [[S // 2, 128], [1, TC]]))
            return emit

        for dc in range(NDC):
            hh = dc % 4
            P.dma(s_agld[dc % 2], ag_slice_emit(dc),
                  reads=[bAgOut[hh][0], bAgOut[hh][1], bToff], writes=[bAcat])
        P.dma(s_wo, lambda g: g.dma_start(
            out=wo_sb[:, :, :], in_=w_out_d.rearrange("(c p) f -> p c f", p=128)),
            writes=[bWo])

        for ttc in range(TC // 128):
            pt = ttc % 2
            P.dma(s_xs[pt], lambda sy, ttc=ttc, pt=pt: sy.dma_start(
                out=xs_t[:, pt, :], in_=x_slice[ttc * 128:(ttc + 1) * 128, :]), writes=[bXs[pt]])
            for f2c in range(4):
                bank = pt * 4 + f2c
                pairs = [(acat_sb[:, dc, ttc, :], wo_sb[:, dc, f2c * 512:(f2c + 1) * 512]) for dc in range(NDC)]
                pairs.append((ONESB[0:1, 0:128], bo_sb[0:1, f2c * 512:(f2c + 1) * 512]))
                P.op("tensor", MMG(ps[bank][:, :], pairs),
                     reads=[bAcat, bWo, bOnes, bBo], writes=[bPs[bank]])
            for f2c in range(4):
                bank = pt * 4 + f2c
                P.op("vector", lambda v, bank=bank, f2c=f2c, pt=pt: v.bn_stats(
                    stat6[:, pt, f2c * 6:f2c * 6 + 6], ps[bank][:, :]),
                    reads=[bPs[bank]], writes=[bStat6[pt]])
            P.op("vector", lambda v, pt=pt: v.bn_aggr(stat2[:, pt, 0:2], stat6[:, pt, 0:24]), reads=[bStat6[pt]], writes=[bStat2[pt]])
            P.op("scalar", lambda s, pt=pt: s.activation(stat2[:, pt, 2:3], stat2[:, pt, 1:2], AF.Sqrt, bias=EPS_AP), reads=[bStat2[pt]], writes=[bStat2[pt]])
            P.op("vector", lambda v, pt=pt: v.reciprocal(stat2[:, pt, 3:4], stat2[:, pt, 2:3]), reads=[bStat2[pt]], writes=[bStat2[pt]])
            for f2c in range(4):
                bank = pt * 4 + f2c
                P.op("vector", lambda v, bank=bank, pt=pt, f2c=f2c: v.tensor_scalar(
                    ao_t[:, pt, f2c * 512:(f2c + 1) * 512], ps[bank][:, :],
                    stat2[:, pt, 0:1], stat2[:, pt, 3:4], ALU.subtract, ALU.mult),
                    reads=[bPs[bank], bStat2[pt]], writes=[bAo[pt]])
            P.op("vector", lambda v, pt=pt: v.tensor_add(
                x2_t[:, pt, :], ao_t[:, pt, :], xs_t[:, pt, :]),
                reads=[bAo[pt], bXs[pt]], writes=[bX2[pt]])
            P.dma(s_x2st, lambda s, ttc=ttc, pt=pt: s.dma_start(
                out=x2_dram[ttc * 128:(ttc + 1) * 128, :], in_=x2_t[:, pt, :]),
                reads=[bX2[pt]], writes=[bX2D])
            for qd in range(4):
                P.op("vector", lambda v, pt=pt, qd=qd: v.bn_stats(
                    stat6[:, pt, qd * 6:qd * 6 + 6], x2_t[:, pt, qd * 512:(qd + 1) * 512]),
                    reads=[bX2[pt]], writes=[bStat6[pt]])
            P.op("vector", lambda v, pt=pt: v.bn_aggr(stat2[:, pt, 0:2], stat6[:, pt, 0:24]), reads=[bStat6[pt]], writes=[bStat2[pt]])
            P.op("scalar", lambda s, pt=pt: s.activation(stat2[:, pt, 2:3], stat2[:, pt, 1:2], AF.Sqrt, bias=EPS_AP), reads=[bStat2[pt]], writes=[bStat2[pt]])
            P.op("vector", lambda v, pt=pt: v.reciprocal(stat2[:, pt, 3:4], stat2[:, pt, 2:3]), reads=[bStat2[pt]], writes=[bStat2[pt]])
            P.op("vector", lambda v, pt=pt: v.tensor_scalar(
                x2ln_t[:, pt, :], x2_t[:, pt, :], stat2[:, pt, 0:1], stat2[:, pt, 3:4],
                ALU.subtract, ALU.mult), reads=[bX2[pt], bStat2[pt]], writes=[bX2ln[pt]])
            P.dma(s_x2lnst, lambda s, ttc=ttc, pt=pt: s.dma_start(
                out=x2ln_dram[ttc * 128:(ttc + 1) * 128, :], in_=x2ln_t[:, pt, :]),
                reads=[bX2ln[pt]], writes=[bX2lnD])

        if DEBUG_DUMP:
            P.dma(s_dbg, lambda s: s.dma_start(out=dbg["x2"][:, :], in_=x2_dram[:, :]), reads=[bX2D])

    P.barrier_all()

    hbf_sb = None

    def ln_fm(P, src, bSrc, sqt_sb, bSqt, bc, bBc, bank0, bank1, g_col=None, b_col=None,
              dstbf=None, bDstBf=None):
        for th in range(2):
            pairs = [(onesfc_sb[:, 0:1], src[dc][:, th * 512:(th + 1) * 512]) for dc in range(NDC)]
            P.op("tensor", MMG(ps[bank0][0:1, :], pairs), reads=[bSrc, bOnesFC], writes=[bPs[bank0]])
            P.op("vector", lambda v, th=th: v.tensor_scalar(
                rows[0:1, th * 512:(th + 1) * 512], ps[bank0][0:1, :], 1.0 / D, None, ALU.mult),
                reads=[bPs[bank0]], writes=[bRows])
            for dc in range(NDC):
                sl = dc % 4
                P.op("scalar", lambda s, dc=dc, sl=sl, th=th: s.activation(
                    sqt_sb[:, sl, :], src[dc][:, th * 512:(th + 1) * 512], AF.Square),
                    reads=[bSrc], writes=[bSqt[sl]])
                P.op("tensor", lambda t, sl=sl, dc=dc, bank1=bank1: t.matmul(
                    ps[bank1][0:1, :], onesfc_sb[:, 0:1], sqt_sb[:, sl, :],
                    start=(dc == 0), stop=(dc == NDC - 1)),
                    reads=[bSqt[sl], bOnesFC], writes=[bPs[bank1]])
            P.op("vector", lambda v, th=th: v.tensor_scalar(
                rows[0:1, 1024 + th * 512:1024 + (th + 1) * 512], ps[bank1][0:1, :], 1.0 / D, None, ALU.mult),
                reads=[bPs[bank1]], writes=[bRows])
        for th in range(2):
            sc = rows[0:1, 2048:2560]
            mm_ = rows[0:1, th * 512:(th + 1) * 512]
            vv = rows[0:1, 1024 + th * 512:1024 + (th + 1) * 512]
            P.op("vector", lambda v, sc=sc, mm_=mm_: v.tensor_mul(sc, mm_, mm_), reads=[bRows], writes=[bRows])
            P.op("vector", lambda v, vv=vv, sc=sc: v.tensor_sub(vv, vv, sc), reads=[bRows], writes=[bRows])
            P.op("scalar", lambda s, sc=sc, vv=vv: s.activation(sc, vv, AF.Sqrt, bias=EPS_AP[0:1, :]), reads=[bRows], writes=[bRows])
            P.op("vector", lambda v, vv=vv, sc=sc: v.reciprocal(vv, sc), reads=[bRows], writes=[bRows])
        for th in range(2):
            mm_ = rows[0:1, th * 512:(th + 1) * 512]
            rr = rows[0:1, 1024 + th * 512:1024 + (th + 1) * 512]
            P.op("tensor", MMG(ps[bank0][:, :], [(onesf_sb[0:1, :], mm_)]), reads=[bRows, bOnesF], writes=[bPs[bank0]])
            P.op("vector", lambda v, th=th: v.tensor_copy(bc[:, 0, th * 512:(th + 1) * 512], ps[bank0][:, :]),
                 reads=[bPs[bank0]], writes=[bBc])
            P.op("tensor", MMG(ps[bank1][:, :], [(onesf_sb[0:1, :], rr)]), reads=[bRows, bOnesF], writes=[bPs[bank1]])
            P.op("vector", lambda v, th=th: v.tensor_copy(bc[:, 1, th * 512:(th + 1) * 512], ps[bank1][:, :]),
                 reads=[bPs[bank1]], writes=[bBc])
        for th in range(2):
            hsl = slice(th * 512, (th + 1) * 512)
            for dc in range(NDC):
                P.op("vector", lambda v, dc=dc, hsl=hsl: v.tensor_sub(
                    src[dc][:, hsl], src[dc][:, hsl], bc[:, 0, hsl]),
                    reads=[bSrc, bBc], writes=[bSrc])
                P.op("vector", lambda v, dc=dc, hsl=hsl: v.tensor_mul(
                    src[dc][:, hsl], src[dc][:, hsl], bc[:, 1, hsl]),
                    reads=[bSrc, bBc], writes=[bSrc])
                if g_col is not None:
                    P.op("vector", lambda v, dc=dc, hsl=hsl: v.tensor_scalar(
                        src[dc][:, hsl], src[dc][:, hsl], g_col(dc), b_col(dc), ALU.mult, ALU.add),
                        reads=[bSrc, bFB], writes=[bSrc])
                if dstbf is not None:
                    P.op("vector", lambda v, dc=dc, hsl=hsl: v.tensor_copy(
                        dstbf[dc][:, hsl], src[dc][:, hsl]),
                        reads=[bSrc], writes=[bDstBf])

    with (
        nc.sbuf_tensor("hbf", [128, NDC, TC], BF16) as hbf_sb,
        nc.sbuf_tensor("x2fm", [128, NDC, TC], BF16) as x2fm_sb,
        nc.sbuf_tensor("bigA", [128, 3, NDC, 512], F32) as bigA_sb,
        nc.sbuf_tensor("wsl1", [128, 4, NDC, 128], BF16) as wsl1_sb,
        nc.sbuf_tensor("ht1", [128, 4, 512], F32) as ht1_sb,
    ):
        bHbf = Buf("hbf"); bX2fm = Buf("x2fm"); bBigA = Buf("bigA")
        bWsl1 = [Buf("wsl1a"), Buf("wsl1b"), Buf("wsl1c"), Buf("wsl1d")]
        bHt1 = [Buf("ht1a"), Buf("ht1b"), Buf("ht1c"), Buf("ht1d")]

        for dc in range(NDC):
            P.dma(s_x2fm, lambda sy, dc=dc: sy.dma_start(
                out=x2fm_sb[:, dc, :],
                in_=x2ln_dram[:, dc * 128:(dc + 1) * 128], transpose=True),
                reads=[bX2lnD], writes=[bX2fm])

        TH2 = 512
        tanh_t = lambda fc: bigA_sb[:, 0, fc, :]
        s2_t = lambda fc: bigA_sb[:, 1, fc, :]
        s4_t = lambda fc: bigA_sb[:, 2, fc, :]
        for tqi in range(2):
            tsl = slice(tqi * TH2, (tqi + 1) * TH2)
            for fc in range(NDC):
                sl = fc % 4
                P.dma(s_wa[sl], lambda g, fc=fc, sl=sl: g.dma_start(
                    out=wsl1_sb[:, sl, :, :],
                    in_=ph_w_d[:, fc * 128:(fc + 1) * 128].rearrange("(c p) f -> p c f", p=128)),
                    writes=[bWsl1[sl]])
                bank = fc % 4
                pairs = [(wsl1_sb[:, sl, dc, :], x2fm_sb[:, dc, tsl]) for dc in range(NDC)]
                P.op("tensor", MMG(ps[bank][:, :], pairs), reads=[bWsl1[sl], bX2fm], writes=[bPs[bank]])
                P.op("scalar", lambda s, fc=fc, bank=bank: s.activation(
                    tanh_t(fc), ps[bank][:, :], AF.Tanh, bias=fbias[:, FB_PHB + fc:FB_PHB + fc + 1]),
                    reads=[bPs[bank], bFB], writes=[bBigA])
            for fc in range(NDC):
                P.op("scalar", lambda s, fc=fc: s.activation(s2_t(fc), tanh_t(fc), AF.Sin, scale=math.pi / 2),
                     reads=[bBigA], writes=[bBigA])
                P.op("scalar", lambda s, fc=fc: s.activation(s4_t(fc), tanh_t(fc), AF.Sin, scale=math.pi / 4),
                     reads=[bBigA], writes=[bBigA])
            amp_ps = []
            for fc in range(NDC):
                sl = fc % 4
                P.dma(s_wa[sl], lambda g, fc=fc, sl=sl: g.dma_start(
                    out=wsl1_sb[:, sl, :, :],
                    in_=amp_w_d[:, fc * 128:(fc + 1) * 128].rearrange("(c p) f -> p c f", p=128)),
                    writes=[bWsl1[sl]])
                bank = fc % 4
                pairs = [(wsl1_sb[:, sl, dc, :], x2fm_sb[:, dc, tsl]) for dc in range(NDC)]
                P.op("tensor", MMG(ps[bank][:, :], pairs), reads=[bWsl1[sl], bX2fm], writes=[bPs[bank]])
                amp_ps.append(bank)
                sig = ht1_sb[:, 2 + fc % 2, :]
                P.op("scalar", lambda s, bank=bank, sig=sig, fc=fc: s.activation(
                    sig, ps[bank][:, :], AF.Sigmoid, bias=fbias[:, FB_AMPB + fc:FB_AMPB + fc + 1]),
                    reads=[bPs[bank], bFB], writes=[bHt1[2 + fc % 2]])
                cs = tanh_t(fc)
                u = ht1_sb[:, fc % 2, :]
                P.op("vector", lambda v, fc=fc, u=u: v.tensor_mul(u, s4_t(fc), s4_t(fc)), reads=[bBigA], writes=[bHt1[fc % 2]])
                P.op("vector", lambda v, u=u: v.tensor_scalar(u, u, -2.0, 1.0, ALU.mult, ALU.add), reads=[bHt1[fc % 2]], writes=[bHt1[fc % 2]])
                P.op("vector", lambda v, fc=fc, u=u: v.scalar_tensor_tensor(u, s2_t(fc), 2.0, u, ALU.mult, ALU.mult),
                     reads=[bBigA, bHt1[fc % 2]], writes=[bHt1[fc % 2]])
                P.op("vector", lambda v, fc=fc, cs=cs: v.tensor_mul(cs, s2_t(fc), s2_t(fc)), reads=[bBigA], writes=[bBigA])
                P.op("vector", lambda v, cs=cs: v.tensor_scalar(cs, cs, -2.0, 1.0, ALU.mult, ALU.add), reads=[bBigA], writes=[bBigA])
                P.op("vector", lambda v, cs=cs, u=u: v.tensor_add(cs, cs, u), reads=[bBigA, bHt1[fc % 2]], writes=[bBigA])
                hf = ht1_sb[:, fc % 2, :]
                P.op("vector", lambda v, fc=fc, sig=sig, hf=hf: v.tensor_mul(hf, tanh_t(fc), sig),
                     reads=[bBigA, bHt1[2 + fc % 2]], writes=[bHt1[fc % 2]])
                P.op("vector", lambda v, fc=fc, hf=hf, tsl=tsl: v.tensor_copy(hbf_sb[:, fc, tsl], hf),
                     reads=[bHt1[fc % 2]], writes=[bHbf])
                P.dma(s_hsp[fc % 2], lambda s, fc=fc, tqi=tqi, hf=hf: s.dma_start(
                    out=h_dram[0][fc * 128:(fc + 1) * 128, tqi * TH2:(tqi + 1) * TH2], in_=hf),
                    reads=[bHt1[fc % 2]], writes=[bHD[0]])

    P.barrier_all()
    if DEBUG_DUMP:
        P.dma(s_dbg, lambda s: s.dma_start(out=dbg["h1"][:, :], in_=h_dram[0][:, :]), reads=[bHD[0]])

    with (
        nc.sbuf_tensor("ea", [128, NDC, TC], F32) as ea_sb,
        nc.sbuf_tensor("sqt", [128, 4, 512], F32) as sqt_sb,
        nc.sbuf_tensor("rows_b", [1, 2560], F32) as rows,
    ):
        bEa = Buf("ea")
        bSqt = [Buf("sqt0"), Buf("sqt1"), Buf("sqt2"), Buf("sqt3")]
        EA = lambda dc: ea_sb[:, dc, :]
        layer_scope = contextlib.ExitStack()
        hbf2_sb = layer_scope.enter_context(nc.sbuf_tensor("hbf2", [128, NDC, TC], BF16))
        eh_sb = layer_scope.enter_context(nc.sbuf_tensor("eh", [128, 16, TC], BF16))
        wsl_sb = layer_scope.enter_context(nc.sbuf_tensor("wsl", [128, 2, NDC, 128], BF16))
        pwsl_sb = layer_scope.enter_context(nc.sbuf_tensor("pwsl", [128, 2, 16, 128], BF16))
        ht2_sb = layer_scope.enter_context(nc.sbuf_tensor("ht2", [128, 2, TC], F32))
        bc2_sb = layer_scope.enter_context(nc.sbuf_tensor("bc2", [128, 2, TC], F32))
        modb_sb = layer_scope.enter_context(nc.sbuf_tensor("modb", [128, 2, D], F32))
        bModb = Buf("modb")
        bHbf2 = Buf("hbf2"); bEh = Buf("eh")
        bWsl = [Buf("wsl0"), Buf("wsl1")]
        bPwsl = [Buf("pwsl0"), Buf("pwsl1")]
        bHt2 = [Buf("ht2a"), Buf("ht2b")]
        bBc2 = Buf("bc2")

        for dc in range(NDC):
            P.dma(s_hb[dc % 2], lambda g, dc=dc: g.dma_start(
                out=hbf2_sb[:, dc, :], in_=h_dram[0][dc * 128:(dc + 1) * 128, :]),
                reads=[bHD[0]], writes=[bHbf2])

        with nc.sbuf_tensor("mcc", [128, 4 * NDC], F32) as mcc_sb:
            bMcc = Buf("mcc")
            for lp in range(L * PW):
                P.dma(s_modl, lambda sy, lp=lp: sy.dma_start(
                    out=mcc_sb[:, lp * NDC:(lp + 1) * NDC],
                    in_=bass.AP(modr_dram_t, lp * D, [[1, 128], [128, NDC]]),
                    allow_slow_non_contiguous=True),
                    reads=[bModbD], writes=[bMcc])
            for l in range(L):
                dstc = fbias[:, FB_BMS + l * NDC:FB_BMS + (l + 1) * NDC]
                p0b = fbias[:, FB_PB + (l * PW) * NDC:FB_PB + (l * PW + 1) * NDC]
                p1b = fbias[:, FB_PB + (l * PW + 1) * NDC:FB_PB + (l * PW + 2) * NDC]
                m0 = mcc_sb[:, (l * PW) * NDC:(l * PW + 1) * NDC]
                m1 = mcc_sb[:, (l * PW + 1) * NDC:(l * PW + 2) * NDC]
                P.op("vector", lambda v, dstc=dstc, p0b=p0b, m0=m0: v.tensor_mul(dstc, p0b, m0),
                     reads=[bFB, bMcc], writes=[bFB])
                P.op("vector", lambda v, p1b=p1b, m1=m1, l=l: v.tensor_mul(
                    sqt_sb[:, 0, 0:NDC], p1b, m1), reads=[bFB, bMcc], writes=[bSqt[0]])
                P.op("vector", lambda v, dstc=dstc: v.tensor_add(dstc, dstc, sqt_sb[:, 0, 0:NDC]),
                     reads=[bFB, bSqt[0]], writes=[bFB])

        cur_h = 0
        for l in range(L):
            for fq in range(4):
                for fc_l in range(16):
                    fc = fq * 16 + fc_l
                    sl = fc % 2
                    P.dma(s_we[sl], lambda g, l=l, fc=fc, sl=sl: g.dma_start(
                        out=wsl_sb[:, sl, :, :],
                        in_=ent_w_d[l, :, fc * 128:(fc + 1) * 128].rearrange("(c p) f -> p c f", p=128)),
                        writes=[bWsl[sl]])
                    for th in range(2):
                        bank = (fc * 2 + th) % 4
                        pairs = [(wsl_sb[:, sl, dc, :], hbf2_sb[:, dc, th * 512:(th + 1) * 512]) for dc in range(NDC)]
                        P.op("tensor", MMG(ps[bank][:, :], pairs), reads=[bWsl[sl], bHbf2], writes=[bPs[bank]])
                        P.op("vector", lambda v, fc_l=fc_l, th=th, bank=bank, l=l, fc=fc: v.tensor_scalar(
                            eh_sb[:, fc_l, th * 512:(th + 1) * 512], ps[bank][:, :],
                            fbias[:, FB_ENTB + l * NFC + fc:FB_ENTB + l * NFC + fc + 1], None, ALU.add),
                            reads=[bPs[bank], bFB], writes=[bEh])
                for d2c in range(NDC):
                    sl = d2c % 2
                    P.dma(s_wp[sl], lambda g, l=l, fq=fq, d2c=d2c, sl=sl: g.dma_start(
                        out=pwsl_sb[:, sl, :, :],
                        in_=ent_pw_d[l, fq * 2048:(fq + 1) * 2048, d2c * 128:(d2c + 1) * 128].rearrange("(c p) f -> p c f", p=128)),
                        writes=[bPwsl[sl]])
                    for th in range(2):
                        bank = 4 + (d2c * 2 + th) % 4
                        pairs = [(pwsl_sb[:, sl, fc_l, :], eh_sb[:, fc_l, th * 512:(th + 1) * 512]) for fc_l in range(16)]
                        P.op("tensor", MMG(ps[bank][:, :], pairs), reads=[bPwsl[sl], bEh], writes=[bPs[bank]])
                        if fq == 0:
                            P.op("vector", lambda v, d2c=d2c, th=th, bank=bank, l=l: v.tensor_scalar(
                                ea_sb[:, d2c, th * 512:(th + 1) * 512], ps[bank][:, :],
                                fbias[:, FB_ENTPB + l * NDC + d2c:FB_ENTPB + l * NDC + d2c + 1], None, ALU.add),
                                reads=[bPs[bank], bFB], writes=[bEa])
                        else:
                            P.op("vector", lambda v, d2c=d2c, th=th, bank=bank: v.tensor_add(
                                ea_sb[:, d2c, th * 512:(th + 1) * 512],
                                ea_sb[:, d2c, th * 512:(th + 1) * 512], ps[bank][:, :]),
                                reads=[bPs[bank], bEa], writes=[bEa])
            for dc in range(NDC):
                pt = dc % 2
                P.dma(s_hld[pt], lambda sy, dc=dc, pt=pt, cur_h=cur_h: sy.dma_start(
                    out=ht2_sb[:, pt, :], in_=h_dram[cur_h][dc * 128:(dc + 1) * 128, :]),
                    reads=[bHD[cur_h]], writes=[bHt2[pt]])
                P.op("vector", lambda v, dc=dc, pt=pt: v.tensor_add(
                    ea_sb[:, dc, :], ea_sb[:, dc, :], ht2_sb[:, pt, :]),
                    reads=[bEa, bHt2[pt]], writes=[bEa])
            ln_fm(P, [EA(dc) for dc in range(NDC)], bEa, sqt_sb, bSqt, bc2_sb, bBc2, 6, 7,
                  g_col=lambda dc, l=l: fbias[:, FB_LN + (l * 4 + 0) * NDC + dc:FB_LN + (l * 4 + 0) * NDC + dc + 1],
                  b_col=lambda dc, l=l: fbias[:, FB_LN + (l * 4 + 1) * NDC + dc:FB_LN + (l * 4 + 1) * NDC + dc + 1],
                  dstbf=[hbf2_sb[:, dc, :] for dc in range(NDC)], bDstBf=bHbf2)
            for p in range(PW):
                P.dma(s_modl, lambda sy, l=l, p=p: sy.dma_start(
                    out=modb_sb[:, p, :],
                    in_=bass.AP(modr_dram_t, (l * PW + p) * D, [[0, 128], [1, D]])),
                    reads=[bModbD], writes=[bModb])
            for fc in range(NDC):
                sl = fc % 2
                P.dma(s_wa[sl], lambda g, l=l, fc=fc, sl=sl: g.dma_start(
                    out=wsl_sb[:, sl, :, :],
                    in_=path_w_d[l, 0, :, fc * 128:(fc + 1) * 128].rearrange("(c p2) f -> p2 c f", p2=128)),
                    writes=[bWsl[sl]])
                P.dma(s_wp[sl], lambda g, l=l, fc=fc, sl=sl: g.dma_start(
                    out=pwsl_sb[:, sl, :, :],
                    in_=path_w_d[l, 1, :, fc * 128:(fc + 1) * 128].rearrange("(c p2) f -> p2 c f", p2=128)),
                    writes=[bPwsl[sl]])
                mb0 = bass.AP(modb_sb, fc * 128, [[2 * D, 128], [0, NDC], [1, 128]])
                mb1 = bass.AP(modb_sb, D + fc * 128, [[2 * D, 128], [0, NDC], [1, 128]])
                P.op("vector", lambda v, sl=sl, mb0=mb0: v.tensor_tensor(
                    wsl_sb[:, sl, :, :], wsl_sb[:, sl, :, :], mb0, ALU.mult),
                    reads=[bWsl[sl], bModb], writes=[bWsl[sl]])
                P.op("vector", lambda v, sl=sl, mb1=mb1: v.tensor_tensor(
                    pwsl_sb[:, sl, :, :], pwsl_sb[:, sl, :, :], mb1, ALU.mult),
                    reads=[bPwsl[sl], bModb], writes=[bPwsl[sl]])
                P.op("vector", lambda v, sl=sl: v.tensor_tensor(
                    wsl_sb[:, sl, :, :].rearrange("p c f -> p (c f)"),
                    wsl_sb[:, sl, :, :].rearrange("p c f -> p (c f)"),
                    pwsl_sb[:, sl, :, :].rearrange("p c f -> p (c f)"), ALU.add),
                    reads=[bWsl[sl], bPwsl[sl]], writes=[bWsl[sl]])
                for th in range(2):
                    bank = (fc * 2 + th) % 4
                    pairs = [(wsl_sb[:, sl, dc, :], hbf2_sb[:, dc, th * 512:(th + 1) * 512]) for dc in range(NDC)]
                    P.op("tensor", MMG(ps[bank][:, :], pairs),
                         reads=[bWsl[sl], bHbf2], writes=[bPs[bank]])
                    P.op("vector", lambda v, fc=fc, th=th, bank=bank: v.tensor_add(
                        ea_sb[:, fc, th * 512:(th + 1) * 512],
                        ea_sb[:, fc, th * 512:(th + 1) * 512], ps[bank][:, :]),
                        reads=[bPs[bank], bEa], writes=[bEa])
            for dc in range(NDC):
                P.op("vector", lambda v, dc=dc, l=l: v.tensor_scalar(
                    ea_sb[:, dc, :], ea_sb[:, dc, :],
                    fbias[:, FB_BMS + l * NDC + dc:FB_BMS + l * NDC + dc + 1], None, ALU.add),
                    reads=[bEa, bFB], writes=[bEa])
            last = (l == L - 1)
            ln_fm(P, [EA(dc) for dc in range(NDC)], bEa, sqt_sb, bSqt, bc2_sb, bBc2, 6, 7,
                  g_col=lambda dc, l=l: fbias[:, FB_LN + (l * 4 + 2) * NDC + dc:FB_LN + (l * 4 + 2) * NDC + dc + 1],
                  b_col=lambda dc, l=l: fbias[:, FB_LN + (l * 4 + 3) * NDC + dc:FB_LN + (l * 4 + 3) * NDC + dc + 1],
                  dstbf=None if last else [hbf2_sb[:, dc, :] for dc in range(NDC)],
                  bDstBf=bHbf2)
            if not last:
                nxt = 1 - cur_h
                for dc in range(NDC):
                    P.dma(s_hsp[dc % 2], lambda s, dc=dc, nxt=nxt: s.dma_start(
                        out=h_dram[nxt][dc * 128:(dc + 1) * 128, :], in_=ea_sb[:, dc, :]),
                        reads=[bEa], writes=[bHD[nxt]])
                cur_h = nxt
            if DEBUG_DUMP and l == 0:
                for dc in range(NDC):
                    P.dma(s_dbg, lambda s, dc=dc: s.dma_start(
                        out=dbg["hl0"][dc * 128:(dc + 1) * 128, :], in_=ea_sb[:, dc, :]), reads=[bEa])

        layer_scope.close()
        P.barrier_all()

        for th in range(2):
            pairs = [(onesfc_sb[:, 0:1], ea_sb[:, dc, th * 512:(th + 1) * 512]) for dc in range(NDC)]
            P.op("tensor", MMG(ps[6][0:1, :], pairs), reads=[bEa, bOnesFC], writes=[bPs[6]])
            P.op("vector", lambda v, th=th: v.tensor_scalar(
                rows[0:1, th * 512:(th + 1) * 512], ps[6][0:1, :], 1.0 / D, None, ALU.mult),
                reads=[bPs[6]], writes=[bRows])
            for dc in range(NDC):
                sl = dc % 4
                P.op("scalar", lambda s, dc=dc, sl=sl, th=th: s.activation(
                    sqt_sb[:, sl, :], ea_sb[:, dc, th * 512:(th + 1) * 512], AF.Square),
                    reads=[bEa], writes=[bSqt[sl]])
                P.op("tensor", lambda t, sl=sl, dc=dc: t.matmul(
                    ps[7][0:1, :], onesfc_sb[:, 0:1], sqt_sb[:, sl, :],
                    start=(dc == 0), stop=(dc == NDC - 1)),
                    reads=[bSqt[sl], bOnesFC], writes=[bPs[7]])
            P.op("vector", lambda v, th=th: v.tensor_scalar(
                rows[0:1, 1024 + th * 512:1024 + (th + 1) * 512], ps[7][0:1, :], 1.0 / D, None, ALU.mult),
                reads=[bPs[7]], writes=[bRows])
        for th in range(2):
            sc = rows[0:1, 2048:2560]
            mm_ = rows[0:1, th * 512:(th + 1) * 512]
            vv = rows[0:1, 1024 + th * 512:1024 + (th + 1) * 512]
            P.op("vector", lambda v, sc=sc, mm_=mm_: v.tensor_mul(sc, mm_, mm_), reads=[bRows], writes=[bRows])
            P.op("vector", lambda v, vv=vv, sc=sc: v.tensor_sub(vv, vv, sc), reads=[bRows], writes=[bRows])
            P.op("scalar", lambda s, sc=sc, vv=vv: s.activation(sc, vv, AF.Sqrt, bias=EPS_AP[0:1, :]), reads=[bRows], writes=[bRows])
            P.op("vector", lambda v, vv=vv, sc=sc: v.reciprocal(vv, sc), reads=[bRows], writes=[bRows])
        with (
            nc.sbuf_tensor("mr_t", [128, 2, 8], F32) as mr_t,
            nc.sbuf_tensor("x2r", [128, 2, D], F32) as x2r_sb,
            nc.sbuf_tensor("outT", [128, 2, D], F32) as outT_sb,
        ):
            bMr = Buf("mr")
            bX2r = [Buf("x2r0"), Buf("x2r1")]
            bOutT = [Buf("outT0"), Buf("outT1")]
            P.dma(s_mrst, lambda s: s.dma_start(out=mr_dram[0:1, 0:1024], in_=rows[0:1, 0:1024]),
                  reads=[bRows], writes=[bMrD])
            P.dma(s_mrst, lambda s: s.dma_start(out=mr_dram[0:1, 1024:2048], in_=rows[0:1, 1024:2048]),
                  reads=[bRows], writes=[bMrD])
            mr_src = bass.AP(mr_dram_t, 0, [[1, 128], [1024, 2], [128, 8]])
            P.dma(s_mrld, lambda sy: sy.dma_start(out=mr_t[:, :, :], in_=mr_src,
                                                  allow_slow_non_contiguous=True),
                  reads=[bMrD], writes=[bMr])
            for ttc in range(TC // 128):
                pt = ttc % 2
                P.dma(s_x2r[pt], lambda sy, ttc=ttc, pt=pt: sy.dma_start(
                    out=x2r_sb[:, pt, :], in_=x2_dram[ttc * 128:(ttc + 1) * 128, :]),
                    reads=[bX2D], writes=[bX2r[pt]])
                for dc in range(NDC):
                    bank = pt * 4 + dc // 4
                    col = (dc % 4) * 128
                    P.op("tensor", lambda t, ttc=ttc, dc=dc, bank=bank, col=col: t.transpose(
                        ps[bank][:, col:col + 128],
                        ea_sb[:, dc, ttc * 128:(ttc + 1) * 128], ident_sb[:, :]),
                        reads=[bEa, bIdent], writes=[bPs[bank]])
                    P.op("vector", lambda v, pt=pt, dc=dc, bank=bank, col=col, ttc=ttc: v.tensor_scalar(
                        outT_sb[:, pt, dc * 128:(dc + 1) * 128], ps[bank][:, col:col + 128],
                        mr_t[:, 0, ttc:ttc + 1], mr_t[:, 1, ttc:ttc + 1], ALU.subtract, ALU.mult),
                        reads=[bPs[bank], bMr], writes=[bOutT[pt]])
                P.op("vector", lambda v, pt=pt: v.tensor_add(
                    outT_sb[:, pt, :], outT_sb[:, pt, :], x2r_sb[:, pt, :]),
                    reads=[bOutT[pt], bX2r[pt]], writes=[bOutT[pt]])
                P.dma(s_outst, lambda s, ttc=ttc, pt=pt: s.dma_start(
                    out=out_d[ttc * 128:(ttc + 1) * 128, :], in_=outT_sb[:, pt, :]),
                    reads=[bOutT[pt]], writes=[])

    P.barrier_all()
    P.emit(nc)
    ctx.close()
    return nc


_PROGRAM_CACHE = {}


def _build_cached():
    if "nc" not in _PROGRAM_CACHE:
        _PROGRAM_CACHE["nc"] = build_program()
    return _PROGRAM_CACHE["nc"]


def _chunk_cols(v):
    return np.ascontiguousarray(np.asarray(v, dtype=np.float32).reshape(-1, 128).T)


def make_in_maps(inputs):
    x = np.asarray(inputs["L0_attn_x"], dtype=np.float32)
    w_in = np.asarray(inputs["L0_attn_w_in"], dtype=np.float32)
    b_in = np.asarray(inputs["L0_attn_b_in"], dtype=np.float32)
    w_out = np.ascontiguousarray(np.asarray(inputs["L0_attn_w_out"], dtype=np.float32))
    b_out = np.asarray(inputs["L0_attn_b_out"], dtype=np.float32)
    amp_w = np.ascontiguousarray(np.asarray(inputs["L0_ffn_amp_w"], dtype=np.float32))
    amp_b = np.asarray(inputs["L0_ffn_amp_b"], dtype=np.float32)
    ph_w = np.ascontiguousarray(np.asarray(inputs["L0_ffn_ph_w"], dtype=np.float32))
    ph_b = np.asarray(inputs["L0_ffn_ph_b"], dtype=np.float32)
    ent_w = np.ascontiguousarray(np.asarray(inputs["L0_ffn_ent_w"], dtype=np.float32))
    ent_b = np.asarray(inputs["L0_ffn_ent_b"], dtype=np.float32)
    ent_pw = np.ascontiguousarray(np.asarray(inputs["L0_ffn_ent_pw"], dtype=np.float32))
    ent_pb = np.asarray(inputs["L0_ffn_ent_pb"], dtype=np.float32)
    path_w = np.ascontiguousarray(np.asarray(inputs["L0_ffn_path_w"], dtype=np.float32))
    path_b = np.asarray(inputs["L0_ffn_path_b"], dtype=np.float32)
    phase = np.asarray(inputs["L0_ffn_phase"], dtype=np.float32)
    ln3_g = np.asarray(inputs["L0_ffn_ln3_g"], dtype=np.float32)
    ln3_b = np.asarray(inputs["L0_ffn_ln3_b"], dtype=np.float32)
    ln4_g = np.asarray(inputs["L0_ffn_ln4_g"], dtype=np.float32)
    ln4_b = np.asarray(inputs["L0_ffn_ln4_b"], dtype=np.float32)

    fb = np.zeros((128, FB_COLS), dtype=np.float32)
    fb[:, FB_AMPB:FB_AMPB + 16] = _chunk_cols(amp_b)
    fb[:, FB_PHB:FB_PHB + 16] = _chunk_cols(ph_b)
    for l in range(L):
        fb[:, FB_ENTB + l * NFC:FB_ENTB + (l + 1) * NFC] = _chunk_cols(ent_b[l])
        fb[:, FB_ENTPB + l * NDC:FB_ENTPB + (l + 1) * NDC] = _chunk_cols(ent_pb[l])
        fb[:, FB_LN + (l * 4 + 0) * NDC:FB_LN + (l * 4 + 1) * NDC] = _chunk_cols(ln3_g[l])
        fb[:, FB_LN + (l * 4 + 1) * NDC:FB_LN + (l * 4 + 2) * NDC] = _chunk_cols(ln3_b[l])
        fb[:, FB_LN + (l * 4 + 2) * NDC:FB_LN + (l * 4 + 3) * NDC] = _chunk_cols(ln4_g[l])
        fb[:, FB_LN + (l * 4 + 3) * NDC:FB_LN + (l * 4 + 4) * NDC] = _chunk_cols(ln4_b[l])
        for p in range(PW):
            fb[:, FB_MOD + (l * PW + p) * NDC:FB_MOD + (l * PW + p + 1) * NDC] = _chunk_cols(phase[l, p])
            fb[:, FB_PB + (l * PW + p) * NDC:FB_PB + (l * PW + p + 1) * NDC] = _chunk_cols(path_b[l, p])

    fb[:, FB_EPS] = EPS

    masks = np.zeros((128, 4, 512), dtype=np.float32)
    ki = np.arange(128)[:, None]
    qq = np.arange(512)[None, :]
    for j in range(4):
        masks[:, j, :] = (qq >= 128 * j + ki).astype(np.float32)
    masks = np.ascontiguousarray(masks.reshape(128, 2048)).astype(ml_dtypes.bfloat16)

    ident = np.eye(128, dtype=np.float32)
    ones_bf = np.ones((128, 512), dtype=ml_dtypes.bfloat16)
    onesf = np.ones((1, 128), dtype=np.float32)
    onesfc = np.ones((128, 1), dtype=np.float32)

    in_maps = []
    for c in range(8):
        b = c // W
        j = c % W
        heads = [W * j + i for i in range(HPC)]
        qcols = np.concatenate([w_in[:, h * 128:(h + 1) * 128] for h in heads], axis=1)
        kcols = np.concatenate([w_in[:, D + h * 128:D + (h + 1) * 128] for h in heads], axis=1)
        vcols = np.concatenate([w_in[:, 2 * D + h * 128:2 * D + (h + 1) * 128] for h in heads], axis=1)
        wqkv = np.ascontiguousarray(np.concatenate([qcols, kcols, vcols], axis=1))
        bqk = np.stack([b_in[h * 128:(h + 1) * 128] for h in heads] +
                       [b_in[D + h * 128:D + (h + 1) * 128] for h in heads], axis=1)
        bv = np.concatenate([b_in[2 * D + h * 128:2 * D + (h + 1) * 128] for h in heads])[None, :]
        m = {
            "x_batch": np.ascontiguousarray(x[b]),
            "x_slice": np.ascontiguousarray(x[b, j * TC:(j + 1) * TC]),
            "w_qkv": wqkv,
            "b_qk": np.ascontiguousarray(bqk),
            "b_v_row": np.ascontiguousarray(bv),
            "w_out": w_out,
            "b_out_row": np.ascontiguousarray(b_out[None, :]),
            "amp_w": amp_w,
            "ph_w": ph_w,
            "fb_in": fb,
            "ent_w": ent_w,
            "ent_pw": ent_pw,
            "path_w": path_w,
            "masks_in": masks,
            "phase_r": np.ascontiguousarray(phase.reshape(1, L * PW * D)),
            "ident_in": ident,
            "ones_in": ones_bf,
            "onesf_in": onesf,
            "onesfc_in": onesfc,
            "tok_off": np.array([[(j // 2) * (HPC * 512 * (S // 2))
                                  + ((dc % 4) * 512 + (dc // 4) * 128) * (S // 2)
                                  + (j % 2) * TC
                                  for dc in range(NDC)]], dtype=np.int32),
        }
        in_maps.append(m)
    return in_maps


def kernel(**inputs):
    nc = _build_cached()
    in_maps = make_in_maps(inputs)
    res = run_bass_kernel_spmd(nc, in_maps, core_ids=list(range(8)))
    out = np.empty((B, S, D), dtype=np.float32)
    for c in range(8):
        b = c // W
        j = c % W
        out[b, j * TC:(j + 1) * TC] = np.asarray(res.results[c]["out"])
    return out


if __name__ == "__main__":
    build_program()
    print("build ok")


# version 24
# speedup vs baseline: 1.0212x; 1.0108x over previous
import math
import numpy as np
import ml_dtypes

import concourse.bass as bass
import concourse.mybir as mybir
from concourse.bass_utils import run_bass_kernel_spmd

F32 = mybir.dt.float32
BF16 = mybir.dt.bfloat16
I32 = mybir.dt.int32
AF = mybir.ActivationFunctionType
ALU = mybir.AluOpType

DEBUG_DUMP = False

ENGINES = ("tensor", "vector", "scalar", "gpsimd", "sync")


class Buf:
    __slots__ = ("name", "lw", "readers")

    def __init__(self, name):
        self.name = name
        self.lw = {}
        self.readers = {}


class Stream:
    __slots__ = ("key", "cnt", "engine")

    def __init__(self, key, engine):
        self.key = key
        self.cnt = 0
        self.engine = engine


class Plan:
    def __init__(self):
        self.ops = {e: [] for e in ENGINES}
        self.cnt = {e: 0 for e in ENGINES}
        self.watermark = {e: {} for e in ENGINES}
        self.sem_keys = set(ENGINES)
        self.streams = {}

    def stream(self, name, engine):
        key = f"dma_{name}"
        assert key not in self.sem_keys
        self.sem_keys.add(key)
        s = Stream(key, engine)
        self.streams[key] = s
        return s

    def custom_sem(self, name):
        key = f"cs_{name}"
        self.sem_keys.add(key)
        return key

    def _waits_for(self, eng, reads, writes):
        need = {}

        def add(sem_key, val):
            if eng == "tensor" and sem_key == "tensor":
                return
            if self.watermark[eng].get(sem_key, 0) >= val:
                return
            if need.get(sem_key, 0) < val:
                need[sem_key] = val

        for b in reads:
            for k, v in b.lw.items():
                add(k, v)
        for b in writes:
            for k, v in b.lw.items():
                add(k, v)
            for k, v in b.readers.items():
                add(k, v)
        return need

    def _commit(self, eng, need):
        for k, v in need.items():
            self.watermark[eng][k] = v

    def _record(self, sem_key, val, reads, writes):
        for b in reads:
            if b.readers.get(sem_key, 0) < val:
                b.readers[sem_key] = val
        for b in writes:
            if b.lw.get(sem_key, 0) < val:
                b.lw[sem_key] = val

    def op(self, eng, emit, reads=(), writes=()):
        need = self._waits_for(eng, reads, writes)
        self._commit(eng, need)
        self.cnt[eng] += 1
        val = self.cnt[eng]
        self.ops[eng].append((need, emit, eng, 1))
        self._record(eng, val, reads, writes)

    def op_custom(self, eng, emit, sem_key, done_val, reads=(), writes=()):
        need = self._waits_for(eng, reads, writes)
        self._commit(eng, need)
        self.ops[eng].append((need, emit, sem_key, done_val))
        self._record(sem_key, done_val, reads, writes)

    def dma(self, stream, emit, reads=(), writes=()):
        eng = stream.engine
        need = self._waits_for(eng, reads, writes)
        if stream.cnt > 0 and self.watermark[eng].get(stream.key, 0) < stream.cnt:
            if need.get(stream.key, 0) < stream.cnt:
                need[stream.key] = stream.cnt
        self._commit(eng, need)
        stream.cnt += 16
        val = stream.cnt
        self.ops[eng].append((need, emit, stream.key, 16))
        self._record(stream.key, val, reads, writes)

    def raw(self, eng, emit):
        self.ops[eng].append((None, emit, None, 0))

    def stamp(self, bufs, stream):
        for b in bufs:
            b.lw = {stream.key: stream.cnt}
            b.readers = {}

    def barrier_all(self):
        snapshot = dict(self.cnt)
        for s in self.streams.values():
            snapshot[s.key] = s.cnt
        for e in ENGINES:
            need = {}
            for k, v in snapshot.items():
                if v > 0 and k != e and self.watermark[e].get(k, 0) < v:
                    need[k] = v
                    self.watermark[e][k] = v
            if need:
                self.ops[e].append((need, None, None, 0))

    def emit(self, nc):
        import contextlib

        with contextlib.ExitStack() as st:
            sems = {}
            for k in sorted(self.sem_keys):
                sems[k] = st.enter_context(nc.semaphore(f"s_{k}"))
            with nc.Block() as block:
                for eng in ENGINES:
                    ops = self.ops[eng]
                    if not ops:
                        continue

                    def make_body(ops=ops):
                        def body(proxy):
                            for need, emit, sem_key, inc in ops:
                                if need:
                                    for k, v in sorted(need.items()):
                                        proxy.wait_ge(sems[k], v)
                                if emit is not None:
                                    inst = emit(proxy)
                                    if inc:
                                        inst.then_inc(sems[sem_key], inc)
                        return body

                    getattr(block, eng)(make_body())


def MMG(psum_ap, pairs):
    def emit(t):
        last = None
        n = len(pairs)
        for i, (l, r) in enumerate(pairs):
            last = t.matmul(psum_ap, l, r, start=(i == 0), stop=(i == n - 1))
        return last
    return emit


B, S, D = 2, 4096, 2048
NH, DH = 16, 128
HPC = 4
W = 4
TC = 1024
F1 = 8192
L = 2
PW = 2
EPS = 1e-5
SCALE = 1.0 / math.sqrt(DH)
NDC = D // 128
NFC = F1 // 128

FB_AMPB = 0
FB_PHB = 16
FB_ENTB = 32
FB_ENTPB = 160
FB_LN = 192
FB_MOD = 448
FB_PB = 512
FB_BMS = 576
FB_EPS = 608
FB_COLS = 640


def build_program():
    nc = bass.Bass("TRN2", target_bir_lowering=False)
    P = Plan()
    import contextlib
    ctx = contextlib.ExitStack()

    x_batch = nc.dram_tensor("x_batch", [S, D], F32, kind="ExternalInput").ap()
    x_slice = nc.dram_tensor("x_slice", [TC, D], F32, kind="ExternalInput").ap()
    w_qkv = nc.dram_tensor("w_qkv", [D, 3 * HPC * 128], F32, kind="ExternalInput").ap()
    b_qk = nc.dram_tensor("b_qk", [128, 2 * HPC], F32, kind="ExternalInput").ap()
    b_v_row = nc.dram_tensor("b_v_row", [1, HPC * 128], F32, kind="ExternalInput").ap()
    w_out_d = nc.dram_tensor("w_out", [D, D], F32, kind="ExternalInput").ap()
    b_out_row = nc.dram_tensor("b_out_row", [1, D], F32, kind="ExternalInput").ap()
    amp_w_d = nc.dram_tensor("amp_w", [D, D], F32, kind="ExternalInput").ap()
    ph_w_d = nc.dram_tensor("ph_w", [D, D], F32, kind="ExternalInput").ap()
    fb_in = nc.dram_tensor("fb_in", [128, FB_COLS], F32, kind="ExternalInput").ap()
    ent_w_d = nc.dram_tensor("ent_w", [L, D, F1], F32, kind="ExternalInput").ap()
    ent_pw_d = nc.dram_tensor("ent_pw", [L, F1, D], F32, kind="ExternalInput").ap()
    path_w_d = nc.dram_tensor("path_w", [L, PW, D, D], F32, kind="ExternalInput").ap()
    masks_in = nc.dram_tensor("masks_in", [128, 4 * 512], BF16, kind="ExternalInput").ap()
    phase_r = nc.dram_tensor("phase_r", [1, L * PW * D], F32, kind="ExternalInput").ap()
    ident_in = nc.dram_tensor("ident_in", [128, 128], F32, kind="ExternalInput").ap()
    ones_in = nc.dram_tensor("ones_in", [128, 512], BF16, kind="ExternalInput").ap()
    onesf_in = nc.dram_tensor("onesf_in", [1, 128], F32, kind="ExternalInput").ap()
    onesfc_in = nc.dram_tensor("onesfc_in", [128, 1], F32, kind="ExternalInput").ap()
    tok_off = nc.dram_tensor("tok_off", [1, NDC], I32, kind="ExternalInput").ap()

    out_d = nc.dram_tensor("out", [TC, D], F32, kind="ExternalOutput").ap()

    xln_dram = nc.dram_tensor("xln_dram", [S, D], BF16).ap()
    ag_in = nc.dram_tensor("ag_in", [2, HPC * 128, S // 2], BF16).ap()
    ag_out = nc.dram_tensor("ag_out", [2 * HPC * 512, S // 2], BF16).ap()
    x2_dram = nc.dram_tensor("x2_dram", [TC, D], F32).ap()
    x2ln_dram = nc.dram_tensor("x2ln_dram", [TC, D], BF16).ap()
    h_dram = [nc.dram_tensor(f"h_dram{i}", [D, TC], F32).ap() for i in range(2)]
    wo_bf = nc.dram_tensor("wo_bf", [NDC, 128, D], BF16).ap()
    modb_dram = nc.dram_tensor("modb_dram", [L * PW, 128, D], BF16).ap()
    mr_dram_t = nc.dram_tensor("mr_dram", [1, 2048], F32)
    mr_dram = mr_dram_t.ap()

    dbg = {}
    if DEBUG_DUMP:
        dbg["q"] = nc.dram_tensor("dbg_q", [HPC * 128, S], BF16, kind="ExternalOutput").ap()
        dbg["k"] = nc.dram_tensor("dbg_k", [HPC * 128, S], BF16, kind="ExternalOutput").ap()
        dbg["v"] = nc.dram_tensor("dbg_v", [S, HPC * 128], BF16, kind="ExternalOutput").ap()
        dbg["att"] = nc.dram_tensor("dbg_att", [HPC * 128, S], BF16, kind="ExternalOutput").ap()
        dbg["x2"] = nc.dram_tensor("dbg_x2", [TC, D], F32, kind="ExternalOutput").ap()
        dbg["h1"] = nc.dram_tensor("dbg_h1", [D, TC], F32, kind="ExternalOutput").ap()
        dbg["hl0"] = nc.dram_tensor("dbg_hl0", [D, TC], F32, kind="ExternalOutput").ap()
        dbg["modb"] = nc.dram_tensor("dbg_modb", [L * PW, 128, D], BF16, kind="ExternalOutput").ap()
        dbg["mrow"] = nc.dram_tensor("dbg_mrow", [1, 3 * L * PW * D], F32, kind="ExternalOutput").ap()

    bXlnD = Buf("xlnD")
    bAgIn = [[Buf(f"agIn{h}_{hf}") for hf in range(2)] for h in range(HPC)]
    bAgOut = [[Buf(f"agOut{h}_{hf}") for hf in range(2)] for h in range(HPC)]
    bX2D = Buf("x2D"); bX2lnD = Buf("x2lnD"); bMrD = Buf("mrD")
    bHD = [Buf("hD0"), Buf("hD1")]

    s_const = P.stream("const", "gpsimd")
    s_wq = P.stream("wq", "gpsimd")
    s_wo = P.stream("wo", "gpsimd")
    s_wo2 = P.stream("wo2", "scalar")
    s_x = [P.stream(f"x{i}", "sync") for i in range(2)]
    s_xlnst = P.stream("xlnst", "scalar")
    s_xfm = [P.stream(f"xfm{i}", "sync") for i in range(4)]
    s_agst = P.stream("agst", "scalar")
    s_agld = [P.stream(f"agld{i}", "sync") for i in range(2)]
    s_xs = [P.stream(f"xs{i}", "sync") for i in range(2)]
    s_x2st = P.stream("x2st", "scalar")
    s_x2lnst = P.stream("x2lnst", "scalar")
    s_x2fm = P.stream("x2fm", "sync")
    s_wa = [P.stream(f"wa{i}", "gpsimd") for i in range(4)]
    s_we = [P.stream(f"we{i}", "gpsimd") for i in range(2)]
    s_hb = [P.stream(f"hb{i}", "gpsimd") for i in range(2)]
    s_wp = [P.stream(f"wp{i}", "gpsimd") for i in range(2)]
    s_hsp = [P.stream(f"hsp{i}", "scalar") for i in range(2)]
    s_hld = [P.stream(f"hld{i}", "sync") for i in range(2)]
    s_mrst = P.stream("mrst", "scalar")
    s_mrld = P.stream("mrld", "sync")
    s_x2r = [P.stream(f"x2r{i}", "sync") for i in range(2)]
    s_toff = P.stream("toff", "sync")
    s_outst = P.stream("outst", "scalar")
    s_dbg = P.stream("dbg", "scalar") if DEBUG_DUMP else None
    cc_keys = [[P.custom_sem(f"cc{h}_{hf}") for hf in range(2)] for h in range(HPC)]
    bWoBf = Buf("woBf")
    bModbD = Buf("modbD")
    s_modb = P.stream("modb", "scalar")
    s_modl = P.stream("modl", "sync")

    ps = [ctx.enter_context(nc.psum_tensor(f"ps{i}", [128, 512], F32)) for i in range(8)]
    bPs = [Buf(f"ps{i}") for i in range(8)]

    ones_sb = ctx.enter_context(nc.sbuf_tensor("ones", [128, 512], BF16))
    ident_sb = ctx.enter_context(nc.sbuf_tensor("ident", [128, 128], F32))
    onesf_sb = ctx.enter_context(nc.sbuf_tensor("onesf", [1, 128], F32))
    onesfc_sb = ctx.enter_context(nc.sbuf_tensor("onesfc", [128, 1], F32))
    bias_qk = ctx.enter_context(nc.sbuf_tensor("bias_qk", [128, 2 * HPC], F32))
    fbias = ctx.enter_context(nc.sbuf_tensor("fbias", [128, FB_COLS], F32))
    stat6 = ctx.enter_context(nc.sbuf_tensor("stat6", [128, 2, 24], F32))
    stat2 = ctx.enter_context(nc.sbuf_tensor("stat2", [128, 2, 4], F32))

    bOnes = Buf("ones"); bIdent = Buf("ident"); bOnesF = Buf("onesf")
    bOnesFC = Buf("onesfc"); bBqk = Buf("bqk"); bFB = Buf("fbias")
    bRows = Buf("rows"); bStat6 = [Buf("stat6a"), Buf("stat6b")]; bStat2 = [Buf("stat2a"), Buf("stat2b")]

    fbc = lambda c: fbias[:, c:c + 1]
    EPS_AP = fbias[:, FB_EPS:FB_EPS + 1]
    ONESB = ones_sb

    def cl(dst, src):
        P.dma(s_const, lambda g, dst=dst, src=src: g.dma_start(out=dst, in_=src))

    cl(ident_sb[:, :], ident_in[:, :])
    cl(ones_sb[:, :], ones_in[:, :])
    cl(onesf_sb[:, :], onesf_in[:, :])
    cl(onesfc_sb[:, :], onesfc_in[:, :])
    cl(bias_qk[:, :], b_qk[:, :])
    cl(fbias[:, :], fb_in[:, :])
    P.stamp([bOnes, bIdent, bOnesF, bOnesFC, bBqk, bFB], s_const)

    NP4 = L * PW * NDC
    with nc.sbuf_tensor("modtmp", [128, 4 * NP4], F32) as mq:
        bMq = Buf("mq")
        a4 = mq[:, 0:NP4]; a8 = mq[:, NP4:2 * NP4]
        c4 = mq[:, 2 * NP4:3 * NP4]; tq = mq[:, 3 * NP4:4 * NP4]
        phs = fbias[:, FB_MOD:FB_MOD + NP4]
        P.op("scalar", lambda s, a4=a4, phs=phs: s.activation(a4, phs, AF.Sin, scale=0.25), reads=[bFB], writes=[bMq])
        P.op("scalar", lambda s, a8=a8, phs=phs: s.activation(a8, phs, AF.Sin, scale=0.125), reads=[bFB], writes=[bMq])
        P.op("vector", lambda v, c4=c4, a8=a8: v.tensor_mul(c4, a8, a8), reads=[bMq], writes=[bMq])
        P.op("vector", lambda v, c4=c4: v.tensor_scalar(c4, c4, -2.0, 1.0, ALU.mult, ALU.add), reads=[bMq], writes=[bMq])
        P.op("vector", lambda v, tq=tq, a4=a4, c4=c4: v.scalar_tensor_tensor(tq, a4, 2.0, c4, ALU.mult, ALU.mult), reads=[bMq], writes=[bMq])
        P.op("vector", lambda v, c4=c4, a4=a4: v.tensor_mul(c4, a4, a4), reads=[bMq], writes=[bMq])
        P.op("vector", lambda v, c4=c4: v.tensor_scalar(c4, c4, -2.0, 1.0, ALU.mult, ALU.add), reads=[bMq], writes=[bMq])
        P.op("vector", lambda v, a4=a4, tq=tq, c4=c4: v.scalar_tensor_tensor(a4, tq, 2.0, c4, ALU.mult, ALU.mult), reads=[bMq], writes=[bMq])
        P.op("vector", lambda v, a8=a8, tq=tq: v.tensor_mul(a8, tq, tq), reads=[bMq], writes=[bMq])
        P.op("vector", lambda v, a8=a8: v.tensor_scalar(a8, a8, -2.0, 1.0, ALU.mult, ALU.add), reads=[bMq], writes=[bMq])
        P.op("vector", lambda v, phs=phs, a8=a8, a4=a4: v.tensor_add(phs, a8, a4), reads=[bMq, bFB], writes=[bFB])
        for l in range(L):
            dstc = fbias[:, FB_BMS + l * NDC:FB_BMS + (l + 1) * NDC]
            p0b = fbias[:, FB_PB + (l * PW) * NDC:FB_PB + (l * PW + 1) * NDC]
            p0m = fbias[:, FB_MOD + (l * PW) * NDC:FB_MOD + (l * PW + 1) * NDC]
            p1b = fbias[:, FB_PB + (l * PW + 1) * NDC:FB_PB + (l * PW + 2) * NDC]
            p1m = fbias[:, FB_MOD + (l * PW + 1) * NDC:FB_MOD + (l * PW + 2) * NDC]
            tmp = mq[:, 0:NDC]
            P.op("vector", lambda v, dstc=dstc, p0b=p0b, p0m=p0m: v.tensor_mul(dstc, p0b, p0m), reads=[bFB], writes=[bFB])
            P.op("vector", lambda v, tmp=tmp, p1b=p1b, p1m=p1m: v.tensor_mul(tmp, p1b, p1m), reads=[bFB], writes=[bMq])
            P.op("vector", lambda v, dstc=dstc, tmp=tmp: v.tensor_add(dstc, dstc, tmp), reads=[bMq, bFB], writes=[bFB])
    P.barrier_all()

    with (
        nc.sbuf_tensor("mrow", [1, 3 * L * PW * D], F32) as mrow_sb,
        nc.sbuf_tensor("modb_t", [128, D], BF16) as modbt_sb,
    ):
        bMrow = Buf("mrow"); bModbT = Buf("modbT")
        NR = L * PW * D
        ra4 = mrow_sb[0:1, 0:NR]
        ra8 = mrow_sb[0:1, NR:2 * NR]
        rc4 = mrow_sb[0:1, 2 * NR:3 * NR]
        P.dma(s_modl, lambda sy: sy.dma_start(out=ra4, in_=phase_r[:, :]), writes=[bMrow])
        P.op("scalar", lambda s: s.activation(ra8, ra4, AF.Sin, scale=0.125), reads=[bMrow], writes=[bMrow])
        P.op("scalar", lambda s: s.activation(ra4, ra4, AF.Sin, scale=0.25), reads=[bMrow], writes=[bMrow])
        P.op("vector", lambda v: v.tensor_mul(rc4, ra8, ra8), reads=[bMrow], writes=[bMrow])
        P.op("vector", lambda v: v.tensor_scalar(rc4, rc4, -2.0, 1.0, ALU.mult, ALU.add), reads=[bMrow], writes=[bMrow])
        P.op("vector", lambda v: v.scalar_tensor_tensor(ra8, ra4, 2.0, rc4, ALU.mult, ALU.mult), reads=[bMrow], writes=[bMrow])
        P.op("vector", lambda v: v.tensor_mul(rc4, ra4, ra4), reads=[bMrow], writes=[bMrow])
        P.op("vector", lambda v: v.tensor_scalar(rc4, rc4, -2.0, 1.0, ALU.mult, ALU.add), reads=[bMrow], writes=[bMrow])
        P.op("vector", lambda v: v.scalar_tensor_tensor(ra4, ra8, 2.0, rc4, ALU.mult, ALU.mult), reads=[bMrow], writes=[bMrow])
        P.op("vector", lambda v: v.tensor_mul(rc4, ra8, ra8), reads=[bMrow], writes=[bMrow])
        P.op("vector", lambda v: v.tensor_scalar(rc4, rc4, -2.0, 1.0, ALU.mult, ALU.add), reads=[bMrow], writes=[bMrow])
        P.op("vector", lambda v: v.tensor_add(ra4, rc4, ra4), reads=[bMrow], writes=[bMrow])
        if DEBUG_DUMP:
            P.dma(s_dbg, lambda s: s.dma_start(out=dbg["mrow"][:, :], in_=mrow_sb[0:1, :]), reads=[bMrow])
        for lp in range(L * PW):
            for ch in range(4):
                P.op("tensor", MMG(ps[ch % 2][:, :],
                     [(onesf_sb[0:1, :], ra4[0:1, lp * D + ch * 512:lp * D + (ch + 1) * 512])]),
                     reads=[bMrow, bOnesF], writes=[bPs[ch % 2]])
                P.op("vector", lambda v, ch=ch: v.tensor_copy(
                    modbt_sb[:, ch * 512:(ch + 1) * 512], ps[ch % 2][:, :]),
                    reads=[bPs[ch % 2]], writes=[bModbT])
            P.dma(s_modb, lambda s, lp=lp: s.dma_start(
                out=modb_dram[lp, :, :], in_=modbt_sb[:, :]),
                reads=[bModbT], writes=[bModbD])

    if DEBUG_DUMP:
        P.dma(s_dbg, lambda s: s.dma_start(out=dbg["modb"][:, :, :], in_=modb_dram[:, :, :]), reads=[bModbD])
    P.barrier_all()

    with (
        nc.sbuf_tensor("q_sb", [128, HPC, S], BF16) as q_sb,
        nc.sbuf_tensor("k_sb", [128, HPC, S], BF16) as k_sb,
        nc.sbuf_tensor("v_sb", [128, S // 128, HPC * 128], BF16) as v_sb,
    ):
        bQ = Buf("q"); bK = Buf("k"); bV = Buf("v")

        with (
            nc.sbuf_tensor("wqk", [128, NDC, 2 * HPC * 128], BF16) as wqk_sb,
            nc.sbuf_tensor("wv", [128, NDC, HPC * 128], BF16) as wv_sb,
            nc.sbuf_tensor("bv", [1, HPC * 128], BF16) as bv_sb,
            nc.sbuf_tensor("x_t", [128, 2, D], F32) as x_t,
            nc.sbuf_tensor("xln_t", [128, D], BF16) as xln_t,
            nc.sbuf_tensor("xfm", [128, 2, NDC, 512], BF16) as xfm_sb,
        ):
            bWqk = Buf("wqk"); bWv = Buf("wv"); bBv = Buf("bv")
            bXt = [Buf("xt0"), Buf("xt1")]
            bXln = Buf("xln")
            bXfm = [Buf("xfm0"), Buf("xfm1")]

            P.dma(s_wq, lambda g: g.dma_start(
                out=wqk_sb[:, :, :],
                in_=w_qkv.rearrange("(c p) f -> p c f", p=128)[:, :, 0:2 * HPC * 128]),
                writes=[bWqk])
            P.dma(s_wq, lambda g: g.dma_start(
                out=wv_sb[:, :, :],
                in_=w_qkv.rearrange("(c p) f -> p c f", p=128)[:, :, 2 * HPC * 128:3 * HPC * 128]),
                writes=[bWv])
            P.dma(s_const, lambda g: g.dma_start(out=bv_sb[:, :], in_=b_v_row[:, :]), writes=[bBv])

            def qkv_prep(blk):
                pb = blk % 2
                for tt in range(4):
                    g_tt = blk * 4 + tt
                    pt = g_tt % 2
                    xt = x_t[:, pt, :]
                    P.dma(s_x[pt], lambda sy, g_tt=g_tt, xt=xt: sy.dma_start(
                        out=xt, in_=x_batch[g_tt * 128:(g_tt + 1) * 128, :]), writes=[bXt[pt]])
                    for qd in range(4):
                        P.op("vector", lambda v, xt=xt, qd=qd, pt=pt: v.bn_stats(
                            stat6[:, pt, qd * 6:qd * 6 + 6], xt[:, qd * 512:(qd + 1) * 512]),
                            reads=[bXt[pt]], writes=[bStat6[pt]])
                    P.op("vector", lambda v, pt=pt: v.bn_aggr(stat2[:, pt, 0:2], stat6[:, pt, 0:24]), reads=[bStat6[pt]], writes=[bStat2[pt]])
                    P.op("scalar", lambda s, pt=pt: s.activation(stat2[:, pt, 2:3], stat2[:, pt, 1:2], AF.Sqrt, bias=EPS_AP), reads=[bStat2[pt]], writes=[bStat2[pt]])
                    P.op("vector", lambda v, pt=pt: v.reciprocal(stat2[:, pt, 3:4], stat2[:, pt, 2:3]), reads=[bStat2[pt]], writes=[bStat2[pt]])
                    P.op("vector", lambda v, xt=xt, pt=pt: v.tensor_scalar(
                        xln_t[:, :], xt, stat2[:, pt, 0:1], stat2[:, pt, 3:4], ALU.subtract, ALU.mult),
                        reads=[bXt[pt], bStat2[pt]], writes=[bXln])
                    P.dma(s_xlnst, lambda s, g_tt=g_tt: s.dma_start(
                        out=xln_dram[g_tt * 128:(g_tt + 1) * 128, :], in_=xln_t[:, :]),
                        reads=[bXln], writes=[bXlnD])
                for dc in range(NDC):
                    P.dma(s_xfm[pb * 2 + dc % 2], lambda sy, blk=blk, dc=dc, pb=pb: sy.dma_start(
                        out=xfm_sb[:, pb, dc, :],
                        in_=xln_dram[blk * 512:(blk + 1) * 512, dc * 128:(dc + 1) * 128],
                        transpose=True), reads=[bXlnD], writes=[bXfm[pb]])

            def qkv_mms(blk):
                pb = blk % 2
                for fh in range(2 * HPC):
                    bank = fh % 4
                    fcol = fh * 128
                    pairs = [(wqk_sb[:, dc, fcol:fcol + 128], xfm_sb[:, pb, dc, :]) for dc in range(NDC)]
                    P.op("tensor", MMG(ps[bank][:, :], pairs),
                         reads=[bWqk, bXfm[pb]], writes=[bPs[bank]])
                    h = fh % HPC
                    dst = (q_sb if fh < HPC else k_sb)[:, h, blk * 512:(blk + 1) * 512]
                    dstbuf = bQ if fh < HPC else bK
                    P.op("vector", lambda v, dst=dst, bank=bank, fh=fh: v.tensor_scalar(
                        dst, ps[bank][:, :], bias_qk[:, fh:fh + 1], None, ALU.add),
                        reads=[bPs[bank], bBqk], writes=[dstbuf])
                for tt in range(4):
                    g_tt = blk * 4 + tt
                    bank = 4 + tt % 2
                    pairs = [(xfm_sb[:, pb, dc, tt * 128:(tt + 1) * 128], wv_sb[:, dc, :]) for dc in range(NDC)]
                    pairs.append((ONESB[0:1, 0:128], bv_sb[0:1, :]))
                    P.op("tensor", MMG(ps[bank][:, :], pairs),
                         reads=[bWv, bXfm[pb], bOnes, bBv], writes=[bPs[bank]])
                    P.op("vector", lambda v, g_tt=g_tt, bank=bank: v.tensor_copy(
                        v_sb[:, g_tt, :], ps[bank][:, :]), reads=[bPs[bank]], writes=[bV])

            qkv_prep(0)
            for blk in range(1, 8):
                qkv_prep(blk)
                qkv_mms(blk - 1)
            qkv_mms(7)
        P.barrier_all()

        if DEBUG_DUMP:
            for h in range(HPC):
                P.dma(s_dbg, lambda s, h=h: s.dma_start(out=dbg["q"][h * 128:(h + 1) * 128, :], in_=q_sb[:, h, :]), reads=[bQ])
                P.dma(s_dbg, lambda s, h=h: s.dma_start(out=dbg["k"][h * 128:(h + 1) * 128, :], in_=k_sb[:, h, :]), reads=[bK])
            for g_tt in range(32):
                P.dma(s_dbg, lambda s, g_tt=g_tt: s.dma_start(out=dbg["v"][g_tt * 128:(g_tt + 1) * 128, :], in_=v_sb[:, g_tt, :]), reads=[bV])

        att_scope = contextlib.ExitStack()
        masks_sb = att_scope.enter_context(nc.sbuf_tensor("masks", [128, 4 * 512], BF16))
        E_sb = att_scope.enter_context(nc.sbuf_tensor("E_sb", [128, 4, 512], BF16))
        rows = att_scope.enter_context(nc.sbuf_tensor("rows_a", [1, 1024], F32))
        rcp_sb = att_scope.enter_context(nc.sbuf_tensor("rcp", [128, 2, 512], F32))
        oev_sb = att_scope.enter_context(nc.sbuf_tensor("oev", [128, 2, 512], BF16))
        bMasks = Buf("masks")
        bE = [Buf("E0"), Buf("E1"), Buf("E2"), Buf("E3")]
        bRcp = [Buf("rcp0"), Buf("rcp1")]
        bOev = [Buf("oev0"), Buf("oev1")]
        MASK = lambda j: masks_sb[:, j * 512:(j + 1) * 512]
        P.dma(s_const, lambda g: g.dma_start(out=masks_sb[:, :], in_=masks_in[:, :]), writes=[bMasks])
        def emit_score(side, h, qc, kt):
            bank_sc = 2 * side + kt % 2
            P.op("tensor", MMG(ps[bank_sc][:, :],
                 [(k_sb[:, h, kt * 128:(kt + 1) * 128], q_sb[:, h, qc * 512:(qc + 1) * 512])]),
                 reads=[bK, bQ], writes=[bPs[bank_sc]])

        def make_tail(side, h, qc):
            bank_o = 4 + side
            bank_s = 6 + side

            def tail():
                rrow = rows[0:1, 512 * side:512 * side + 512]
                P.op("vector", lambda v, bank_s=bank_s, rrow=rrow: v.reciprocal(rrow, ps[bank_s][0:1, :]),
                     reads=[bPs[bank_s]], writes=[bRows])
                P.op("tensor", MMG(ps[bank_s][:, :], [(onesf_sb[0:1, :], rrow)]),
                     reads=[bRows, bOnesF], writes=[bPs[bank_s]])
                P.op("vector", lambda v, side=side, bank_s=bank_s: v.tensor_copy(
                    rcp_sb[:, side, :], ps[bank_s][:, :]), reads=[bPs[bank_s]], writes=[bRcp[side]])
                P.op("vector", lambda v, side=side, bank_o=bank_o: v.tensor_mul(
                    oev_sb[:, side, :], ps[bank_o][:, :], rcp_sb[:, side, :]),
                    reads=[bPs[bank_o], bRcp[side]], writes=[bOev[side]])
                P.dma(s_agst, lambda s, h=h, qc=qc, side=side: s.dma_start(
                    out=ag_in[qc // 4, h * 128:(h + 1) * 128, (qc % 4) * 512:(qc % 4 + 1) * 512],
                    in_=oev_sb[:, side, :]), reads=[bOev[side]], writes=[bAgIn[h][qc // 4]])
            return tail

        def emit_consume(side, h, qc, kt, nkt):
            pe = 2 * side + kt % 2
            bank_sc = 2 * side + kt % 2
            bank_o = 4 + side
            bank_s = 6 + side
            P.op("scalar", lambda s, pe=pe, bank_sc=bank_sc: s.activation(
                E_sb[:, pe, :], ps[bank_sc][:, :], AF.Exp, scale=SCALE),
                reads=[bPs[bank_sc]], writes=[bE[pe]])
            j = kt - 4 * qc
            if j >= 0:
                P.op("vector", lambda v, pe=pe, j=j: v.tensor_mul(
                    E_sb[:, pe, :], E_sb[:, pe, :], MASK(j)),
                    reads=[bE[pe], bMasks], writes=[bE[pe]])
            if kt + 2 < nkt:
                emit_score(side, h, qc, kt + 2)
            P.op("tensor", lambda t, pe=pe, bank_s=bank_s, kt=kt, nkt=nkt: t.matmul(
                ps[bank_s][0:1, :], ONESB[:, 0:1], E_sb[:, pe, :],
                start=(kt == 0), stop=(kt == nkt - 1)),
                reads=[bE[pe], bOnes], writes=[bPs[bank_s]])
            P.op("tensor", lambda t, pe=pe, bank_o=bank_o, kt=kt, nkt=nkt, h=h: t.matmul(
                ps[bank_o][:, :], v_sb[:, kt, h * 128:(h + 1) * 128], E_sb[:, pe, :],
                start=(kt == 0), stop=(kt == nkt - 1)),
                reads=[bE[pe], bV], writes=[bPs[bank_o]])

        def emit_cc(h, hf):
            r0 = hf * HPC * 512 + h * 512
            P.op_custom("gpsimd", lambda g, h=h, hf=hf, r0=r0: g.collective_compute(
                "AllGather", ALU.bypass,
                replica_groups=[[0, 1, 2, 3], [4, 5, 6, 7]],
                ins=[ag_in[hf, h * 128:(h + 1) * 128, :].opt()],
                outs=[ag_out[r0:r0 + 512, :].opt()]),
                cc_keys[h][hf], 1, reads=[bAgIn[h][hf]], writes=[bAgOut[h][hf]])

        pending_tails = []
        for hp in (0, 2):
            for qc in range(8):
                hA, hB = hp, hp + 1
                nkt = 4 * qc + 4
                emit_score(0, hA, qc, 0)
                emit_score(1, hB, qc, 0)
                if nkt > 1:
                    emit_score(0, hA, qc, 1)
                    emit_score(1, hB, qc, 1)
                for t in pending_tails:
                    t()
                pending_tails = []
                for kt in range(nkt):
                    emit_consume(0, hA, qc, kt, nkt)
                    emit_consume(1, hB, qc, kt, nkt)
                pending_tails = [make_tail(0, hA, qc), make_tail(1, hB, qc)]
                if qc == 4:
                    emit_cc(hA, 0)
                    emit_cc(hB, 0)
            for t in pending_tails:
                t()
            pending_tails = []
            emit_cc(hp, 1)
            emit_cc(hp + 1, 1)

        if DEBUG_DUMP:
            for h in range(HPC):
                P.dma(s_dbg, lambda s, h=h: s.dma_start(
                    out=dbg["att"][h * 128:(h + 1) * 128, :], in_=ag_in[h * 128:(h + 1) * 128, :]), reads=[bAgIn[h]])

        att_scope.close()

    P.barrier_all()

    with (
        nc.sbuf_tensor("acat", [128, NDC, TC // 128, 128], BF16) as acat_sb,
        nc.sbuf_tensor("wo", [128, NDC, D], BF16) as wo_sb,
        nc.sbuf_tensor("bo", [1, D], BF16) as bo_sb,
        nc.sbuf_tensor("toff", [1, NDC], I32) as toff_t,
        nc.sbuf_tensor("xs_t", [128, 2, D], F32) as xs_t,
        nc.sbuf_tensor("ao_t", [128, 2, D], F32) as ao_t,
        nc.sbuf_tensor("x2_t", [128, 2, D], F32) as x2_t,
        nc.sbuf_tensor("x2ln_t", [128, 2, D], BF16) as x2ln_t,
    ):
        bAcat = Buf("acat"); bWo = Buf("wo"); bBo = Buf("bo"); bToff = Buf("toff")
        bXs = [Buf("xs0"), Buf("xs1")]
        bAo = [Buf("ao0"), Buf("ao1")]
        bX2 = [Buf("x20"), Buf("x21")]
        bX2ln = [Buf("x2ln0"), Buf("x2ln1")]

        P.dma(s_const, lambda g: g.dma_start(out=bo_sb[:, :], in_=b_out_row[:, :]), writes=[bBo])
        P.dma(s_toff, lambda sy: sy.dma_start(out=toff_t[:, :], in_=tok_off[:, :]), writes=[bToff])

        reg_cell = [None]

        def ag_slice_emit(dc):
            def emit(sy):
                if reg_cell[0] is None:
                    reg_cell[0] = sy.alloc_register("tokoff")
                sy.reg_load(reg_cell[0], toff_t[:1, dc:dc + 1])
                off = sy.snap(reg_cell[0], min_val=0, max_val=2 * HPC * 512 * (S // 2))
                agf = ag_out.rearrange("a b -> (a b)")
                return sy.dma_start(
                    out=acat_sb[:, dc, :, :].rearrange("p t f -> p (t f)"),
                    in_=bass.AP(agf.tensor, off, [[S // 2, 128], [1, TC]]))
            return emit

        for dc in range(NDC):
            hh = dc % 4
            P.dma(s_agld[dc % 2], ag_slice_emit(dc),
                  reads=[bAgOut[hh][0], bAgOut[hh][1], bToff], writes=[bAcat])
        P.dma(s_wo, lambda g: g.dma_start(
            out=wo_sb[:, :, :], in_=w_out_d.rearrange("(c p) f -> p c f", p=128)),
            writes=[bWo])

        for ttc in range(TC // 128):
            pt = ttc % 2
            P.dma(s_xs[pt], lambda sy, ttc=ttc, pt=pt: sy.dma_start(
                out=xs_t[:, pt, :], in_=x_slice[ttc * 128:(ttc + 1) * 128, :]), writes=[bXs[pt]])
            for f2c in range(4):
                bank = pt * 4 + f2c
                pairs = [(acat_sb[:, dc, ttc, :], wo_sb[:, dc, f2c * 512:(f2c + 1) * 512]) for dc in range(NDC)]
                pairs.append((ONESB[0:1, 0:128], bo_sb[0:1, f2c * 512:(f2c + 1) * 512]))
                P.op("tensor", MMG(ps[bank][:, :], pairs),
                     reads=[bAcat, bWo, bOnes, bBo], writes=[bPs[bank]])
            for f2c in range(4):
                bank = pt * 4 + f2c
                P.op("vector", lambda v, bank=bank, f2c=f2c, pt=pt: v.bn_stats(
                    stat6[:, pt, f2c * 6:f2c * 6 + 6], ps[bank][:, :]),
                    reads=[bPs[bank]], writes=[bStat6[pt]])
            P.op("vector", lambda v, pt=pt: v.bn_aggr(stat2[:, pt, 0:2], stat6[:, pt, 0:24]), reads=[bStat6[pt]], writes=[bStat2[pt]])
            P.op("scalar", lambda s, pt=pt: s.activation(stat2[:, pt, 2:3], stat2[:, pt, 1:2], AF.Sqrt, bias=EPS_AP), reads=[bStat2[pt]], writes=[bStat2[pt]])
            P.op("vector", lambda v, pt=pt: v.reciprocal(stat2[:, pt, 3:4], stat2[:, pt, 2:3]), reads=[bStat2[pt]], writes=[bStat2[pt]])
            for f2c in range(4):
                bank = pt * 4 + f2c
                P.op("vector", lambda v, bank=bank, pt=pt, f2c=f2c: v.tensor_scalar(
                    ao_t[:, pt, f2c * 512:(f2c + 1) * 512], ps[bank][:, :],
                    stat2[:, pt, 0:1], stat2[:, pt, 3:4], ALU.subtract, ALU.mult),
                    reads=[bPs[bank], bStat2[pt]], writes=[bAo[pt]])
            P.op("vector", lambda v, pt=pt: v.tensor_add(
                x2_t[:, pt, :], ao_t[:, pt, :], xs_t[:, pt, :]),
                reads=[bAo[pt], bXs[pt]], writes=[bX2[pt]])
            P.dma(s_x2st, lambda s, ttc=ttc, pt=pt: s.dma_start(
                out=x2_dram[ttc * 128:(ttc + 1) * 128, :], in_=x2_t[:, pt, :]),
                reads=[bX2[pt]], writes=[bX2D])
            for qd in range(4):
                P.op("vector", lambda v, pt=pt, qd=qd: v.bn_stats(
                    stat6[:, pt, qd * 6:qd * 6 + 6], x2_t[:, pt, qd * 512:(qd + 1) * 512]),
                    reads=[bX2[pt]], writes=[bStat6[pt]])
            P.op("vector", lambda v, pt=pt: v.bn_aggr(stat2[:, pt, 0:2], stat6[:, pt, 0:24]), reads=[bStat6[pt]], writes=[bStat2[pt]])
            P.op("scalar", lambda s, pt=pt: s.activation(stat2[:, pt, 2:3], stat2[:, pt, 1:2], AF.Sqrt, bias=EPS_AP), reads=[bStat2[pt]], writes=[bStat2[pt]])
            P.op("vector", lambda v, pt=pt: v.reciprocal(stat2[:, pt, 3:4], stat2[:, pt, 2:3]), reads=[bStat2[pt]], writes=[bStat2[pt]])
            P.op("vector", lambda v, pt=pt: v.tensor_scalar(
                x2ln_t[:, pt, :], x2_t[:, pt, :], stat2[:, pt, 0:1], stat2[:, pt, 3:4],
                ALU.subtract, ALU.mult), reads=[bX2[pt], bStat2[pt]], writes=[bX2ln[pt]])
            P.dma(s_x2lnst, lambda s, ttc=ttc, pt=pt: s.dma_start(
                out=x2ln_dram[ttc * 128:(ttc + 1) * 128, :], in_=x2ln_t[:, pt, :]),
                reads=[bX2ln[pt]], writes=[bX2lnD])

        if DEBUG_DUMP:
            P.dma(s_dbg, lambda s: s.dma_start(out=dbg["x2"][:, :], in_=x2_dram[:, :]), reads=[bX2D])

    P.barrier_all()

    hbf_sb = None

    def ln_fm(P, src, bSrc, sqt_sb, bSqt, bc, bBc, bank0, bank1, g_col=None, b_col=None,
              dstbf=None, bDstBf=None):
        for th in range(2):
            pairs = [(onesfc_sb[:, 0:1], src[dc][:, th * 512:(th + 1) * 512]) for dc in range(NDC)]
            P.op("tensor", MMG(ps[bank0][0:1, :], pairs), reads=[bSrc, bOnesFC], writes=[bPs[bank0]])
            P.op("vector", lambda v, th=th: v.tensor_scalar(
                rows[0:1, th * 512:(th + 1) * 512], ps[bank0][0:1, :], 1.0 / D, None, ALU.mult),
                reads=[bPs[bank0]], writes=[bRows])
            for dc in range(NDC):
                sl = dc % 4
                P.op("scalar", lambda s, dc=dc, sl=sl, th=th: s.activation(
                    sqt_sb[:, sl, :], src[dc][:, th * 512:(th + 1) * 512], AF.Square),
                    reads=[bSrc], writes=[bSqt[sl]])
                P.op("tensor", lambda t, sl=sl, dc=dc, bank1=bank1: t.matmul(
                    ps[bank1][0:1, :], onesfc_sb[:, 0:1], sqt_sb[:, sl, :],
                    start=(dc == 0), stop=(dc == NDC - 1)),
                    reads=[bSqt[sl], bOnesFC], writes=[bPs[bank1]])
            P.op("vector", lambda v, th=th: v.tensor_scalar(
                rows[0:1, 1024 + th * 512:1024 + (th + 1) * 512], ps[bank1][0:1, :], 1.0 / D, None, ALU.mult),
                reads=[bPs[bank1]], writes=[bRows])
        for th in range(2):
            sc = rows[0:1, 2048:2560]
            mm_ = rows[0:1, th * 512:(th + 1) * 512]
            vv = rows[0:1, 1024 + th * 512:1024 + (th + 1) * 512]
            P.op("vector", lambda v, sc=sc, mm_=mm_: v.tensor_mul(sc, mm_, mm_), reads=[bRows], writes=[bRows])
            P.op("vector", lambda v, vv=vv, sc=sc: v.tensor_sub(vv, vv, sc), reads=[bRows], writes=[bRows])
            P.op("scalar", lambda s, sc=sc, vv=vv: s.activation(sc, vv, AF.Sqrt, bias=EPS_AP[0:1, :]), reads=[bRows], writes=[bRows])
            P.op("vector", lambda v, vv=vv, sc=sc: v.reciprocal(vv, sc), reads=[bRows], writes=[bRows])
        for th in range(2):
            mm_ = rows[0:1, th * 512:(th + 1) * 512]
            rr = rows[0:1, 1024 + th * 512:1024 + (th + 1) * 512]
            P.op("tensor", MMG(ps[bank0][:, :], [(onesf_sb[0:1, :], mm_)]), reads=[bRows, bOnesF], writes=[bPs[bank0]])
            P.op("vector", lambda v, th=th: v.tensor_copy(bc[:, 0, th * 512:(th + 1) * 512], ps[bank0][:, :]),
                 reads=[bPs[bank0]], writes=[bBc])
            P.op("tensor", MMG(ps[bank1][:, :], [(onesf_sb[0:1, :], rr)]), reads=[bRows, bOnesF], writes=[bPs[bank1]])
            P.op("vector", lambda v, th=th: v.tensor_copy(bc[:, 1, th * 512:(th + 1) * 512], ps[bank1][:, :]),
                 reads=[bPs[bank1]], writes=[bBc])
        for th in range(2):
            hsl = slice(th * 512, (th + 1) * 512)
            for dc in range(NDC):
                P.op("vector", lambda v, dc=dc, hsl=hsl: v.tensor_sub(
                    src[dc][:, hsl], src[dc][:, hsl], bc[:, 0, hsl]),
                    reads=[bSrc, bBc], writes=[bSrc])
                P.op("vector", lambda v, dc=dc, hsl=hsl: v.tensor_mul(
                    src[dc][:, hsl], src[dc][:, hsl], bc[:, 1, hsl]),
                    reads=[bSrc, bBc], writes=[bSrc])
                if g_col is not None:
                    P.op("vector", lambda v, dc=dc, hsl=hsl: v.tensor_scalar(
                        src[dc][:, hsl], src[dc][:, hsl], g_col(dc), b_col(dc), ALU.mult, ALU.add),
                        reads=[bSrc, bFB], writes=[bSrc])
                if dstbf is not None:
                    P.op("scalar", lambda s, dc=dc, hsl=hsl: s.copy(
                        dstbf[dc][:, hsl], src[dc][:, hsl]),
                        reads=[bSrc], writes=[bDstBf])

    with (
        nc.sbuf_tensor("hbf", [128, NDC, TC], BF16) as hbf_sb,
        nc.sbuf_tensor("x2fm", [128, NDC, TC], BF16) as x2fm_sb,
        nc.sbuf_tensor("bigA", [128, 3, NDC, 512], F32) as bigA_sb,
        nc.sbuf_tensor("wsl1", [128, 4, NDC, 128], BF16) as wsl1_sb,
        nc.sbuf_tensor("ht1", [128, 4, 512], F32) as ht1_sb,
    ):
        bHbf = Buf("hbf"); bX2fm = Buf("x2fm"); bBigA = Buf("bigA")
        bWsl1 = [Buf("wsl1a"), Buf("wsl1b"), Buf("wsl1c"), Buf("wsl1d")]
        bHt1 = [Buf("ht1a"), Buf("ht1b"), Buf("ht1c"), Buf("ht1d")]

        for dc in range(NDC):
            P.dma(s_x2fm, lambda sy, dc=dc: sy.dma_start(
                out=x2fm_sb[:, dc, :],
                in_=x2ln_dram[:, dc * 128:(dc + 1) * 128], transpose=True),
                reads=[bX2lnD], writes=[bX2fm])

        TH2 = 512
        tanh_t = lambda fc: bigA_sb[:, 0, fc, :]
        s2_t = lambda fc: bigA_sb[:, 1, fc, :]
        s4_t = lambda fc: bigA_sb[:, 2, fc, :]
        for tqi in range(2):
            tsl = slice(tqi * TH2, (tqi + 1) * TH2)
            for fc in range(NDC):
                sl = fc % 4
                P.dma(s_wa[sl], lambda g, fc=fc, sl=sl: g.dma_start(
                    out=wsl1_sb[:, sl, :, :],
                    in_=ph_w_d[:, fc * 128:(fc + 1) * 128].rearrange("(c p) f -> p c f", p=128)),
                    writes=[bWsl1[sl]])
                bank = fc % 4
                pairs = [(wsl1_sb[:, sl, dc, :], x2fm_sb[:, dc, tsl]) for dc in range(NDC)]
                P.op("tensor", MMG(ps[bank][:, :], pairs), reads=[bWsl1[sl], bX2fm], writes=[bPs[bank]])
                P.op("scalar", lambda s, fc=fc, bank=bank: s.activation(
                    tanh_t(fc), ps[bank][:, :], AF.Tanh, bias=fbias[:, FB_PHB + fc:FB_PHB + fc + 1]),
                    reads=[bPs[bank], bFB], writes=[bBigA])
            for fc in range(NDC):
                P.op("scalar", lambda s, fc=fc: s.activation(s2_t(fc), tanh_t(fc), AF.Sin, scale=math.pi / 2),
                     reads=[bBigA], writes=[bBigA])
                P.op("scalar", lambda s, fc=fc: s.activation(s4_t(fc), tanh_t(fc), AF.Sin, scale=math.pi / 4),
                     reads=[bBigA], writes=[bBigA])
            amp_ps = []
            for fc in range(NDC):
                sl = fc % 4
                P.dma(s_wa[sl], lambda g, fc=fc, sl=sl: g.dma_start(
                    out=wsl1_sb[:, sl, :, :],
                    in_=amp_w_d[:, fc * 128:(fc + 1) * 128].rearrange("(c p) f -> p c f", p=128)),
                    writes=[bWsl1[sl]])
                bank = fc % 4
                pairs = [(wsl1_sb[:, sl, dc, :], x2fm_sb[:, dc, tsl]) for dc in range(NDC)]
                P.op("tensor", MMG(ps[bank][:, :], pairs), reads=[bWsl1[sl], bX2fm], writes=[bPs[bank]])
                amp_ps.append(bank)
                sig = ht1_sb[:, 2 + fc % 2, :]
                P.op("scalar", lambda s, bank=bank, sig=sig, fc=fc: s.activation(
                    sig, ps[bank][:, :], AF.Sigmoid, bias=fbias[:, FB_AMPB + fc:FB_AMPB + fc + 1]),
                    reads=[bPs[bank], bFB], writes=[bHt1[2 + fc % 2]])
                cs = tanh_t(fc)
                u = ht1_sb[:, fc % 2, :]
                P.op("vector", lambda v, fc=fc, u=u: v.tensor_mul(u, s4_t(fc), s4_t(fc)), reads=[bBigA], writes=[bHt1[fc % 2]])
                P.op("vector", lambda v, u=u: v.tensor_scalar(u, u, -2.0, 1.0, ALU.mult, ALU.add), reads=[bHt1[fc % 2]], writes=[bHt1[fc % 2]])
                P.op("vector", lambda v, fc=fc, u=u: v.scalar_tensor_tensor(u, s2_t(fc), 2.0, u, ALU.mult, ALU.mult),
                     reads=[bBigA, bHt1[fc % 2]], writes=[bHt1[fc % 2]])
                P.op("vector", lambda v, fc=fc, cs=cs: v.tensor_mul(cs, s2_t(fc), s2_t(fc)), reads=[bBigA], writes=[bBigA])
                P.op("vector", lambda v, cs=cs: v.tensor_scalar(cs, cs, -2.0, 1.0, ALU.mult, ALU.add), reads=[bBigA], writes=[bBigA])
                P.op("vector", lambda v, cs=cs, u=u: v.tensor_add(cs, cs, u), reads=[bBigA, bHt1[fc % 2]], writes=[bBigA])
                hf = ht1_sb[:, fc % 2, :]
                P.op("vector", lambda v, fc=fc, sig=sig, hf=hf: v.tensor_mul(hf, tanh_t(fc), sig),
                     reads=[bBigA, bHt1[2 + fc % 2]], writes=[bHt1[fc % 2]])
                P.op("vector", lambda v, fc=fc, hf=hf, tsl=tsl: v.tensor_copy(hbf_sb[:, fc, tsl], hf),
                     reads=[bHt1[fc % 2]], writes=[bHbf])
                P.dma(s_hsp[fc % 2], lambda s, fc=fc, tqi=tqi, hf=hf: s.dma_start(
                    out=h_dram[0][fc * 128:(fc + 1) * 128, tqi * TH2:(tqi + 1) * TH2], in_=hf),
                    reads=[bHt1[fc % 2]], writes=[bHD[0]])

    P.barrier_all()
    if DEBUG_DUMP:
        P.dma(s_dbg, lambda s: s.dma_start(out=dbg["h1"][:, :], in_=h_dram[0][:, :]), reads=[bHD[0]])

    with (
        nc.sbuf_tensor("ea", [128, NDC, TC], F32) as ea_sb,
        nc.sbuf_tensor("sqt", [128, 4, 512], F32) as sqt_sb,
        nc.sbuf_tensor("rows_b", [1, 2560], F32) as rows,
    ):
        bEa = Buf("ea")
        bSqt = [Buf("sqt0"), Buf("sqt1"), Buf("sqt2"), Buf("sqt3")]
        EA = lambda dc: ea_sb[:, dc, :]
        layer_scope = contextlib.ExitStack()
        hbf2_sb = layer_scope.enter_context(nc.sbuf_tensor("hbf2", [128, NDC, TC], BF16))
        eh_sb = layer_scope.enter_context(nc.sbuf_tensor("eh", [128, 16, TC], BF16))
        wsl_sb = layer_scope.enter_context(nc.sbuf_tensor("wsl", [128, 2, NDC, 128], BF16))
        pwsl_sb = layer_scope.enter_context(nc.sbuf_tensor("pwsl", [128, 2, 16, 128], BF16))
        ht2_sb = layer_scope.enter_context(nc.sbuf_tensor("ht2", [128, 2, TC], F32))
        bc2_sb = layer_scope.enter_context(nc.sbuf_tensor("bc2", [128, 2, TC], F32))
        modb_sb = layer_scope.enter_context(nc.sbuf_tensor("modb", [128, 2, D], BF16))
        bModb = Buf("modb")
        bHbf2 = Buf("hbf2"); bEh = Buf("eh")
        bWsl = [Buf("wsl0"), Buf("wsl1")]
        bPwsl = [Buf("pwsl0"), Buf("pwsl1")]
        bHt2 = [Buf("ht2a"), Buf("ht2b")]
        bBc2 = Buf("bc2")

        for dc in range(NDC):
            P.dma(s_hb[dc % 2], lambda g, dc=dc: g.dma_start(
                out=hbf2_sb[:, dc, :], in_=h_dram[0][dc * 128:(dc + 1) * 128, :]),
                reads=[bHD[0]], writes=[bHbf2])

        cur_h = 0
        for l in range(L):
            for fq in range(4):
                for fc_l in range(16):
                    fc = fq * 16 + fc_l
                    sl = fc % 2
                    P.dma(s_we[sl], lambda g, l=l, fc=fc, sl=sl: g.dma_start(
                        out=wsl_sb[:, sl, :, :],
                        in_=ent_w_d[l, :, fc * 128:(fc + 1) * 128].rearrange("(c p) f -> p c f", p=128)),
                        writes=[bWsl[sl]])
                    for th in range(2):
                        bank = (fc * 2 + th) % 4
                        pairs = [(wsl_sb[:, sl, dc, :], hbf2_sb[:, dc, th * 512:(th + 1) * 512]) for dc in range(NDC)]
                        P.op("tensor", MMG(ps[bank][:, :], pairs), reads=[bWsl[sl], bHbf2], writes=[bPs[bank]])
                        P.op("vector", lambda v, fc_l=fc_l, th=th, bank=bank, l=l, fc=fc: v.tensor_scalar(
                            eh_sb[:, fc_l, th * 512:(th + 1) * 512], ps[bank][:, :],
                            fbias[:, FB_ENTB + l * NFC + fc:FB_ENTB + l * NFC + fc + 1], None, ALU.add),
                            reads=[bPs[bank], bFB], writes=[bEh])
                for d2c in range(NDC):
                    sl = d2c % 2
                    P.dma(s_wp[sl], lambda g, l=l, fq=fq, d2c=d2c, sl=sl: g.dma_start(
                        out=pwsl_sb[:, sl, :, :],
                        in_=ent_pw_d[l, fq * 2048:(fq + 1) * 2048, d2c * 128:(d2c + 1) * 128].rearrange("(c p) f -> p c f", p=128)),
                        writes=[bPwsl[sl]])
                    for th in range(2):
                        bank = 4 + (d2c * 2 + th) % 4
                        pairs = [(pwsl_sb[:, sl, fc_l, :], eh_sb[:, fc_l, th * 512:(th + 1) * 512]) for fc_l in range(16)]
                        P.op("tensor", MMG(ps[bank][:, :], pairs), reads=[bPwsl[sl], bEh], writes=[bPs[bank]])
                        if fq == 0:
                            P.op("vector", lambda v, d2c=d2c, th=th, bank=bank, l=l: v.tensor_scalar(
                                ea_sb[:, d2c, th * 512:(th + 1) * 512], ps[bank][:, :],
                                fbias[:, FB_ENTPB + l * NDC + d2c:FB_ENTPB + l * NDC + d2c + 1], None, ALU.add),
                                reads=[bPs[bank], bFB], writes=[bEa])
                        else:
                            P.op("vector", lambda v, d2c=d2c, th=th, bank=bank: v.tensor_add(
                                ea_sb[:, d2c, th * 512:(th + 1) * 512],
                                ea_sb[:, d2c, th * 512:(th + 1) * 512], ps[bank][:, :]),
                                reads=[bPs[bank], bEa], writes=[bEa])
            for dc in range(NDC):
                pt = dc % 2
                P.dma(s_hld[pt], lambda sy, dc=dc, pt=pt, cur_h=cur_h: sy.dma_start(
                    out=ht2_sb[:, pt, :], in_=h_dram[cur_h][dc * 128:(dc + 1) * 128, :]),
                    reads=[bHD[cur_h]], writes=[bHt2[pt]])
                P.op("vector", lambda v, dc=dc, pt=pt: v.tensor_add(
                    ea_sb[:, dc, :], ea_sb[:, dc, :], ht2_sb[:, pt, :]),
                    reads=[bEa, bHt2[pt]], writes=[bEa])
            ln_fm(P, [EA(dc) for dc in range(NDC)], bEa, sqt_sb, bSqt, bc2_sb, bBc2, 6, 7,
                  g_col=lambda dc, l=l: fbias[:, FB_LN + (l * 4 + 0) * NDC + dc:FB_LN + (l * 4 + 0) * NDC + dc + 1],
                  b_col=lambda dc, l=l: fbias[:, FB_LN + (l * 4 + 1) * NDC + dc:FB_LN + (l * 4 + 1) * NDC + dc + 1],
                  dstbf=[hbf2_sb[:, dc, :] for dc in range(NDC)], bDstBf=bHbf2)
            for p in range(PW):
                P.dma(s_modl, lambda sy, l=l, p=p: sy.dma_start(
                    out=modb_sb[:, p, :], in_=modb_dram[l * PW + p, :, :]),
                    reads=[bModbD], writes=[bModb])
            for fc in range(NDC):
                sl = fc % 2
                P.dma(s_wa[sl], lambda g, l=l, fc=fc, sl=sl: g.dma_start(
                    out=wsl_sb[:, sl, :, :],
                    in_=path_w_d[l, 0, :, fc * 128:(fc + 1) * 128].rearrange("(c p2) f -> p2 c f", p2=128)),
                    writes=[bWsl[sl]])
                P.dma(s_wp[sl], lambda g, l=l, fc=fc, sl=sl: g.dma_start(
                    out=pwsl_sb[:, sl, :, :],
                    in_=path_w_d[l, 1, :, fc * 128:(fc + 1) * 128].rearrange("(c p2) f -> p2 c f", p2=128)),
                    writes=[bPwsl[sl]])
                mb0 = bass.AP(modb_sb, fc * 128, [[2 * D, 128], [0, NDC], [1, 128]])
                mb1 = bass.AP(modb_sb, D + fc * 128, [[2 * D, 128], [0, NDC], [1, 128]])
                P.op("vector", lambda v, sl=sl, mb0=mb0: v.tensor_tensor(
                    wsl_sb[:, sl, :, :], wsl_sb[:, sl, :, :], mb0, ALU.mult),
                    reads=[bWsl[sl], bModb], writes=[bWsl[sl]])
                P.op("vector", lambda v, sl=sl, mb1=mb1: v.tensor_tensor(
                    pwsl_sb[:, sl, :, :], pwsl_sb[:, sl, :, :], mb1, ALU.mult),
                    reads=[bPwsl[sl], bModb], writes=[bPwsl[sl]])
                P.op("vector", lambda v, sl=sl: v.tensor_tensor(
                    wsl_sb[:, sl, :, :].rearrange("p c f -> p (c f)"),
                    wsl_sb[:, sl, :, :].rearrange("p c f -> p (c f)"),
                    pwsl_sb[:, sl, :, :].rearrange("p c f -> p (c f)"), ALU.add),
                    reads=[bWsl[sl], bPwsl[sl]], writes=[bWsl[sl]])
                for th in range(2):
                    bank = (fc * 2 + th) % 4
                    pairs = [(wsl_sb[:, sl, dc, :], hbf2_sb[:, dc, th * 512:(th + 1) * 512]) for dc in range(NDC)]
                    P.op("tensor", MMG(ps[bank][:, :], pairs),
                         reads=[bWsl[sl], bHbf2], writes=[bPs[bank]])
                    P.op("vector", lambda v, fc=fc, th=th, bank=bank: v.tensor_add(
                        ea_sb[:, fc, th * 512:(th + 1) * 512],
                        ea_sb[:, fc, th * 512:(th + 1) * 512], ps[bank][:, :]),
                        reads=[bPs[bank], bEa], writes=[bEa])
            for dc in range(NDC):
                P.op("vector", lambda v, dc=dc, l=l: v.tensor_scalar(
                    ea_sb[:, dc, :], ea_sb[:, dc, :],
                    fbias[:, FB_BMS + l * NDC + dc:FB_BMS + l * NDC + dc + 1], None, ALU.add),
                    reads=[bEa, bFB], writes=[bEa])
            last = (l == L - 1)
            ln_fm(P, [EA(dc) for dc in range(NDC)], bEa, sqt_sb, bSqt, bc2_sb, bBc2, 6, 7,
                  g_col=lambda dc, l=l: fbias[:, FB_LN + (l * 4 + 2) * NDC + dc:FB_LN + (l * 4 + 2) * NDC + dc + 1],
                  b_col=lambda dc, l=l: fbias[:, FB_LN + (l * 4 + 3) * NDC + dc:FB_LN + (l * 4 + 3) * NDC + dc + 1],
                  dstbf=None if last else [hbf2_sb[:, dc, :] for dc in range(NDC)],
                  bDstBf=bHbf2)
            if not last:
                nxt = 1 - cur_h
                for dc in range(NDC):
                    P.dma(s_hsp[dc % 2], lambda s, dc=dc, nxt=nxt: s.dma_start(
                        out=h_dram[nxt][dc * 128:(dc + 1) * 128, :], in_=ea_sb[:, dc, :]),
                        reads=[bEa], writes=[bHD[nxt]])
                cur_h = nxt
            if DEBUG_DUMP and l == 0:
                for dc in range(NDC):
                    P.dma(s_dbg, lambda s, dc=dc: s.dma_start(
                        out=dbg["hl0"][dc * 128:(dc + 1) * 128, :], in_=ea_sb[:, dc, :]), reads=[bEa])

        layer_scope.close()
        P.barrier_all()

        for th in range(2):
            pairs = [(onesfc_sb[:, 0:1], ea_sb[:, dc, th * 512:(th + 1) * 512]) for dc in range(NDC)]
            P.op("tensor", MMG(ps[6][0:1, :], pairs), reads=[bEa, bOnesFC], writes=[bPs[6]])
            P.op("vector", lambda v, th=th: v.tensor_scalar(
                rows[0:1, th * 512:(th + 1) * 512], ps[6][0:1, :], 1.0 / D, None, ALU.mult),
                reads=[bPs[6]], writes=[bRows])
            for dc in range(NDC):
                sl = dc % 4
                P.op("scalar", lambda s, dc=dc, sl=sl, th=th: s.activation(
                    sqt_sb[:, sl, :], ea_sb[:, dc, th * 512:(th + 1) * 512], AF.Square),
                    reads=[bEa], writes=[bSqt[sl]])
                P.op("tensor", lambda t, sl=sl, dc=dc: t.matmul(
                    ps[7][0:1, :], onesfc_sb[:, 0:1], sqt_sb[:, sl, :],
                    start=(dc == 0), stop=(dc == NDC - 1)),
                    reads=[bSqt[sl], bOnesFC], writes=[bPs[7]])
            P.op("vector", lambda v, th=th: v.tensor_scalar(
                rows[0:1, 1024 + th * 512:1024 + (th + 1) * 512], ps[7][0:1, :], 1.0 / D, None, ALU.mult),
                reads=[bPs[7]], writes=[bRows])
        for th in range(2):
            sc = rows[0:1, 2048:2560]
            mm_ = rows[0:1, th * 512:(th + 1) * 512]
            vv = rows[0:1, 1024 + th * 512:1024 + (th + 1) * 512]
            P.op("vector", lambda v, sc=sc, mm_=mm_: v.tensor_mul(sc, mm_, mm_), reads=[bRows], writes=[bRows])
            P.op("vector", lambda v, vv=vv, sc=sc: v.tensor_sub(vv, vv, sc), reads=[bRows], writes=[bRows])
            P.op("scalar", lambda s, sc=sc, vv=vv: s.activation(sc, vv, AF.Sqrt, bias=EPS_AP[0:1, :]), reads=[bRows], writes=[bRows])
            P.op("vector", lambda v, vv=vv, sc=sc: v.reciprocal(vv, sc), reads=[bRows], writes=[bRows])
        with (
            nc.sbuf_tensor("mr_t", [128, 2, 8], F32) as mr_t,
            nc.sbuf_tensor("x2r", [128, 2, D], F32) as x2r_sb,
            nc.sbuf_tensor("outT", [128, 2, D], F32) as outT_sb,
        ):
            bMr = Buf("mr")
            bX2r = [Buf("x2r0"), Buf("x2r1")]
            bOutT = [Buf("outT0"), Buf("outT1")]
            P.dma(s_mrst, lambda s: s.dma_start(out=mr_dram[0:1, 0:1024], in_=rows[0:1, 0:1024]),
                  reads=[bRows], writes=[bMrD])
            P.dma(s_mrst, lambda s: s.dma_start(out=mr_dram[0:1, 1024:2048], in_=rows[0:1, 1024:2048]),
                  reads=[bRows], writes=[bMrD])
            mr_src = bass.AP(mr_dram_t, 0, [[1, 128], [1024, 2], [128, 8]])
            P.dma(s_mrld, lambda sy: sy.dma_start(out=mr_t[:, :, :], in_=mr_src,
                                                  allow_slow_non_contiguous=True),
                  reads=[bMrD], writes=[bMr])
            for ttc in range(TC // 128):
                pt = ttc % 2
                P.dma(s_x2r[pt], lambda sy, ttc=ttc, pt=pt: sy.dma_start(
                    out=x2r_sb[:, pt, :], in_=x2_dram[ttc * 128:(ttc + 1) * 128, :]),
                    reads=[bX2D], writes=[bX2r[pt]])
                for dc in range(NDC):
                    bank = pt * 4 + dc // 4
                    col = (dc % 4) * 128
                    P.op("tensor", lambda t, ttc=ttc, dc=dc, bank=bank, col=col: t.transpose(
                        ps[bank][:, col:col + 128],
                        ea_sb[:, dc, ttc * 128:(ttc + 1) * 128], ident_sb[:, :]),
                        reads=[bEa, bIdent], writes=[bPs[bank]])
                    P.op("vector", lambda v, pt=pt, dc=dc, bank=bank, col=col, ttc=ttc: v.tensor_scalar(
                        outT_sb[:, pt, dc * 128:(dc + 1) * 128], ps[bank][:, col:col + 128],
                        mr_t[:, 0, ttc:ttc + 1], mr_t[:, 1, ttc:ttc + 1], ALU.subtract, ALU.mult),
                        reads=[bPs[bank], bMr], writes=[bOutT[pt]])
                P.op("vector", lambda v, pt=pt: v.tensor_add(
                    outT_sb[:, pt, :], outT_sb[:, pt, :], x2r_sb[:, pt, :]),
                    reads=[bOutT[pt], bX2r[pt]], writes=[bOutT[pt]])
                P.dma(s_outst, lambda s, ttc=ttc, pt=pt: s.dma_start(
                    out=out_d[ttc * 128:(ttc + 1) * 128, :], in_=outT_sb[:, pt, :]),
                    reads=[bOutT[pt]], writes=[])

    P.barrier_all()
    P.emit(nc)
    ctx.close()
    return nc


_PROGRAM_CACHE = {}


def _build_cached():
    if "nc" not in _PROGRAM_CACHE:
        _PROGRAM_CACHE["nc"] = build_program()
    return _PROGRAM_CACHE["nc"]


def _chunk_cols(v):
    return np.ascontiguousarray(np.asarray(v, dtype=np.float32).reshape(-1, 128).T)


def make_in_maps(inputs):
    x = np.asarray(inputs["L0_attn_x"], dtype=np.float32)
    w_in = np.asarray(inputs["L0_attn_w_in"], dtype=np.float32)
    b_in = np.asarray(inputs["L0_attn_b_in"], dtype=np.float32)
    w_out = np.ascontiguousarray(np.asarray(inputs["L0_attn_w_out"], dtype=np.float32))
    b_out = np.asarray(inputs["L0_attn_b_out"], dtype=np.float32)
    amp_w = np.ascontiguousarray(np.asarray(inputs["L0_ffn_amp_w"], dtype=np.float32))
    amp_b = np.asarray(inputs["L0_ffn_amp_b"], dtype=np.float32)
    ph_w = np.ascontiguousarray(np.asarray(inputs["L0_ffn_ph_w"], dtype=np.float32))
    ph_b = np.asarray(inputs["L0_ffn_ph_b"], dtype=np.float32)
    ent_w = np.ascontiguousarray(np.asarray(inputs["L0_ffn_ent_w"], dtype=np.float32))
    ent_b = np.asarray(inputs["L0_ffn_ent_b"], dtype=np.float32)
    ent_pw = np.ascontiguousarray(np.asarray(inputs["L0_ffn_ent_pw"], dtype=np.float32))
    ent_pb = np.asarray(inputs["L0_ffn_ent_pb"], dtype=np.float32)
    path_w = np.ascontiguousarray(np.asarray(inputs["L0_ffn_path_w"], dtype=np.float32))
    path_b = np.asarray(inputs["L0_ffn_path_b"], dtype=np.float32)
    phase = np.asarray(inputs["L0_ffn_phase"], dtype=np.float32)
    ln3_g = np.asarray(inputs["L0_ffn_ln3_g"], dtype=np.float32)
    ln3_b = np.asarray(inputs["L0_ffn_ln3_b"], dtype=np.float32)
    ln4_g = np.asarray(inputs["L0_ffn_ln4_g"], dtype=np.float32)
    ln4_b = np.asarray(inputs["L0_ffn_ln4_b"], dtype=np.float32)

    fb = np.zeros((128, FB_COLS), dtype=np.float32)
    fb[:, FB_AMPB:FB_AMPB + 16] = _chunk_cols(amp_b)
    fb[:, FB_PHB:FB_PHB + 16] = _chunk_cols(ph_b)
    for l in range(L):
        fb[:, FB_ENTB + l * NFC:FB_ENTB + (l + 1) * NFC] = _chunk_cols(ent_b[l])
        fb[:, FB_ENTPB + l * NDC:FB_ENTPB + (l + 1) * NDC] = _chunk_cols(ent_pb[l])
        fb[:, FB_LN + (l * 4 + 0) * NDC:FB_LN + (l * 4 + 1) * NDC] = _chunk_cols(ln3_g[l])
        fb[:, FB_LN + (l * 4 + 1) * NDC:FB_LN + (l * 4 + 2) * NDC] = _chunk_cols(ln3_b[l])
        fb[:, FB_LN + (l * 4 + 2) * NDC:FB_LN + (l * 4 + 3) * NDC] = _chunk_cols(ln4_g[l])
        fb[:, FB_LN + (l * 4 + 3) * NDC:FB_LN + (l * 4 + 4) * NDC] = _chunk_cols(ln4_b[l])
        for p in range(PW):
            fb[:, FB_MOD + (l * PW + p) * NDC:FB_MOD + (l * PW + p + 1) * NDC] = _chunk_cols(phase[l, p])
            fb[:, FB_PB + (l * PW + p) * NDC:FB_PB + (l * PW + p + 1) * NDC] = _chunk_cols(path_b[l, p])

    fb[:, FB_EPS] = EPS

    masks = np.zeros((128, 4, 512), dtype=np.float32)
    ki = np.arange(128)[:, None]
    qq = np.arange(512)[None, :]
    for j in range(4):
        masks[:, j, :] = (qq >= 128 * j + ki).astype(np.float32)
    masks = np.ascontiguousarray(masks.reshape(128, 2048)).astype(ml_dtypes.bfloat16)

    ident = np.eye(128, dtype=np.float32)
    ones_bf = np.ones((128, 512), dtype=ml_dtypes.bfloat16)
    onesf = np.ones((1, 128), dtype=np.float32)
    onesfc = np.ones((128, 1), dtype=np.float32)

    in_maps = []
    for c in range(8):
        b = c // W
        j = c % W
        heads = [W * j + i for i in range(HPC)]
        qcols = np.concatenate([w_in[:, h * 128:(h + 1) * 128] for h in heads], axis=1)
        kcols = np.concatenate([w_in[:, D + h * 128:D + (h + 1) * 128] for h in heads], axis=1)
        vcols = np.concatenate([w_in[:, 2 * D + h * 128:2 * D + (h + 1) * 128] for h in heads], axis=1)
        wqkv = np.ascontiguousarray(np.concatenate([qcols, kcols, vcols], axis=1))
        bqk = np.stack([b_in[h * 128:(h + 1) * 128] for h in heads] +
                       [b_in[D + h * 128:D + (h + 1) * 128] for h in heads], axis=1)
        bv = np.concatenate([b_in[2 * D + h * 128:2 * D + (h + 1) * 128] for h in heads])[None, :]
        m = {
            "x_batch": np.ascontiguousarray(x[b]),
            "x_slice": np.ascontiguousarray(x[b, j * TC:(j + 1) * TC]),
            "w_qkv": wqkv,
            "b_qk": np.ascontiguousarray(bqk),
            "b_v_row": np.ascontiguousarray(bv),
            "w_out": w_out,
            "b_out_row": np.ascontiguousarray(b_out[None, :]),
            "amp_w": amp_w,
            "ph_w": ph_w,
            "fb_in": fb,
            "ent_w": ent_w,
            "ent_pw": ent_pw,
            "path_w": path_w,
            "masks_in": masks,
            "phase_r": np.ascontiguousarray(phase.reshape(1, L * PW * D)),
            "ident_in": ident,
            "ones_in": ones_bf,
            "onesf_in": onesf,
            "onesfc_in": onesfc,
            "tok_off": np.array([[(j // 2) * (HPC * 512 * (S // 2))
                                  + ((dc % 4) * 512 + (dc // 4) * 128) * (S // 2)
                                  + (j % 2) * TC
                                  for dc in range(NDC)]], dtype=np.int32),
        }
        in_maps.append(m)
    return in_maps


def kernel(**inputs):
    nc = _build_cached()
    in_maps = make_in_maps(inputs)
    res = run_bass_kernel_spmd(nc, in_maps, core_ids=list(range(8)))
    out = np.empty((B, S, D), dtype=np.float32)
    for c in range(8):
        b = c // W
        j = c % W
        out[b, j * TC:(j + 1) * TC] = np.asarray(res.results[c]["out"])
    return out


if __name__ == "__main__":
    build_program()
    print("build ok")
